# version 07
# baseline (speedup 1.0000x reference)
import numpy as np
import concourse.bass as bass
import concourse.mybir as mybir
from concourse.bass_utils import run_bass_kernel_spmd

F32 = mybir.dt.float32
BF16 = mybir.dt.bfloat16
AF = mybir.ActivationFunctionType
ALU = mybir.AluOpType

NCORES = 8
B, L, D = 4, 4096, 1024
TK = 2048
DT = D // 128
DI = 2048
DS = 16
RK = 64
CH = 512
NCH = TK // CH
EPS = 1e-5


def run_spmd(nc, in_maps, core_ids):
    import concourse.mybir as _mb
    want = {}
    for alloc in nc.m.functions[0].allocations:
        if isinstance(alloc, _mb.MemoryLocationSet) and alloc.kind == "ExternalInput":
            name = alloc.memorylocations[0].name
            if nc.partition_id_tensor is not None and name == nc.partition_id_tensor.name:
                continue
            want[name] = (tuple(alloc.tensor_shape), _mb.dt.np(alloc.dtype))
    for im in in_maps:
        for name, (shp, dty) in want.items():
            assert name in im, f"missing input {name}"
            a = im[name]
            assert tuple(a.shape) == shp, f"{name}: got {a.shape}, want {shp}"
            assert a.dtype == dty, f"{name}: got {a.dtype}, want {dty}"
        extra = set(im) - set(want)
        assert not extra, f"extra inputs: {extra}"
    return run_bass_kernel_spmd(nc, in_maps, core_ids=core_ids)


class Ctr:

    def __init__(self, handle):
        self.h = handle
        self.n = 0

    def inc(self, inst, k=1):
        inst.then_inc(self.h, k)
        self.n += k
        return self.n

    def dma(self, inst):
        return self.inc(inst, 16)

    def wait(self, eng, val=None):
        eng.wait_ge(self.h, self.n if val is None else val)

    def wait_relax(self, eng, j, back=1, first=2):
        eng.wait_ge(self.h, self.n if j < first else max(0, self.n - back))


_ALLOC = {"stack": None, "pfx": ""}


def sb(nc, name, p, f, dtype):
    name = _ALLOC["pfx"] + name
    if _ALLOC["stack"] is None:
        return nc.alloc_sbuf_tensor(name, [p, f], dtype)
    return _ALLOC["stack"].enter_context(nc.sbuf_tensor(name, [p, f], dtype))


def sb_tiles(nc, name, ntiles, p, f, dtype):
    return [sb(nc, f"{name}{i}", p, f, dtype) for i in range(ntiles)]


def psum(nc, name, p, f, dtype=F32):
    name = _ALLOC["pfx"] + name
    if _ALLOC["stack"] is None:
        return nc.alloc_psum_tensor(name, [p, f], dtype)
    return _ALLOC["stack"].enter_context(nc.psum_tensor(name, [p, f], dtype))


def ln_alloc(nc, tag, nt, Tloc, xb=None, xsq=None):
    return {
        "xb": xb if xb is not None else sb_tiles(nc, f"{tag}_lnxb", nt, 128, CH, BF16),
        "xsq": xsq if xsq is not None else sb_tiles(nc, f"{tag}_lnxsq", nt, 128, CH, BF16),
        "row_mu": sb(nc, f"{tag}_row_mu", 1, Tloc, F32),
        "row_ms": sb(nc, f"{tag}_row_ms", 1, Tloc, F32),
        "row_rs": sb(nc, f"{tag}_row_rs", 1, Tloc, F32),
        "row_d": sb(nc, f"{tag}_row_d", 1, Tloc, F32),
        "tmp": sb(nc, f"{tag}_lntmp", 128, CH, F32),
        "ps_sum": psum(nc, f"{tag}_ps_sum", 1, CH),
        "ps_sq": psum(nc, f"{tag}_ps_sq", 1, CH),
        "ps_mu": psum(nc, f"{tag}_ps_mu", 128, CH),
        "ps_rs": psum(nc, f"{tag}_ps_rs", 128, CH),
    }


def ln_plain_T(nc, block, ctrs, x_tiles, out_tiles, nt, Tloc, consts, bufs,
               w_cols=None, b_cols=None, out_dtype_note=None):
    t_ctr, s_ctr, v_ctr = ctrs["pe"], ctrs["sca"], ctrs["vec"]
    xb, xsq = bufs["xb"], bufs["xsq"]
    row_mu, row_ms, row_rs = bufs["row_mu"], bufs["row_ms"], bufs["row_rs"]
    ps_sum, ps_sq = bufs["ps_sum"], bufs["ps_sq"]
    ps_mu, ps_rs = bufs["ps_mu"], bufs["ps_rs"]
    c_starts = [i * CH for i in range(Tloc // CH)]
    if Tloc % CH:
        c_starts.append(Tloc - CH)

    for c0 in c_starts:
        w = CH
        cs = slice(c0, c0 + w)

        @block.scalar
        def _(s):
            v_ctr.wait(s)
            t_ctr.wait(s)
            for i in range(nt):
                s.copy(xb[i][:, 0:w], x_tiles[i][:, cs])
                ins = s.activation(out=xsq[i][:, 0:w], in_=x_tiles[i][:, cs], func=AF.Square)
            s_ctr.inc(ins)

        @block.tensor
        def _(t):
            s_ctr.wait(t)
            for i in range(nt):
                t.matmul(ps_sum[0:1, 0:w], consts["ones_col_bf"][:], xb[i][:, 0:w],
                         start=(i == 0), stop=(i == nt - 1))
            for i in range(nt):
                ins = t.matmul(ps_sq[0:1, 0:w], consts["ones_col_bf"][:], xsq[i][:, 0:w],
                               start=(i == 0), stop=(i == nt - 1))
            t_ctr.inc(ins)

        @block.scalar
        def _(s):
            t_ctr.wait(s)
            s.activation(out=row_mu[0:1, cs], in_=ps_sum[0:1, 0:w], func=AF.Copy,
                         scale=1.0 / (nt * 128.0))
            ins = s.activation(out=row_ms[0:1, cs], in_=ps_sq[0:1, 0:w], func=AF.Copy,
                               scale=1.0 / (nt * 128.0))
            s_ctr.inc(ins)

    @block.vector
    def _(v):
        s_ctr.wait(v)
        v.tensor_tensor(bufs["row_d"][:], row_mu[:], row_mu[:], ALU.mult)
        ins = v.tensor_sub(bufs["row_d"][:], row_ms[:], bufs["row_d"][:])
        v_ctr.inc(ins)

    @block.scalar
    def _(s):
        v_ctr.wait(s)
        ins = s.activation(out=row_rs[:], in_=bufs["row_d"][:], func=AF.Ln, bias=consts["eps11"][:])
        s_ctr.inc(ins)
        s_ctr.wait(s)
        ins = s.activation(out=row_rs[:], in_=row_rs[:], func=AF.Exp, scale=-0.5)
        s_ctr.inc(ins)

    for c0 in c_starts:
        w = CH
        cs = slice(c0, c0 + w)

        @block.tensor
        def _(t):
            s_ctr.wait(t)
            v_ctr.wait(t)
            t.matmul(ps_mu[:, 0:w], consts["ones_row_f32"][:], row_mu[0:1, cs], start=True, stop=True)
            ins = t.matmul(ps_rs[:, 0:w], consts["ones_row_f32"][:], row_rs[0:1, cs], start=True, stop=True)
            t_ctr.inc(ins)

        @block.vector
        def _(v):
            t_ctr.wait(v)
            for i in range(nt):
                v.tensor_sub(bufs["tmp"][:, 0:w], x_tiles[i][:, cs], ps_mu[:, 0:w])
                if w_cols is None:
                    ins = v.tensor_mul(out_tiles[i][:, cs], bufs["tmp"][:, 0:w], ps_rs[:, 0:w])
                else:
                    v.tensor_mul(bufs["tmp"][:, 0:w], bufs["tmp"][:, 0:w], ps_rs[:, 0:w])
                    ins = v.tensor_scalar(out=out_tiles[i][:, cs], in0=bufs["tmp"][:, 0:w],
                                          scalar1=w_cols[i], scalar2=b_cols[i],
                                          op0=ALU.mult, op1=ALU.add)
            v_ctr.inc(ins)


def make_consts(nc):
    import ml_dtypes
    ones_col_bf = nc.inline_tensor(np.ones((128, 1), dtype=ml_dtypes.bfloat16), "ones_col_bf")
    ones_row_f32 = nc.inline_tensor(np.ones((1, 128), dtype=np.float32), "ones_row_f32")
    eps11 = nc.inline_tensor(np.full((1, 1), EPS, dtype=np.float32), "eps11")
    ones_col_f32 = nc.inline_tensor(np.ones((128, 1), dtype=np.float32), "ones_col_f32")
    sel = np.zeros((16, 16 * 128), dtype=np.float32)
    for s_ in range(16):
        sel[s_, s_ * 128:(s_ + 1) * 128] = 1.0
    sel16 = nc.inline_tensor(sel, "sel16")
    return {"t_ones_col_bf": ones_col_bf, "t_ones_row_f32": ones_row_f32, "t_eps11": eps11,
            "t_ones_col_f32": ones_col_f32, "t_sel16": sel16}


def load_consts(nc, block, consts, ld, need_sel=True):
    cb = sb(nc, "c_ones_col_bf", 128, 1, BF16)
    rf = sb(nc, "c_ones_row_f32", 1, 128, F32)
    ep = sb(nc, "c_eps11", 1, 1, F32)
    cf = sb(nc, "c_ones_col_f32", 128, 1, F32)
    sl = sb(nc, "c_sel16", 16, 16 * 128, F32) if need_sel else None

    @block.sync
    def _(q):
        ld.dma(q.dma_start(out=cb[:], in_=consts["t_ones_col_bf"].ap()))
        ld.dma(q.dma_start(out=rf[:], in_=consts["t_ones_row_f32"].ap()))
        ld.dma(q.dma_start(out=ep[:], in_=consts["t_eps11"].ap()))
        ld.dma(q.dma_start(out=cf[:], in_=consts["t_ones_col_f32"].ap()))
        if need_sel:
            ld.dma(q.dma_start(out=sl[:], in_=consts["t_sel16"].ap()))
    consts["ones_col_bf"] = cb
    consts["ones_row_f32"] = rf
    consts["eps11"] = ep
    consts["ones_col_f32"] = cf
    consts["sel16"] = sl


def build_rwkv(F=None):
    standalone = F is None
    nc = bass.Bass("TRN2", target_bir_lowering=False, debug=False, num_devices=8) if standalone else F["nc"]
    w_d = {w: nc.dram_tensor(w, [D, D], F32, kind="ExternalInput") for w in ("wrT", "wkT", "wvT")}
    mur_d = nc.dram_tensor("mur", [D, 1], F32, kind="ExternalInput")
    muk_d = nc.dram_tensor("muk", [D, 1], F32, kind="ExternalInput")
    if standalone:
        xh_d = nc.dram_tensor("xh", [D, TK + 1], F32, kind="ExternalInput")
        o_d = nc.dram_tensor("x3T", [D, TK], F32, kind="ExternalOutput")
    else:
        o_d = F["io"]["x3T"]
        pm_d = nc.dram_tensor("pmask", [128, 1], F32, kind="ExternalInput")

    xh = sb_tiles(nc, "xh", DT, 128, TK + 1, F32)
    wr = sb_tiles(nc, "wr", DT, 128, D, BF16)
    wk = sb_tiles(nc, "wk", DT, 128, D, BF16)
    wv = sb_tiles(nc, "wv", DT, 128, D, BF16)
    mur = sb_tiles(nc, "mur", DT, 128, 1, F32)
    muk = sb_tiles(nc, "muk", DT, 128, 1, F32)
    mur1 = sb_tiles(nc, "mur1", DT, 128, 1, F32)
    muk1 = sb_tiles(nc, "muk1", DT, 128, 1, F32)
    xr = sb_tiles(nc, "xr", DT, 128, CH, BF16)
    xk = sb_tiles(nc, "xk", DT, 128, CH, BF16)
    rg = sb_tiles(nc, "rg", DT, 128, CH, F32)
    kact = sb_tiles(nc, "kact", DT, 128, CH, BF16)
    x3p = sb_tiles(nc, "x3p", DT, 128, CH, F32)
    lnb = ln_alloc(nc, "r", DT, CH)
    ktmp = sb(nc, "ktmp", 128, CH, F32)

    psA = psum(nc, "psA", 128, CH)
    psB = psum(nc, "psB", 128, CH)

    import contextlib
    es = contextlib.ExitStack()
    block, consts, ld, wld, ctrs, sto = _f_ctx(
        F, nc, es, ["ld", "pe", "sca", "vec", "sto"], need_sel=False)
    with es:
        if standalone:
            @block.sync
            def _(q):
                for i in range(DT):
                    ld.dma(q.dma_start(out=xh[i][:], in_=xh_d.ap()[i * 128:(i + 1) * 128, :]))
        else:
            pmc = sb(nc, "pmc", 128, 1, F32)

            @block.sync
            def _(q):
                ld.dma(q.dma_start(out=pmc[:], in_=pm_d.ap()))
                for i in range(DT):
                    r = slice(i * 128, (i + 1) * 128)
                    ld.dma(q.dma_start(out=xh[i][:, 1:TK + 1], in_=F["io"]["x2T"].ap()[r, :]))

                def _hl(sl):
                    for i in range(DT):
                        r = slice(i * 128, (i + 1) * 128)
                        ld.dma(q.dma_start(out=xh[i][:, 0:1], in_=F["io"]["x2c"].ap()[r, sl],
                                           allow_slow_non_contiguous=True))
                par_branch(q, lambda: _hl(slice(1, 2)), lambda: _hl(slice(0, 1)), bal=(ld,))

            @block.vector
            def _(v):
                ld.wait(v)
                for i in range(DT):
                    ins = v.tensor_scalar_mul(xh[i][:, 0:1], xh[i][:, 0:1], pmc[:])
                ctrs["vec"].inc(ins)

        @block.sync
        def _(q):
            for i in range(DT):
                ld.dma(q.dma_start(out=mur[i][:], in_=mur_d.ap()[i * 128:(i + 1) * 128, :]))
                ld.dma(q.dma_start(out=muk[i][:], in_=muk_d.ap()[i * 128:(i + 1) * 128, :]))

        @block.gpsimd
        def _(g):
            for i in range(DT):
                ld.dma(g.dma_start(out=wr[i][:], in_=w_d["wrT"].ap()[i * 128:(i + 1) * 128, :]))
                ld.dma(g.dma_start(out=wk[i][:], in_=w_d["wkT"].ap()[i * 128:(i + 1) * 128, :]))
                ld.dma(g.dma_start(out=wv[i][:], in_=w_d["wvT"].ap()[i * 128:(i + 1) * 128, :]))

        @block.vector
        def _(v):
            ld.wait(v)
            for i in range(DT):
                v.tensor_scalar(out=mur1[i][:], in0=mur[i][:], scalar1=-1.0, scalar2=1.0,
                                op0=ALU.mult, op1=ALU.add)
                ins = v.tensor_scalar(out=muk1[i][:], in0=muk[i][:], scalar1=-1.0, scalar2=1.0,
                                      op0=ALU.mult, op1=ALU.add)
            ctrs["vec"].inc(ins)

        for c in range(NCH):
            cur = slice(1 + c * CH, 1 + (c + 1) * CH)
            prv = slice(c * CH, (c + 1) * CH)

            @block.vector
            def _(v):
                ctrs["pe"].wait(v)
                for i in range(DT):
                    v.tensor_scalar_mul(lnb["tmp"][:, 0:CH], xh[i][:, cur], mur[i][:])
                    v.scalar_tensor_tensor(out=xr[i][:], in0=xh[i][:, prv], scalar=mur1[i][:],
                                           in1=lnb["tmp"][:, 0:CH], op0=ALU.mult, op1=ALU.add)
                    v.tensor_scalar_mul(lnb["tmp"][:, 0:CH], xh[i][:, cur], muk[i][:])
                    ins = v.scalar_tensor_tensor(out=xk[i][:], in0=xh[i][:, prv], scalar=muk1[i][:],
                                                 in1=lnb["tmp"][:, 0:CH], op0=ALU.mult, op1=ALU.add)
                ctrs["vec"].inc(ins)

            for j in range(DT):
                ps = psA if j % 2 == 0 else psB
                jc = slice(j * 128, (j + 1) * 128)

                @block.tensor
                def _(t):
                    ctrs["vec"].wait(t)
                    ctrs["sca"].wait_relax(t, j)
                    for i in range(DT):
                        ins = t.matmul(ps[:], wr[i][:, jc], xr[i][:], start=(i == 0), stop=(i == DT - 1))
                    ctrs["pe"].inc(ins)

                @block.scalar
                def _(s):
                    ctrs["pe"].wait(s)
                    ins = s.activation(out=rg[j][:], in_=ps[:], func=AF.Sigmoid)
                    ctrs["sca"].inc(ins)
            for j in range(DT):
                ps = psA if j % 2 == 0 else psB
                jc = slice(j * 128, (j + 1) * 128)

                @block.tensor
                def _(t):
                    ctrs["sca"].wait_relax(t, j, back=2)
                    for i in range(DT):
                        ins = t.matmul(ps[:], wk[i][:, jc], xk[i][:], start=(i == 0), stop=(i == DT - 1))
                    ctrs["pe"].inc(ins)

                @block.scalar
                def _(s):
                    ctrs["pe"].wait(s)
                    ins = s.activation(out=ktmp[:], in_=ps[:], func=AF.Relu)
                    ctrs["sca"].inc(ins)
                    ctrs["sca"].wait(s)
                    ins = s.activation(out=kact[j][:], in_=ktmp[:], func=AF.Square)
                    ctrs["sca"].inc(ins)
            for j in range(DT):
                ps = psA if j % 2 == 0 else psB
                jc = slice(j * 128, (j + 1) * 128)

                @block.tensor
                def _(t):
                    ctrs["sca"].wait(t)
                    ctrs["vec"].wait_relax(t, j)
                    for i in range(DT):
                        ins = t.matmul(ps[:], wv[i][:, jc], kact[i][:], start=(i == 0), stop=(i == DT - 1))
                    ctrs["pe"].inc(ins)

                @block.vector
                def _(v):
                    ctrs["pe"].wait(v)
                    sto.wait(v)
                    v.tensor_mul(lnb["tmp"][:, 0:CH], rg[j][:], ps[:])
                    ins = v.tensor_add(x3p[j][:], xh[j][:, cur], lnb["tmp"][:, 0:CH])
                    ctrs["vec"].inc(ins)

            ln_plain_T(nc, block, ctrs, x3p, x3p, DT, CH, consts, lnb)

            @block.sync
            def _(q):
                ctrs["vec"].wait(q)
                for i in range(DT):
                    sto.dma(q.dma_start(out=o_d.ap()[i * 128:(i + 1) * 128, c * CH:(c + 1) * CH],
                                        in_=x3p[i][:]))

            if not standalone:
                @block.gpsimd
                def _(g):
                    ctrs["vec"].wait(g)

                    def _sx(off):
                        for i in range(DT):
                            sto.dma(g.dma_start(
                                out=F["io"]["sh_x3"].ap()[i * 128:(i + 1) * 128,
                                                          off + c * CH:off + (c + 1) * CH],
                                in_=x3p[i][:]))
                    par_branch(g, lambda: _sx(0), lambda: _sx(TK), bal=(sto,))

        @block.sync
        def _(q):
            sto.wait(q)
    return nc


def _shard_tok_T(a_bld, core, halo_left=0, halo_right=0):
    b, h = core // 2, core % 2
    s = h * TK
    aT = np.ascontiguousarray(np.transpose(a_bld[b], (1, 0)))
    if halo_left == 0 and halo_right == 0:
        return np.ascontiguousarray(aT[:, s:s + TK])
    out = np.zeros((aT.shape[0], halo_left + TK + halo_right), np.float32)
    lo = max(0, s - halo_left)
    hi = min(L, s + TK + halo_right)
    out[:, halo_left - (s - lo):halo_left + (hi - s)] = aT[:, lo:hi]
    return out


def run_rwkv(x2, inputs):
    nc = build_rwkv()
    wrT = np.ascontiguousarray(inputs["L0_ffn_W_r"].T)
    wkT = np.ascontiguousarray(inputs["L0_ffn_W_k"].T)
    wvT = np.ascontiguousarray(inputs["L0_ffn_W_v"].T)
    mur = np.ascontiguousarray(inputs["L0_ffn_mu_r"].reshape(D, 1))
    muk = np.ascontiguousarray(inputs["L0_ffn_mu_k"].reshape(D, 1))
    in_maps = []
    for c in range(NCORES):
        in_maps.append({
            "xh": _shard_tok_T(x2, c, halo_left=1),
            "wrT": wrT, "wkT": wkT, "wvT": wvT, "mur": mur, "muk": muk,
        })
    res = run_spmd(nc, in_maps, core_ids=list(range(NCORES)))
    x3 = np.empty((B, L, D), np.float32)
    for c in range(NCORES):
        b, h = c // 2, c % 2
        x3[b, h * TK:(h + 1) * TK] = res.results[c]["x3T"].T
    return x3


_PROGS = {}
_PAR_REGS = {}


def _parity_reg(e):
    key = (id(e.bass), str(e.engine))
    if key not in _PAR_REGS:
        pid = e.partition_id()
        preg = e.alloc_register("parreg")
        e.reg_alu(preg, e.to_reg(pid), 2, ALU.mod)
        _PAR_REGS[key] = preg
    return _PAR_REGS[key]


def par_branch(e, emit0, emit1, bal=()):
    preg = _parity_reg(e)
    marks = [c.n for c in bal]
    with e.If_eq(preg, 0):
        emit0()
    after = [c.n for c in bal]
    for c, m in zip(bal, marks):
        c.n = m
    with e.Else():
        emit1()
    for c, a in zip(bal, after):
        assert c.n == a, "unbalanced branch sem counts"


def _f_ctx(F, nc, es, semnames, need_sel, selname=None):
    if F is None:
        consts = make_consts(nc)
        hs = {n: es.enter_context(nc.semaphore(n)) for n in semnames}
        block = es.enter_context(nc.Block())
        ld = Ctr(hs["ld"])
        wld = Ctr(hs["wld"]) if "wld" in hs else None
        ctrs = {"pe": Ctr(hs["pe"]), "sca": Ctr(hs["sca"]), "vec": Ctr(hs["vec"])}
        sto = Ctr(hs["sto"])
        load_consts(nc, block, consts, ld, need_sel=need_sel)
        return block, consts, ld, wld, ctrs, sto
    return F["block"], F["consts"], F["ld"], F["wld"], F["ctrs"], F["sto"]


def kernel(**inputs):
    if FUSED:
        return kernel_fused(inputs)
    x = np.ascontiguousarray(inputs["L0_attn_x"], dtype=np.float32)
    pre = run_mamba_pre(x, inputs)
    ys = run_scan(pre, inputs)
    x2 = run_mamba_post(x, pre, ys, inputs)
    x3 = run_rwkv(x2, inputs)
    x4 = run_attn(x3, inputs)
    out = run_moe(x4, inputs)
    return out


def ln_stream_T(nc, block, ctrs, ld, x_dram, nt, Tloc, out_tiles, consts, bufs,
                xch, w_cols=None, b_cols=None):
    t_ctr, s_ctr, v_ctr = ctrs["pe"], ctrs["sca"], ctrs["vec"]
    xb, xsq = bufs["xb"], bufs["xsq"]
    row_mu, row_ms, row_rs = bufs["row_mu"], bufs["row_ms"], bufs["row_rs"]
    ps_sum, ps_sq = bufs["ps_sum"], bufs["ps_sq"]
    ps_mu, ps_rs = bufs["ps_mu"], bufs["ps_rs"]
    c_starts = [i * CH for i in range(Tloc // CH)]
    if Tloc % CH:
        c_starts.append(Tloc - CH)

    for c0 in c_starts:
        w = CH
        cs = slice(c0, c0 + w)

        @block.sync
        def _(q):
            s_ctr.wait(q)
            for i in range(nt):
                ld.dma(q.dma_start(out=xch[i][:, 0:w], in_=x_dram.ap()[i * 128:(i + 1) * 128, cs]))

        @block.scalar
        def _(s):
            ld.wait(s)
            t_ctr.wait(s)
            for i in range(nt):
                s.copy(xb[i][:, 0:w], xch[i][:, 0:w])
                ins = s.activation(out=xsq[i][:, 0:w], in_=xch[i][:, 0:w], func=AF.Square)
            s_ctr.inc(ins)

        @block.tensor
        def _(t):
            s_ctr.wait(t)
            for i in range(nt):
                t.matmul(ps_sum[0:1, 0:w], consts["ones_col_bf"][:], xb[i][:, 0:w],
                         start=(i == 0), stop=(i == nt - 1))
            for i in range(nt):
                ins = t.matmul(ps_sq[0:1, 0:w], consts["ones_col_bf"][:], xsq[i][:, 0:w],
                               start=(i == 0), stop=(i == nt - 1))
            t_ctr.inc(ins)

        @block.scalar
        def _(s):
            t_ctr.wait(s)
            s.activation(out=row_mu[0:1, cs], in_=ps_sum[0:1, 0:w], func=AF.Copy,
                         scale=1.0 / (nt * 128.0))
            ins = s.activation(out=row_ms[0:1, cs], in_=ps_sq[0:1, 0:w], func=AF.Copy,
                               scale=1.0 / (nt * 128.0))
            s_ctr.inc(ins)

    @block.vector
    def _(v):
        s_ctr.wait(v)
        v.tensor_tensor(bufs["row_d"][:], row_mu[:], row_mu[:], ALU.mult)
        ins = v.tensor_sub(bufs["row_d"][:], row_ms[:], bufs["row_d"][:])
        v_ctr.inc(ins)

    @block.scalar
    def _(s):
        v_ctr.wait(s)
        ins = s.activation(out=row_rs[:], in_=bufs["row_d"][:], func=AF.Ln, bias=consts["eps11"][:])
        s_ctr.inc(ins)
        s_ctr.wait(s)
        ins = s.activation(out=row_rs[:], in_=row_rs[:], func=AF.Exp, scale=-0.5)
        s_ctr.inc(ins)

    for c0 in c_starts:
        w = CH
        cs = slice(c0, c0 + w)

        @block.sync
        def _(q):
            v_ctr.wait(q)
            for i in range(nt):
                ld.dma(q.dma_start(out=xch[i][:, 0:w], in_=x_dram.ap()[i * 128:(i + 1) * 128, cs]))

        @block.tensor
        def _(t):
            s_ctr.wait(t)
            v_ctr.wait(t)
            t.matmul(ps_mu[:, 0:w], consts["ones_row_f32"][:], row_mu[0:1, cs], start=True, stop=True)
            ins = t.matmul(ps_rs[:, 0:w], consts["ones_row_f32"][:], row_rs[0:1, cs], start=True, stop=True)
            t_ctr.inc(ins)

        @block.vector
        def _(v):
            t_ctr.wait(v)
            ld.wait(v)
            for i in range(nt):
                v.tensor_sub(bufs["tmp"][:, 0:w], xch[i][:, 0:w], ps_mu[:, 0:w])
                if w_cols is None:
                    ins = v.tensor_mul(out_tiles[i][:, cs], bufs["tmp"][:, 0:w], ps_rs[:, 0:w])
                else:
                    v.tensor_mul(bufs["tmp"][:, 0:w], bufs["tmp"][:, 0:w], ps_rs[:, 0:w])
                    ins = v.tensor_scalar(out=out_tiles[i][:, cs], in0=bufs["tmp"][:, 0:w],
                                          scalar1=w_cols[i], scalar2=b_cols[i],
                                          op0=ALU.mult, op1=ALU.add)
            v_ctr.inc(ins)


def build_mamba_pre(F=None):
    standalone = F is None
    nc = bass.Bass("TRN2", target_bir_lowering=False, debug=False, num_devices=8) if standalone else F["nc"]
    TE = TK + 6
    DIT = DI // 128
    xh_d = nc.dram_tensor("xhT", [D, TE], F32, kind="ExternalInput")
    lnw_d = nc.dram_tensor("lnw", [D, 1], F32, kind="ExternalInput")
    lnbb_d = nc.dram_tensor("lnb", [D, 1], F32, kind="ExternalInput")
    lm_d = nc.dram_tensor("lmask", [128, 1], F32, kind="ExternalInput")
    rm_d = nc.dram_tensor("rmask", [128, 1], F32, kind="ExternalInput")
    dirs = ("f", "b")
    din = {}
    dout = {}
    for d in dirs:
        din[d] = {
            "inWT": nc.dram_tensor(f"inWT_{d}", [D, 2 * DI], F32, kind="ExternalInput"),
            "convW": nc.dram_tensor(f"convW_{d}", [DI, 4], F32, kind="ExternalInput"),
            "convb": nc.dram_tensor(f"convb_{d}", [DI, 1], F32, kind="ExternalInput"),
            "xpdtT": nc.dram_tensor(f"xpdtT_{d}", [DI, RK], F32, kind="ExternalInput"),
            "xpBT": nc.dram_tensor(f"xpBT_{d}", [DI, DS], F32, kind="ExternalInput"),
            "xpCT": nc.dram_tensor(f"xpCT_{d}", [DI, DS], F32, kind="ExternalInput"),
            "dtWT": nc.dram_tensor(f"dtWT_{d}", [RK, DI], F32, kind="ExternalInput"),
            "dtb": nc.dram_tensor(f"dtb_{d}", [DI, 1], F32, kind="ExternalInput"),
        }
        if standalone:
            dout[d] = {
                "xs": nc.dram_tensor(f"xs_{d}", [DI, TK], F32, kind="ExternalOutput"),
                "zs": nc.dram_tensor(f"zs_{d}", [DI, TK], F32, kind="ExternalOutput"),
                "dt": nc.dram_tensor(f"dt_{d}", [DI, TK], F32, kind="ExternalOutput"),
                "Bm": nc.dram_tensor(f"Bm_{d}", [DS, TK], F32, kind="ExternalOutput"),
                "Cm": nc.dram_tensor(f"Cm_{d}", [DS, TK], F32, kind="ExternalOutput"),
            }
        else:
            dout[d] = {
                "xs": F["io"][f"sh_xs_{d}"], "dt": F["io"][f"sh_dt_{d}"],
                "Bm": F["io"][f"sh_Bm_{d}"], "Cm": F["io"][f"sh_Cm_{d}"],
                "zs": F["io"][f"zs_{d}"],
            }

    normed = sb_tiles(nc, "normed", DT, 128, TE, BF16)
    xch = sb_tiles(nc, "xch", DT, 128, CH, F32)
    wz = sb_tiles(nc, "wz", DT, 128, DI, BF16)
    xspre = sb_tiles(nc, "xspre", DIT, 128, CH + 6, BF16)
    xs_ch = sb_tiles(nc, "xs_ch", DIT, 128, CH, BF16)
    lncols = sb_tiles(nc, "lncol", DT, 128, 1, F32)
    lnbcols = sb_tiles(nc, "lnbcol", DT, 128, 1, F32)
    lmask = sb(nc, "lmask_s", 128, 1, F32)
    rmask = sb(nc, "rmask_s", 128, 1, F32)
    convw = {d: sb_tiles(nc, f"convw_{d}", DIT, 128, 4, F32) for d in dirs}
    convb = {d: sb_tiles(nc, f"convb_{d}", DIT, 128, 1, F32) for d in dirs}
    dtb = {d: sb_tiles(nc, f"dtb_{d}", DIT, 128, 1, F32) for d in dirs}
    xpdt = {d: sb_tiles(nc, f"xpdt_{d}", DIT, 128, RK, BF16) for d in dirs}
    xpB = {d: sb_tiles(nc, f"xpB_{d}", DIT, 128, DS, BF16) for d in dirs}
    xpC = {d: sb_tiles(nc, f"xpC_{d}", DIT, 128, DS, BF16) for d in dirs}
    dtw = {d: sb(nc, f"dtw_{d}", RK, DI, BF16) for d in dirs}
    zstage = [sb(nc, f"zstage{k}", 128, CH, F32) for k in range(2)]
    xsstage = [sb(nc, f"xsstage{k}", 128, CH, F32) for k in range(2)]
    acc = [sb(nc, f"acc{k}", 128, CH, F32) for k in range(2)]
    dtraw = sb(nc, "dtraw", RK, CH, BF16)
    bstage = sb(nc, "bstage", DS, CH, F32)
    cstage = sb(nc, "cstage", DS, CH, F32)
    sp1 = sb(nc, "sp1", 128, CH, F32)
    sp2 = sb(nc, "sp2", 128, CH, F32)
    sp3 = sb(nc, "sp3", 128, CH, F32)
    lnbufs = ln_alloc(nc, "m", DT, TE)

    psA = psum(nc, "psA", 128, CH)
    psB = psum(nc, "psB", 128, CH)
    psH = psum(nc, "psH", 128, 8)
    psDT = psum(nc, "psDT", RK, CH)

    import contextlib
    es = contextlib.ExitStack()
    block, consts, ld, wld, ctrs, sto = _f_ctx(
        F, nc, es, ["ld", "wld", "pe", "sca", "vec", "sto"], need_sel=False)
    with es:

        @block.sync
        def _(q):
            for i in range(DT):
                ld.dma(q.dma_start(out=lncols[i][:], in_=lnw_d.ap()[i * 128:(i + 1) * 128, :]))
                ld.dma(q.dma_start(out=lnbcols[i][:], in_=lnbb_d.ap()[i * 128:(i + 1) * 128, :]))
            ld.dma(q.dma_start(out=lmask[:], in_=lm_d.ap()))
            ld.dma(q.dma_start(out=rmask[:], in_=rm_d.ap()))

        @block.gpsimd
        def _(g):
            for d in dirs:
                for i in range(DIT):
                    r = slice(i * 128, (i + 1) * 128)
                    ld.dma(g.dma_start(out=convw[d][i][:], in_=din[d]["convW"].ap()[r, :]))
                    ld.dma(g.dma_start(out=convb[d][i][:], in_=din[d]["convb"].ap()[r, :]))
                    ld.dma(g.dma_start(out=dtb[d][i][:], in_=din[d]["dtb"].ap()[r, :]))
                    ld.dma(g.dma_start(out=xpdt[d][i][:], in_=din[d]["xpdtT"].ap()[r, :]))
                    ld.dma(g.dma_start(out=xpB[d][i][:], in_=din[d]["xpBT"].ap()[r, :]))
                    ld.dma(g.dma_start(out=xpC[d][i][:], in_=din[d]["xpCT"].ap()[r, :]))
                ld.dma(g.dma_start(out=dtw[d][:], in_=din[d]["dtWT"].ap()))

        ln_stream_T(nc, block, ctrs, ld, xh_d, DT, TE, normed, consts, lnbufs, xch,
                    w_cols=[lncols[i][:] for i in range(DT)],
                    b_cols=[lnbcols[i][:] for i in range(DT)])

        @block.vector
        def _(v):
            ld.wait(v)
            for i in range(DT):
                v.tensor_scalar_mul(normed[i][:, 0:3], normed[i][:, 0:3], lmask[:])
                ins = v.tensor_scalar_mul(normed[i][:, TE - 3:TE], normed[i][:, TE - 3:TE], rmask[:])
            ctrs["vec"].inc(ins)

        for d in dirs:
            @block.gpsimd
            def _(g):
                ctrs["pe"].wait(g)
                for i in range(DT):
                    wld.dma(g.dma_start(out=wz[i][:], in_=din[d]["inWT"].ap()[i * 128:(i + 1) * 128, DI:2 * DI]))

            for c in range(NCH):
                tok = slice(3 + c * CH, 3 + (c + 1) * CH)
                for j in range(DIT):
                    ps = psA if j % 2 == 0 else psB
                    jc = slice(j * 128, (j + 1) * 128)

                    @block.tensor
                    def _(t):
                        wld.wait(t)
                        ctrs["vec"].wait(t)
                        ctrs["sca"].wait_relax(t, c * DIT + j)
                        for i in range(DT):
                            ins = t.matmul(ps[:], wz[i][:, jc], normed[i][:, tok],
                                           start=(i == 0), stop=(i == DT - 1))
                        ctrs["pe"].inc(ins)

                    @block.scalar
                    def _(s):
                        ctrs["pe"].wait(s)
                        sto.wait(s, sto.n - 16 if sto.n >= 16 else 0)
                        zst = zstage[j % 2]
                        ins = s.activation(out=zst[:], in_=ps[:], func=AF.Silu)
                        ctrs["sca"].inc(ins)

                    @block.sync
                    def _(q):
                        ctrs["sca"].wait(q)
                        sto.dma(q.dma_start(out=dout[d]["zs"].ap()[jc, c * CH:(c + 1) * CH],
                                            in_=zstage[j % 2][:]))

            @block.gpsimd
            def _(g):
                ctrs["pe"].wait(g)
                for i in range(DT):
                    wld.dma(g.dma_start(out=wz[i][:], in_=din[d]["inWT"].ap()[i * 128:(i + 1) * 128, 0:DI]))

            for c in range(NCH):
                ext = slice(c * CH, c * CH + CH)
                ext2 = slice(c * CH + CH, c * CH + CH + 6)

                for j in range(DIT):
                    ps = psA if j % 2 == 0 else psB
                    jc = slice(j * 128, (j + 1) * 128)

                    @block.tensor
                    def _(t):
                        wld.wait(t)
                        ctrs["sca"].wait(t)
                        for i in range(DT):
                            t.matmul(ps[:], wz[i][:, jc], normed[i][:, ext],
                                     start=(i == 0), stop=(i == DT - 1))
                        for i in range(DT):
                            ins = t.matmul(psH[:, 0:6], wz[i][:, jc], normed[i][:, ext2],
                                           start=(i == 0), stop=(i == DT - 1))
                        ctrs["pe"].inc(ins)

                    @block.scalar
                    def _(s):
                        ctrs["pe"].wait(s)
                        ctrs["vec"].wait(s)
                        s.copy(xspre[j][:, 0:CH], ps[:])
                        ins = s.copy(xspre[j][:, CH:CH + 6], psH[:, 0:6])
                        ctrs["sca"].inc(ins)

                for j in range(DIT):
                    a0, a1 = acc[0], acc[1]

                    @block.vector
                    def _(v):
                        ctrs["sca"].wait(v)
                        shifts = (0, 1, 2, 3) if d == "f" else (6, 5, 4, 3)
                        v.tensor_scalar_mul(a0[:], xspre[j][:, shifts[0]:shifts[0] + CH],
                                            convw[d][j][:, 0:1])
                        for tap in (1, 2, 3):
                            sh = shifts[tap]
                            v.tensor_scalar_mul(a1[:], xspre[j][:, sh:sh + CH], convw[d][j][:, tap:tap + 1])
                            ins = v.tensor_add(a0[:], a0[:], a1[:])
                        ctrs["vec"].inc(ins)

                    @block.scalar
                    def _(s):
                        ctrs["vec"].wait(s)
                        ctrs["pe"].wait(s)
                        sto.wait(s, max(0, sto.n - 16))
                        xst = xsstage[j % 2]
                        s.activation(out=xst[:], in_=a0[:], func=AF.Silu, bias=convb[d][j][:])
                        ins = s.activation(out=xs_ch[j][:], in_=a0[:], func=AF.Silu, bias=convb[d][j][:])
                        ctrs["sca"].inc(ins)

                    @block.sync
                    def _(q):
                        ctrs["sca"].wait(q)
                        if standalone:
                            sto.dma(q.dma_start(out=dout[d]["xs"].ap()[j * 128:(j + 1) * 128, c * CH:(c + 1) * CH],
                                                in_=xsstage[j % 2][:]))
                        else:
                            par_branch(q,
                                lambda: sto.dma(q.dma_start(out=dout[d]["xs"].ap()[j * 128:(j + 1) * 128, c * CH:(c + 1) * CH], in_=xsstage[j % 2][:])),
                                lambda: sto.dma(q.dma_start(out=dout[d]["xs"].ap()[j * 128:(j + 1) * 128, TK + c * CH:TK + (c + 1) * CH], in_=xsstage[j % 2][:])),
                                bal=(sto,))

                @block.tensor
                def _(t):
                    ctrs["sca"].wait(t)
                    for i in range(DIT):
                        ins = t.matmul(psDT[:], xpdt[d][i][:], xs_ch[i][:],
                                       start=(i == 0), stop=(i == DIT - 1))
                    ctrs["pe"].inc(ins)

                @block.scalar
                def _(s):
                    ctrs["pe"].wait(s)
                    ins = s.copy(dtraw[:], psDT[:])
                    ctrs["sca"].inc(ins)

                @block.tensor
                def _(t):
                    ctrs["sca"].wait(t)
                    for i in range(DIT):
                        ins = t.matmul(psDT[0:DS, :], xpB[d][i][:], xs_ch[i][:],
                                       start=(i == 0), stop=(i == DIT - 1))
                    ctrs["pe"].inc(ins)

                @block.scalar
                def _(s):
                    ctrs["pe"].wait(s)
                    sto.wait(s, max(0, sto.n - 32))
                    ins = s.copy(bstage[:], psDT[0:DS, :])
                    ctrs["sca"].inc(ins)

                @block.sync
                def _(q):
                    ctrs["sca"].wait(q)
                    if standalone:
                        sto.dma(q.dma_start(out=dout[d]["Bm"].ap()[:, c * CH:(c + 1) * CH], in_=bstage[:]))
                    else:
                        par_branch(q,
                            lambda: sto.dma(q.dma_start(out=dout[d]["Bm"].ap()[:, c * CH:(c + 1) * CH], in_=bstage[:])),
                            lambda: sto.dma(q.dma_start(out=dout[d]["Bm"].ap()[:, TK + c * CH:TK + (c + 1) * CH], in_=bstage[:])),
                            bal=(sto,))

                @block.tensor
                def _(t):
                    ctrs["sca"].wait(t)
                    for i in range(DIT):
                        ins = t.matmul(psDT[0:DS, :], xpC[d][i][:], xs_ch[i][:],
                                       start=(i == 0), stop=(i == DIT - 1))
                    ctrs["pe"].inc(ins)

                @block.scalar
                def _(s):
                    ctrs["pe"].wait(s)
                    sto.wait(s, max(0, sto.n - 32))
                    ins = s.copy(cstage[:], psDT[0:DS, :])
                    ctrs["sca"].inc(ins)

                @block.sync
                def _(q):
                    ctrs["sca"].wait(q)
                    if standalone:
                        sto.dma(q.dma_start(out=dout[d]["Cm"].ap()[:, c * CH:(c + 1) * CH], in_=cstage[:]))
                    else:
                        par_branch(q,
                            lambda: sto.dma(q.dma_start(out=dout[d]["Cm"].ap()[:, c * CH:(c + 1) * CH], in_=cstage[:])),
                            lambda: sto.dma(q.dma_start(out=dout[d]["Cm"].ap()[:, TK + c * CH:TK + (c + 1) * CH], in_=cstage[:])),
                            bal=(sto,))

                for j in range(DIT):
                    ps = psA if j % 2 == 0 else psB
                    jc = slice(j * 128, (j + 1) * 128)

                    @block.tensor
                    def _(t):
                        ctrs["sca"].wait(t)
                        ctrs["vec"].wait(t)
                        ins = t.matmul(ps[:], dtw[d][0:RK, jc], dtraw[:], start=True, stop=True)
                        ctrs["pe"].inc(ins)

                    @block.vector
                    def _(v):
                        ctrs["pe"].wait(v)
                        sto.wait(v)
                        ins = v.tensor_scalar_add(sp1[:], ps[:], dtb[d][j][:])
                        ctrs["vec"].inc(ins)

                    @block.scalar
                    def _(s):
                        ctrs["vec"].wait(s)
                        ins = s.activation(out=sp2[:], in_=sp1[:], func=AF.Abs)
                        ctrs["sca"].inc(ins)
                        ctrs["sca"].wait(s)
                        ins = s.activation(out=sp3[:], in_=sp2[:], func=AF.Exp, scale=-1.0)
                        ctrs["sca"].inc(ins)
                        ctrs["sca"].wait(s)
                        ins = s.activation(out=sp2[:], in_=sp3[:], func=AF.Ln, bias=consts["ones_col_f32"][:])
                        ctrs["sca"].inc(ins)

                    @block.vector
                    def _(v):
                        ctrs["sca"].wait(v)
                        v.tensor_scalar_max(sp3[:], sp1[:], 0.0)
                        ins = v.tensor_add(sp1[:], sp3[:], sp2[:])
                        ctrs["vec"].inc(ins)

                    @block.sync
                    def _(q):
                        ctrs["vec"].wait(q)
                        if standalone:
                            sto.dma(q.dma_start(out=dout[d]["dt"].ap()[jc, c * CH:(c + 1) * CH], in_=sp1[:]))
                        else:
                            par_branch(q,
                                lambda: sto.dma(q.dma_start(out=dout[d]["dt"].ap()[jc, c * CH:(c + 1) * CH], in_=sp1[:])),
                                lambda: sto.dma(q.dma_start(out=dout[d]["dt"].ap()[jc, TK + c * CH:TK + (c + 1) * CH], in_=sp1[:])),
                                bal=(sto,))

        @block.sync
        def _(q):
            sto.wait(q)
    return nc


def run_mamba_pre(x, inputs):
    nc = build_mamba_pre()
    base = {}
    for d, pref in (("f", "L0_attn_f_"), ("b", "L0_attn_b_")):
        base[f"inWT_{d}"] = np.ascontiguousarray(inputs[pref + "inW"].T)
        base[f"convW_{d}"] = np.ascontiguousarray(inputs[pref + "convW"].reshape(DI, 4))
        base[f"convb_{d}"] = np.ascontiguousarray(inputs[pref + "convb"].reshape(DI, 1))
        xp = inputs[pref + "xprojW"]
        base[f"xpdtT_{d}"] = np.ascontiguousarray(xp[0:RK].T)
        base[f"xpBT_{d}"] = np.ascontiguousarray(xp[RK:RK + DS].T)
        base[f"xpCT_{d}"] = np.ascontiguousarray(xp[RK + DS:RK + 2 * DS].T)
        base[f"dtWT_{d}"] = np.ascontiguousarray(inputs[pref + "dtW"].T)
        base[f"dtb_{d}"] = np.ascontiguousarray(inputs[pref + "dtb"].reshape(DI, 1))
    base["lnw"] = np.ascontiguousarray(inputs["L0_attn_ln_w"].reshape(D, 1))
    base["lnb"] = np.ascontiguousarray(inputs["L0_attn_ln_b"].reshape(D, 1))
    import ml_dtypes
    in_maps = []
    for c in range(NCORES):
        m = dict(base)
        b, h = c // 2, c % 2
        m["xhT"] = _shard_tok_T(x, c, halo_left=3, halo_right=3)
        m["lmask"] = np.full((128, 1), 0.0 if h == 0 else 1.0, np.float32)
        m["rmask"] = np.full((128, 1), 0.0 if h == 1 else 1.0, np.float32)
        in_maps.append(m)
    res = run_spmd(nc, in_maps, core_ids=list(range(NCORES)))
    out = {}
    for name, rows in (("xs", DI), ("zs", DI), ("dt", DI), ("Bm", DS), ("Cm", DS)):
        for d in ("f", "b"):
            a = np.empty((B, L, rows), np.float32)
            for c in range(NCORES):
                b_, h = c // 2, c % 2
                a[b_, h * TK:(h + 1) * TK] = res.results[c][f"{name}_{d}"].T
            out[f"{name}_{d}"] = a
    return out


SCH = 256
SW = DS * (SCH + 1)


def build_scan():
    nc = bass.Bass("TRN2", target_bir_lowering=False, debug=False, num_devices=8)
    DIT = DI // 128
    NSC = L // SCH
    dt_d = nc.dram_tensor("dt", [DI, L], F32, kind="ExternalInput")
    xs_d = nc.dram_tensor("xs", [DI, L], F32, kind="ExternalInput")
    bm_d = nc.dram_tensor("bm", [DS, L], F32, kind="ExternalInput")
    cm_d = nc.dram_tensor("cm", [DS, L], F32, kind="ExternalInput")
    al_d = nc.dram_tensor("A_log", [DI, DS], F32, kind="ExternalInput")
    y_d = nc.dram_tensor("y", [DI, L], F32, kind="ExternalOutput")

    A = sb_tiles(nc, "A", DIT, 128, DS, F32)
    hstate = sb_tiles(nc, "hstate", DIT, 128, DS, F32)
    bmr = sb(nc, "bmr", DS, L, F32)
    cmr = sb(nc, "cmr", DS, L, F32)
    bbc = sb(nc, "bbc", 128, SW, F32)
    cbc = sb(nc, "cbc", 128, SW, F32)
    dtc = [sb(nc, f"dtc{k}", 128, SCH, F32) for k in range(2)]
    xsc = [sb(nc, f"xsc{k}", 128, SCH, F32) for k in range(2)]
    wt = sb(nc, "wt", 128, SCH, F32)
    da = sb(nc, "da", 128, SW, F32)
    dbx = sb(nc, "dbx", 128, SW, F32)
    hh = sb(nc, "hh", 128, SW, F32)
    yt = [sb(nc, f"yt{k}", 128, SCH, F32) for k in range(2)]
    psb = psum(nc, "psb", 128, SCH)
    psc = psum(nc, "psc", 128, SCH)

    with (
        nc.semaphore("ld") as ld_h,
        nc.semaphore("pe") as pe_h,
        nc.semaphore("sca") as sca_h,
        nc.semaphore("vec") as vec_h,
        nc.semaphore("sto") as sto_h,
        nc.Block() as block,
    ):
        ld = Ctr(ld_h)
        ctrs = {"pe": Ctr(pe_h), "sca": Ctr(sca_h), "vec": Ctr(vec_h)}
        pe, sca, vec = ctrs["pe"], ctrs["sca"], ctrs["vec"]
        sto = Ctr(sto_h)
        consts = make_consts(nc)
        load_consts(nc, block, consts, ld)

        @block.sync
        def _(q):
            for i in range(DIT):
                ld.dma(q.dma_start(out=A[i][:], in_=al_d.ap()[i * 128:(i + 1) * 128, :]))
            ld.dma(q.dma_start(out=bmr[:], in_=bm_d.ap()))
            ld.dma(q.dma_start(out=cmr[:], in_=cm_d.ap()))

        @block.vector
        def _(v):
            ld.wait(v)
            for i in range(DIT):
                ins = v.memset(hstate[i][:], 0.0)
            vec.inc(ins)

        @block.scalar
        def _(s):
            ld.wait(s)
            for i in range(DIT):
                ins = s.activation(out=A[i][:], in_=A[i][:], func=AF.Exp)
            sca.inc(ins)

        @block.vector
        def _(v):
            sca.wait(v)
            for i in range(DIT):
                ins = v.tensor_scalar_mul(A[i][:], A[i][:], -1.0)
            vec.inc(ins)

        @block.vector
        def _(v):
            slots = bass.AP(da, 0, [list(da[:].ap[0]), [SCH + 1, DS]])
            ins = v.memset(slots, 0.0)
            vec.inc(ins)

        for c in range(NSC):
            tc = slice(c * SCH, (c + 1) * SCH)
            for s in range(DS):
                reg = slice(s * (SCH + 1) + 1, s * (SCH + 1) + 1 + SCH)

                @block.tensor
                def _(t):
                    ld.wait(t)
                    vec.wait(t)
                    sca.wait(t)
                    selp = consts["sel16"][0:DS, s * 128:(s + 1) * 128]
                    t.matmul(psb[:], selp, bmr[0:DS, tc], start=True, stop=True)
                    ins = t.matmul(psc[:], selp, cmr[0:DS, tc], start=True, stop=True)
                    pe.inc(ins)

                @block.scalar
                def _(s_):
                    pe.wait(s_)
                    s_.copy(bbc[:, reg], psb[:])
                    ins = s_.copy(cbc[:, reg], psc[:])
                    sca.inc(ins)

            for i in range(DIT):
                buf = c % 2

                @block.sync
                def _(q):
                    vec.wait(q)
                    ld.dma(q.dma_start(out=dtc[buf][:], in_=dt_d.ap()[i * 128:(i + 1) * 128, tc]))
                    ld.dma(q.dma_start(out=xsc[buf][:], in_=xs_d.ap()[i * 128:(i + 1) * 128, tc]))

                @block.scalar
                def _(s_):
                    ld.wait(s_)
                    vec.wait(s_)
                    for s in range(DS):
                        reg = slice(s * (SCH + 1) + 1, s * (SCH + 1) + 1 + SCH)
                        ins = s_.activation(out=da[:, reg], in_=dtc[buf][:], func=AF.Exp,
                                            scale=A[i][:, s:s + 1])
                    sca.inc(ins)

                @block.vector
                def _(v):
                    ld.wait(v)
                    sca.wait(v)
                    v.tensor_mul(wt[:], dtc[buf][:], xsc[buf][:])
                    slots = bass.AP(dbx, 0, [list(dbx[:].ap[0]), [SCH + 1, DS]])
                    v.tensor_copy(slots, hstate[i][:])
                    for s in range(DS):
                        reg = slice(s * (SCH + 1) + 1, s * (SCH + 1) + 1 + SCH)
                        v.tensor_tensor(dbx[:, reg], wt[:], bbc[:, reg], ALU.mult)
                    v.tensor_tensor_scan(out=hh[:], data0=da[:], data1=dbx[:],
                                         initial=0.0, op0=ALU.mult, op1=ALU.add)
                    laststates = bass.AP(hh, SCH, [list(hh[:].ap[0]), [SCH + 1, DS]])
                    v.tensor_copy(hstate[i][:], laststates)
                    for s in range(DS):
                        reg = slice(s * (SCH + 1) + 1, s * (SCH + 1) + 1 + SCH)
                        v.tensor_tensor(dbx[:, reg], hh[:, reg], cbc[:, reg], ALU.mult)
                    sto.wait(v, max(0, sto.n - 16))
                    cyv = bass.AP(dbx, 1, [list(dbx[:].ap[0]), [1, SCH], [SCH + 1, DS]])
                    ins = v.tensor_reduce(yt[i % 2][:], cyv, axis=mybir.AxisListType.X, op=ALU.add)
                    vec.inc(ins)

                @block.sync
                def _(q):
                    vec.wait(q)
                    sto.dma(q.dma_start(out=y_d.ap()[i * 128:(i + 1) * 128, tc], in_=yt[i % 2][:]))

        @block.sync
        def _(q):
            sto.wait(q)
    return nc


def run_scan(pre, inputs):
    nc = build_scan()
    in_maps = []
    for c in range(NCORES):
        b, d = c % 4, ("f" if c < 4 else "b")
        rev = d == "b"

        def prep(a):
            a = a[::-1] if rev else a
            return np.ascontiguousarray(a.T)
        m = {
            "dt": prep(pre[f"dt_{d}"][b]),
            "xs": prep(pre[f"xs_{d}"][b]),
            "A_log": np.ascontiguousarray(inputs[f"L0_attn_{d}_A_log"]),
        }
        m["bm"] = prep(pre[f"Bm_{d}"][b])
        m["cm"] = prep(pre[f"Cm_{d}"][b])
        in_maps.append(m)
    res = run_spmd(nc, in_maps, core_ids=list(range(NCORES)))
    out = {}
    for d in ("f", "b"):
        y = np.empty((B, L, DI), np.float32)
        for b in range(4):
            c = b if d == "f" else b + 4
            yy = res.results[c]["y"].T
            y[b] = yy[::-1] if d == "b" else yy
        out[f"y_{d}"] = y
    return out


MPOST_PHASES = 3

def build_mamba_post(F=None):
    standalone = F is None
    nc = bass.Bass("TRN2", target_bir_lowering=False, debug=False, num_devices=8) if standalone else F["nc"]
    DIT = DI // 128
    dirs = ("f", "b")
    din = {}
    for d in dirs:
        din[d] = {
            "Dsk": nc.dram_tensor(f"Dsk_{d}", [DI, 1], F32, kind="ExternalInput"),
            "outWT": nc.dram_tensor(f"outWT_{d}", [DI, D], F32, kind="ExternalInput"),
            "projT": nc.dram_tensor(f"projT_{d}", [D, D], F32, kind="ExternalInput"),
        }
        if standalone:
            din[d]["y"] = nc.dram_tensor(f"y_{d}", [DI, TK], F32, kind="ExternalInput")
            din[d]["xs"] = nc.dram_tensor(f"xs2_{d}", [DI, TK], F32, kind="ExternalInput")
            din[d]["zs"] = nc.dram_tensor(f"zs2_{d}", [DI, TK], F32, kind="ExternalInput")
        else:
            din[d]["y"] = F["io"][f"sh_y_{d}"]
            din[d]["xs"] = F["io"][f"sh_xs_{d}"]
            din[d]["zs"] = F["io"][f"zs_{d}"]
    projb_d = nc.dram_tensor("projb", [D, 1], F32, kind="ExternalInput")
    xT_d = nc.dram_tensor("xT", [D, TK], F32, kind="ExternalInput")
    if standalone:
        o_d = nc.dram_tensor("x2T", [D, TK], F32, kind="ExternalOutput")
    else:
        o_d = F["io"]["x2T"]

    wbig = sb_tiles(nc, "wbig", DIT, 128, D, BF16)
    fo = sb_tiles(nc, "fo", DT, 128, TK, BF16)
    bo = sb_tiles(nc, "bo", DT, 128, TK, BF16)
    ystr = [sb(nc, f"ystr{k}", 128, CH, F32) for k in range(2)]
    xsstr = [sb(nc, f"xsstr{k}", 128, CH, F32) for k in range(2)]
    zsstr = [sb(nc, f"zsstr{k}", 128, CH, F32) for k in range(2)]
    dcol = {d: sb_tiles(nc, f"dcol_{d}", DIT, 128, 1, F32) for d in dirs}
    pbcol = sb_tiles(nc, "pbcol", DT, 128, 1, F32)
    yfin = sb_tiles(nc, "yfin", DIT, 128, CH, BF16)
    xch = sb_tiles(nc, "xch", DT, 128, CH, F32)
    x2p = sb_tiles(nc, "x2p", DT, 128, CH, F32)
    t1 = sb(nc, "t1", 128, CH, F32)
    lnb = ln_alloc(nc, "p", DT, CH)
    psA = psum(nc, "psA", 128, CH)
    psB = psum(nc, "psB", 128, CH)

    import contextlib
    es = contextlib.ExitStack()
    block, consts, ld, wld, ctrs, sto = _f_ctx(
        F, nc, es, ["ld", "wld", "pe", "sca", "vec", "sto"], need_sel=False)
    with es:
        pe, sca, vec = ctrs["pe"], ctrs["sca"], ctrs["vec"]


        @block.gpsimd
        def _(g):
            for d in dirs:
                for i in range(DIT):
                    ld.dma(g.dma_start(out=dcol[d][i][:], in_=din[d]["Dsk"].ap()[i * 128:(i + 1) * 128, :]))
            for i in range(DT):
                ld.dma(g.dma_start(out=pbcol[i][:], in_=projb_d.ap()[i * 128:(i + 1) * 128, :]))

        if MPOST_PHASES < 1:
            @block.sync
            def _(q):
                ld.wait(q)
            return nc
        for d in dirs:
            dst = fo if d == "f" else bo

            @block.gpsimd
            def _(g):
                pe.wait(g)
                for i in range(DIT):
                    wld.dma(g.dma_start(out=wbig[i][:], in_=din[d]["outWT"].ap()[i * 128:(i + 1) * 128, :]))

            for c in range(NCH):
                tc = slice(c * CH, (c + 1) * CH)
                for i in range(DIT):
                    bu = i % 2

                    @block.sync
                    def _(q):
                        vec.wait_relax(q, i)
                        r = slice(i * 128, (i + 1) * 128)
                        if standalone:
                            ld.dma(q.dma_start(out=ystr[bu][:], in_=din[d]["y"].ap()[r, tc]))
                            ld.dma(q.dma_start(out=xsstr[bu][:], in_=din[d]["xs"].ap()[r, tc]))
                        else:
                            def _l(off):
                                cs2 = slice(off + c * CH, off + (c + 1) * CH)
                                ld.dma(q.dma_start(out=ystr[bu][:], in_=din[d]["y"].ap()[r, cs2]))
                                ld.dma(q.dma_start(out=xsstr[bu][:], in_=din[d]["xs"].ap()[r, cs2]))
                            par_branch(q, lambda: _l(0), lambda: _l(TK), bal=(ld,))
                        ld.dma(q.dma_start(out=zsstr[bu][:], in_=din[d]["zs"].ap()[r, tc]))

                    @block.vector
                    def _(v):
                        ld.wait(v)
                        pe.wait(v)
                        v.tensor_scalar_mul(t1[:], xsstr[bu][:], dcol[d][i][:])
                        v.tensor_add(t1[:], ystr[bu][:], t1[:])
                        ins = v.tensor_mul(yfin[i][:], t1[:], zsstr[bu][:])
                        vec.inc(ins)

                for j in range(DT):
                    ps = psA if j % 2 == 0 else psB
                    jc = slice(j * 128, (j + 1) * 128)

                    @block.tensor
                    def _(t):
                        wld.wait(t)
                        vec.wait(t)
                        sca.wait(t)
                        for i in range(DIT):
                            ins = t.matmul(ps[:], wbig[i][:, jc], yfin[i][:],
                                           start=(i == 0), stop=(i == DIT - 1))
                        pe.inc(ins)

                    @block.scalar
                    def _(s):
                        pe.wait(s)
                        ins = s.copy(dst[j][:, tc], ps[:])
                        sca.inc(ins)

        if MPOST_PHASES < 2:
            @block.sync
            def _(q):
                sto.wait(q)
            return nc
        @block.gpsimd
        def _(g):
            pe.wait(g)
            sca.wait(g)
            for i in range(DT):
                wld.dma(g.dma_start(out=wbig[i][:], in_=din["f"]["projT"].ap()[i * 128:(i + 1) * 128, :]))
                wld.dma(g.dma_start(out=wbig[DT + i][:], in_=din["b"]["projT"].ap()[i * 128:(i + 1) * 128, :]))

        for c in range(NCH):
            tc = slice(c * CH, (c + 1) * CH)

            @block.sync
            def _(q):
                vec.wait(q)
                for i in range(DT):
                    ld.dma(q.dma_start(out=xch[i][:], in_=xT_d.ap()[i * 128:(i + 1) * 128, tc]))

            for j in range(DT):
                ps = psA if j % 2 == 0 else psB
                jc = slice(j * 128, (j + 1) * 128)

                @block.tensor
                def _(t):
                    wld.wait(t)
                    vec.wait_relax(t, j)
                    sca.wait(t)
                    for i in range(DT):
                        t.matmul(ps[:], wbig[i][:, jc], fo[i][:, tc], start=(i == 0), stop=False)
                    for i in range(DT):
                        ins = t.matmul(ps[:], wbig[DT + i][:, jc], bo[i][:, tc],
                                       start=False, stop=(i == DT - 1))
                    pe.inc(ins)

                @block.vector
                def _(v):
                    pe.wait(v)
                    ld.wait(v)
                    v.scalar_tensor_tensor(out=t1[:], in0=xch[j][:], scalar=2.0, in1=ps[:],
                                           op0=ALU.mult, op1=ALU.add)
                    ins = v.tensor_scalar_add(x2p[j][:], t1[:], pbcol[j][:])
                    vec.inc(ins)

            if MPOST_PHASES >= 3:
                ln_plain_T(nc, block, ctrs, x2p, x2p, DT, CH, consts, lnb)

            @block.sync
            def _(q):
                vec.wait(q)
                for i in range(DT):
                    sto.dma(q.dma_start(out=o_d.ap()[i * 128:(i + 1) * 128, tc], in_=x2p[i][:]))
                if (not standalone) and c == NCH - 1:
                    def _st(sl):
                        for i in range(DT):
                            sto.dma(q.dma_start(
                                out=F["io"]["x2c"].ap()[i * 128:(i + 1) * 128, sl],
                                in_=x2p[i][:, CH - 1:CH],
                                allow_slow_non_contiguous=True))
                    par_branch(q, lambda: _st(slice(0, 1)), lambda: _st(slice(1, 2)), bal=(sto,))

        @block.sync
        def _(q):
            sto.wait(q)
    return nc


def run_mamba_post(x, pre, ys, inputs):
    nc = build_mamba_post()
    base = {"projb": np.ascontiguousarray(inputs["L0_attn_proj_b"].reshape(D, 1))}
    pw = inputs["L0_attn_proj_W"]
    for d, pref in (("f", "L0_attn_f_"), ("b", "L0_attn_b_")):
        base[f"Dsk_{d}"] = np.ascontiguousarray(inputs[pref + "Dskip"].reshape(DI, 1))
        base[f"outWT_{d}"] = np.ascontiguousarray(inputs[pref + "outW"].T)
    base["projT_f"] = np.ascontiguousarray(pw[:, 0:D].T)
    base["projT_b"] = np.ascontiguousarray(pw[:, D:2 * D].T)
    in_maps = []
    for c in range(NCORES):
        b, h = c // 2, c % 2
        m = dict(base)
        m["xT"] = _shard_tok_T(x, c)
        for d in ("f", "b"):
            m[f"y_{d}"] = np.ascontiguousarray(ys[f"y_{d}"][b, h * TK:(h + 1) * TK].T)
            m[f"xs2_{d}"] = np.ascontiguousarray(pre[f"xs_{d}"][b, h * TK:(h + 1) * TK].T)
            m[f"zs2_{d}"] = np.ascontiguousarray(pre[f"zs_{d}"][b, h * TK:(h + 1) * TK].T)
        in_maps.append(m)
    res = run_spmd(nc, in_maps, core_ids=list(range(NCORES)))
    x2 = np.empty((B, L, D), np.float32)
    for c in range(NCORES):
        b, h = c // 2, c % 2
        x2[b, h * TK:(h + 1) * TK] = res.results[c]["x2T"].T
    return x2


HEADS, KVH, DH, ROT = 16, 8, 64, 32
ASCALE = DH ** (-0.5)


def _rope_tables(n0, n, scale):
    pos = np.arange(n0, n0 + n, dtype=np.float64)
    inv = 1.0 / 10000.0 ** (np.arange(0, ROT, 2, dtype=np.float64) / ROT)
    cos = np.full((128, n), scale, np.float64)
    sin = np.zeros((128, n), np.float64)
    for half in (0, 64):
        for r in range(ROT):
            f = inv[r // 2]
            cos[half + r] = np.cos(pos * f) * scale
            sin[half + r] = np.sin(pos * f) * scale
    return cos.astype(np.float32), sin.astype(np.float32)


def _rope_perm():
    P = np.zeros((128, 128), np.float32)
    for half in (0, 64):
        for r in range(0, ROT, 2):
            P[half + r + 1, half + r] = -1.0
            P[half + r, half + r + 1] = 1.0
    return P


def build_attn(F=None):
    standalone = F is None
    nc = bass.Bass("TRN2", target_bir_lowering=False, debug=False, num_devices=8) if standalone else F["nc"]
    NKT = L // 128
    if standalone:
        x3f_d = nc.dram_tensor("x3bf", [D, L], F32, kind="ExternalInput")
        x3o_d = nc.dram_tensor("x3own", [D, TK], F32, kind="ExternalInput")
    else:
        x3f_d = F["io"]["sh_x3"]
        x3o_d = F["io"]["x3T"]
    wq_d = nc.dram_tensor("Wq", [D, D], F32, kind="ExternalInput")
    wk_d = nc.dram_tensor("Wk", [D, KVH * DH], F32, kind="ExternalInput")
    wv_d = nc.dram_tensor("Wv", [D, KVH * DH], F32, kind="ExternalInput")
    wo_d = nc.dram_tensor("Wo", [D, D], F32, kind="ExternalInput")
    bo_d = nc.dram_tensor("bo", [D, 1], F32, kind="ExternalInput")
    cosq_d = nc.dram_tensor("cosq", [128, TK], F32, kind="ExternalInput")
    sinq_d = nc.dram_tensor("sinq", [128, TK], F32, kind="ExternalInput")
    cosk_d = nc.dram_tensor("cosk", [128, L], F32, kind="ExternalInput")
    sink_d = nc.dram_tensor("sink", [128, L], F32, kind="ExternalInput")
    ropeP_d = nc.dram_tensor("ropeP", [128, 128], F32, kind="ExternalInput")
    if standalone:
        o_d = nc.dram_tensor("x4T", [D, TK], F32, kind="ExternalOutput")
    else:
        o_d = F["io"]["x4T"]

    KD = KVH * DH
    x3c = sb_tiles(nc, "x3c", DT, 128, CH, BF16)
    wqb = sb_tiles(nc, "wqb", DT, 128, D, BF16)
    wkb = sb_tiles(nc, "wkb", DT, 128, KD, BF16)
    wvb = sb_tiles(nc, "wvb", DT, 128, KD, BF16)
    wob = sb_tiles(nc, "wob", DT, 128, D, BF16)
    bocol = sb_tiles(nc, "bocol", DT, 128, 1, F32)
    ropeP = sb(nc, "ropeP_s", 128, 128, F32)
    ctab = sb(nc, "ctab", 128, CH, F32)
    stab = sb(nc, "stab", 128, CH, F32)
    krope = [sb(nc, f"krope{i}", 128, L, BF16) for i in range(KD // 128)]
    vaug = [sb(nc, f"vaug{i}", 128, KVH * (DH + 1), BF16) for i in range(NKT)]
    qrope = sb_tiles(nc, "qrope", DT, 128, CH, F32)
    qaug = sb(nc, "qaug", DH + 1, CH, BF16)
    kaug = sb(nc, "kaug", DH + 1, L, BF16)
    pT = [sb(nc, f"pT{k}", 128, CH, BF16) for k in range(2)]
    sqv = sb(nc, "sqv", 128, CH, BF16)
    nrow = sb(nc, "nrow", 1, CH, F32)
    brow = sb(nc, "brow", 1, CH, F32)
    krow = sb(nc, "krow", 1, CH, F32)
    kmax = sb(nc, "kmax", 1, 1, F32)
    kcur = sb(nc, "kcur", 1, 1, F32)
    kmaxs = sb_tiles(nc, "kmaxs", KVH, 1, 1, F32)
    orow = sb(nc, "orow", 1, CH, F32)
    of32 = sb(nc, "of32", DH, CH, F32)
    otb = sb_tiles(nc, "otb", DT, 128, CH, BF16)
    x4p = sb_tiles(nc, "x4p", DT, 128, CH, F32)
    t1 = sb(nc, "at1", 128, CH, F32)
    t2 = sb(nc, "at2", 128, CH, F32)
    lnb = ln_alloc(nc, "a", DT, CH, xb=x3c, xsq=otb)
    psA = psum(nc, "psA", 128, CH)
    psB = psum(nc, "psB", 128, CH)
    psO = psum(nc, "psO", DH + 1, CH)
    psR = psum(nc, "psR", DH, CH)

    import contextlib
    es = contextlib.ExitStack()
    block, consts, ld, wld, ctrs, sto = _f_ctx(
        F, nc, es, ["ld", "pe", "sca", "vec", "sto"], need_sel=False)
    with es:
        pe, sca, vec = ctrs["pe"], ctrs["sca"], ctrs["vec"]

        @block.gpsimd
        def _(g):
            for i in range(DT):
                r = slice(i * 128, (i + 1) * 128)
                ld.dma(g.dma_start(out=wqb[i][:], in_=wq_d.ap()[r, :]))
                ld.dma(g.dma_start(out=wkb[i][:], in_=wk_d.ap()[r, :]))
                ld.dma(g.dma_start(out=wvb[i][:], in_=wv_d.ap()[r, :]))
                ld.dma(g.dma_start(out=wob[i][:], in_=wo_d.ap()[r, :]))
                ld.dma(g.dma_start(out=bocol[i][:], in_=bo_d.ap()[r, :]))
            ld.dma(g.dma_start(out=ropeP[:], in_=ropeP_d.ap()))

        for c in range(L // CH):
            tc = slice(c * CH, (c + 1) * CH)

            @block.gpsimd
            def _(g):
                pe.wait(g)
                for i in range(DT):
                    ld.dma(g.dma_start(out=x3c[i][:], in_=x3f_d.ap()[i * 128:(i + 1) * 128, tc]))

            @block.sync
            def _(q):
                vec.wait(q)
                ld.dma(q.dma_start(out=ctab[:], in_=cosk_d.ap()[:, tc]))
                ld.dma(q.dma_start(out=stab[:], in_=sink_d.ap()[:, tc]))

            for j in range(KD // 128):
                ps = psA if j % 2 == 0 else psB
                rps = psB if j % 2 == 0 else psA
                jc = slice(j * 128, (j + 1) * 128)

                @block.tensor
                def _(t):
                    ld.wait(t)
                    sca.wait(t)
                    vec.wait(t)
                    for i in range(DT):
                        ins = t.matmul(ps[:], wkb[i][:, jc], x3c[i][:], start=(i == 0), stop=(i == DT - 1))
                    pe.inc(ins)

                @block.scalar
                def _(s):
                    pe.wait(s)
                    ins = s.copy(t1[:], ps[:])
                    sca.inc(ins)

                @block.tensor
                def _(t):
                    sca.wait(t)
                    ins = t.matmul(rps[:], ropeP[:], t1[:], start=True, stop=True)
                    pe.inc(ins)

                @block.vector
                def _(v):
                    pe.wait(v)
                    sca.wait(v)
                    v.tensor_tensor(t2[:], t1[:], ctab[:], ALU.mult)
                    v.tensor_tensor(t1[:], rps[:], stab[:], ALU.mult)
                    ins = v.tensor_add(krope[j][:, tc], t2[:], t1[:])
                    vec.inc(ins)

            for tt in range(CH // 128):
                kt = c * (CH // 128) + tt
                ps = psA if tt % 2 == 0 else psB

                @block.tensor
                def _(t):
                    vec.wait(t)
                    sca.wait(t)
                    for i in range(DT):
                        ins = t.matmul(ps[:], x3c[i][:, tt * 128:(tt + 1) * 128], wvb[i][:],
                                       start=(i == 0), stop=(i == DT - 1))
                    pe.inc(ins)

                @block.scalar
                def _(s):
                    pe.wait(s)
                    for h2 in range(KVH):
                        ins = s.copy(vaug[kt][:, h2 * (DH + 1):h2 * (DH + 1) + DH],
                                     ps[:, h2 * DH:(h2 + 1) * DH])
                    sca.inc(ins)

        @block.vector
        def _(v):
            sca.wait(v)
            for kt in range(NKT):
                oc = bass.AP(vaug[kt], DH, [list(vaug[kt][:].ap[0]), [DH + 1, KVH]])
                ins = v.memset(oc, 1.0)
            vec.inc(ins)

        for h2 in range(KVH):
            ti, ro = h2 // 2, (h2 % 2) * 64

            for c in range(L // CH):
                @block.scalar
                def _(s):
                    pe.wait(s)
                    vec.wait(s)
                    ins = s.activation(out=sqv[0:DH, 0:CH], in_=krope[ti][ro:ro + DH, c * CH:(c + 1) * CH],
                                       func=AF.Square)
                    sca.inc(ins)

                @block.tensor
                def _(t):
                    sca.wait(t)
                    vec.wait(t)
                    ins = t.matmul(psA[0:1, 0:CH], consts["ones_col_bf"][0:DH, :], sqv[0:DH, 0:CH],
                                   start=True, stop=True)
                    pe.inc(ins)

                @block.vector
                def _(v):
                    pe.wait(v)
                    v.tensor_reduce(kcur[:], psA[0:1, 0:CH], axis=mybir.AxisListType.X, op=ALU.max)
                    if c == 0:
                        ins = v.tensor_copy(kmax[:], kcur[:])
                    else:
                        ins = v.tensor_max(kmax[:], kmax[:], kcur[:])
                    vec.inc(ins)

            @block.scalar
            def _(s):
                vec.wait(s)
                ins = s.activation(out=kmaxs[h2][:], in_=kmax[:], func=AF.Sqrt)
                sca.inc(ins)

        for c in range(NCH):
            tc = slice(c * CH, (c + 1) * CH)

            @block.gpsimd
            def _(g):
                pe.wait(g)
                for i in range(DT):
                    ld.dma(g.dma_start(out=x3c[i][:], in_=x3o_d.ap()[i * 128:(i + 1) * 128, tc]))

            @block.sync
            def _(q):
                vec.wait(q)
                ld.dma(q.dma_start(out=ctab[:], in_=cosq_d.ap()[:, tc]))
                ld.dma(q.dma_start(out=stab[:], in_=sinq_d.ap()[:, tc]))

            for j in range(DT):
                ps = psA if j % 2 == 0 else psB
                rps = psB if j % 2 == 0 else psA
                jc = slice(j * 128, (j + 1) * 128)

                @block.tensor
                def _(t):
                    ld.wait(t)
                    vec.wait(t)
                    sca.wait(t)
                    for i in range(DT):
                        ins = t.matmul(ps[:], wqb[i][:, jc], x3c[i][:], start=(i == 0), stop=(i == DT - 1))
                    pe.inc(ins)

                @block.scalar
                def _(s):
                    pe.wait(s)
                    ins = s.copy(t1[:], ps[:])
                    sca.inc(ins)

                @block.tensor
                def _(t):
                    sca.wait(t)
                    ins = t.matmul(rps[:], ropeP[:], t1[:], start=True, stop=True)
                    pe.inc(ins)

                @block.vector
                def _(v):
                    pe.wait(v)
                    sca.wait(v)
                    v.tensor_tensor(t2[:], t1[:], ctab[:], ALU.mult)
                    v.tensor_tensor(t1[:], rps[:], stab[:], ALU.mult)
                    ins = v.tensor_add(qrope[j][:], t2[:], t1[:])
                    vec.inc(ins)

            for h in range(HEADS):
                ti, ro = h // 2, (h % 2) * 64
                h2 = h // 2

                @block.scalar
                def _(s):
                    vec.wait(s)
                    pe.wait(s)
                    s.copy(qaug[0:DH, :], qrope[ti][ro:ro + DH, :])
                    ins = s.activation(out=sqv[0:DH, 0:CH], in_=qrope[ti][ro:ro + DH, :], func=AF.Square)
                    sca.inc(ins)

                @block.tensor
                def _(t):
                    sca.wait(t)
                    vec.wait(t)
                    ins = t.matmul(psO[0:1, 0:CH], consts["ones_col_bf"][0:DH, :], sqv[0:DH, 0:CH],
                                   start=True, stop=True)
                    pe.inc(ins)

                @block.scalar
                def _(s):
                    pe.wait(s)
                    ins = s.activation(out=nrow[:], in_=psO[0:1, 0:CH], func=AF.Sqrt)
                    sca.inc(ins)

                @block.vector
                def _(v):
                    sca.wait(v)
                    v.tensor_scalar(out=brow[:], in0=nrow[:], scalar1=kmaxs[h2][:],
                                    scalar2=-1.02, op0=ALU.mult, op1=ALU.mult)
                    ins = v.tensor_copy(qaug[DH:DH + 1, :], brow[:])
                    vec.inc(ins)

                if h % 2 == 0:
                    kti, kro = h2 // 2, (h2 % 2) * 64
                    @block.vector
                    def _(v):
                        pe.wait(v)
                        v.tensor_copy(kaug[0:DH, :], krope[kti][kro:kro + DH, :])
                        ins = v.memset(kaug[DH:DH + 1, :], 1.0)
                        vec.inc(ins)

                for kp in range(NKT // 2):
                    kt0, kt1 = 2 * kp, 2 * kp + 1
                    b0, b1 = psA, psB

                    @block.tensor
                    def _(t):
                        vec.wait(t)
                        sca.wait(t)
                        t.matmul(b0[:], kaug[0:DH + 1, kt0 * 128:(kt0 + 1) * 128],
                                 qaug[0:DH + 1, :], start=True, stop=True)
                        ins = t.matmul(b1[:], kaug[0:DH + 1, kt1 * 128:(kt1 + 1) * 128],
                                       qaug[0:DH + 1, :], start=True, stop=True)
                        pe.inc(ins)

                    @block.scalar
                    def _(s):
                        pe.wait(s)
                        s.activation(out=pT[0][:], in_=b0[:], func=AF.Exp)
                        ins = s.activation(out=pT[1][:], in_=b1[:], func=AF.Exp)
                        sca.inc(ins)

                    @block.tensor
                    def _(t):
                        sca.wait(t)
                        t.matmul(psO[0:DH + 1, 0:CH], vaug[kt0][:, h2 * (DH + 1):(h2 + 1) * (DH + 1)],
                                 pT[0][:], start=(kt0 == 0), stop=False)
                        ins = t.matmul(psO[0:DH + 1, 0:CH], vaug[kt1][:, h2 * (DH + 1):(h2 + 1) * (DH + 1)],
                                       pT[1][:], start=False, stop=(kt1 == NKT - 1))
                        pe.inc(ins)

                @block.vector
                def _(v):
                    pe.wait(v)
                    ins = v.reciprocal(orow[:], psO[DH:DH + 1, 0:CH])
                    vec.inc(ins)

                @block.scalar
                def _(s):
                    pe.wait(s)
                    ins = s.copy(of32[:], psO[0:DH, 0:CH])
                    sca.inc(ins)

                @block.tensor
                def _(t):
                    vec.wait(t)
                    sca.wait(t)
                    ins = t.matmul(psR[0:DH, 0:CH], consts["ones_row_f32"][0:1, 0:DH], orow[:],
                                   start=True, stop=True)
                    pe.inc(ins)

                @block.vector
                def _(v):
                    pe.wait(v)
                    sca.wait(v)
                    ins = v.tensor_tensor(otb[ti][ro:ro + DH, :], of32[:], psR[0:DH, 0:CH], ALU.mult)
                    vec.inc(ins)

            xoch = qrope
            @block.sync
            def _(q):
                vec.wait(q)
                sca.wait(q)
                for i in range(DT):
                    ld.dma(q.dma_start(out=xoch[i][:], in_=x3o_d.ap()[i * 128:(i + 1) * 128, tc]))

            for j in range(DT):
                ps = psA if j % 2 == 0 else psB
                jc = slice(j * 128, (j + 1) * 128)

                @block.tensor
                def _(t):
                    vec.wait(t)
                    sca.wait(t)
                    for i in range(DT):
                        ins = t.matmul(ps[:], wob[i][:, jc], otb[i][:], start=(i == 0), stop=(i == DT - 1))
                    pe.inc(ins)

                @block.vector
                def _(v):
                    pe.wait(v)
                    ld.wait(v)
                    v.tensor_scalar_add(t1[:], ps[:], bocol[j][:])
                    ins = v.tensor_add(x4p[j][:], xoch[j][:], t1[:])
                    vec.inc(ins)

            ln_plain_T(nc, block, ctrs, x4p, x4p, DT, CH, consts, lnb)

            @block.sync
            def _(q):
                vec.wait(q)
                for i in range(DT):
                    sto.dma(q.dma_start(out=o_d.ap()[i * 128:(i + 1) * 128, tc], in_=x4p[i][:]))

        @block.sync
        def _(q):
            sto.wait(q)
    return nc


def run_attn(x3, inputs):
    nc = build_attn()
    base = {
        "Wq": np.ascontiguousarray(inputs["L1_attn_Wq"]),
        "Wk": np.ascontiguousarray(inputs["L1_attn_Wk"]),
        "Wv": np.ascontiguousarray(inputs["L1_attn_Wv"]),
        "Wo": np.ascontiguousarray(inputs["L1_attn_Wo"]),
        "bo": np.ascontiguousarray(inputs["L1_attn_bo"].reshape(D, 1)),
        "ropeP": _rope_perm(),
    }
    ck, sk = _rope_tables(0, L, 1.0)
    base["cosk"], base["sink"] = ck, sk
    in_maps = []
    for c in range(NCORES):
        b, h = c // 2, c % 2
        m = dict(base)
        x3T = np.ascontiguousarray(x3[b].T)
        m["x3bf"] = x3T
        m["x3own"] = np.ascontiguousarray(x3T[:, h * TK:(h + 1) * TK])
        cq, sq = _rope_tables(h * TK, TK, ASCALE)
        m["cosq"], m["sinq"] = cq, sq
        in_maps.append(m)
    res = run_spmd(nc, in_maps, core_ids=list(range(NCORES)))
    x4 = np.empty((B, L, D), np.float32)
    for c in range(NCORES):
        b, h = c // 2, c % 2
        x4[b, h * TK:(h + 1) * TK] = res.results[c]["x4T"].T
    return x4


NE = 16
NG, EPG, TKG, TOPK = 4, 4, 2, 4
MSCALE = 2.5
HALF = TK // 2


def build_moe(F=None):
    standalone = F is None
    nc = bass.Bass("TRN2", target_bir_lowering=False, debug=False, num_devices=8) if standalone else F["nc"]
    x4_d = nc.dram_tensor("x4T", [D, TK], F32, kind="ExternalInput") if standalone else F["io"]["x4T"]
    rt_d = nc.dram_tensor("routerT", [D, NE], F32, kind="ExternalInput")
    bias_d = nc.dram_tensor("bias16", [1, NE], F32, kind="ExternalInput")
    gw_d = nc.dram_tensor("gate_w", [NE, D, 1024], F32, kind="ExternalInput")
    uw_d = nc.dram_tensor("up_w", [NE, D, 1024], F32, kind="ExternalInput")
    dw_d = nc.dram_tensor("down_w", [NE, 1024, D], F32, kind="ExternalInput")
    shg_d = nc.dram_tensor("shg", [D, 2048], F32, kind="ExternalInput")
    shu_d = nc.dram_tensor("shu", [D, 2048], F32, kind="ExternalInput")
    shd_d = nc.dram_tensor("shd", [2048, D], F32, kind="ExternalInput")
    id_d = nc.dram_tensor("ident128", [128, 128], F32, kind="ExternalInput")
    o_d = nc.dram_tensor("outT", [D, TK], F32, kind="ExternalOutput")

    x4bf = sb_tiles(nc, "m_x4bf", DT, 128, HALF, BF16)
    racc = sb_tiles(nc, "m_racc", DT, 128, HALF, F32)
    wbuf2 = [sb_tiles(nc, f"m_wbuf{k}", DT, 128, 2048, BF16) for k in range(2)]
    hush = sb_tiles(nc, "m_hush", 2 * DT, 128, HALF, BF16)
    hs = hush[0:DT]
    rtb = sb_tiles(nc, "m_rtb", DT, 128, NE, BF16)
    biasr = sb(nc, "m_biasr", 1, NE, F32)
    biasb = sb(nc, "m_biasb", 128, NE, F32)
    ident = sb(nc, "m_ident", 128, 128, F32)
    scoresT = sb(nc, "m_scoresT", NE, HALF, F32)
    combT = sb(nc, "m_combT", NE, HALF, F32)
    stok = sb(nc, "m_stok", 128, NE, F32)
    sbv = sb(nc, "m_sb", 128, NE, F32)
    rank16 = sb(nc, "m_rank16", 128, NE, F32)
    m01 = sb(nc, "m_m01", 128, NE, F32)
    gsc = sb(nc, "m_gsc", 128, NG, F32)
    grank = sb(nc, "m_grank", 128, NG, F32)
    gm01 = sb(nc, "m_gm01", 128, NG, F32)
    emask = sb(nc, "m_emask", 128, NE, F32)
    ones16 = sb(nc, "m_ones16", 128, NE, F32)
    neg1 = sb(nc, "m_neg1", 128, NE, F32)
    den = sb(nc, "m_den", 128, 1, F32)
    thrc = sb(nc, "m_thrc", 128, 1, F32)
    ut = sb(nc, "m_ut", 128, CH, BF16)
    cb = sb(nc, "m_cb", 128, CH, F32)
    xch = sb_tiles(nc, "m_xch", DT, 128, CH, F32)
    x5p = sb_tiles(nc, "m_x5p", DT, 128, CH, F32)
    lnb = ln_alloc(nc, "m5", DT, CH, xb=hush[0:DT], xsq=hush[DT:2 * DT])
    tmoe = lnb["tmp"]
    psA = psum(nc, "psA", 128, CH)
    psB = psum(nc, "psB", 128, CH)
    psC = psum(nc, "psC", NE, CH)
    psT = psum(nc, "psT", 128, CH)

    import contextlib
    es = contextlib.ExitStack()
    block, consts, ld, wld, ctrs, sto = _f_ctx(
        F, nc, es, ["ld", "wld", "pe", "sca", "vec", "sto"], need_sel=True)
    with es:
        pe, sca, vec = ctrs["pe"], ctrs["sca"], ctrs["vec"]
        if F is not None:
            sl16 = sb(nc, "c_sel16_moe", 16, 16 * 128, F32)

            @block.sync
            def _(q):
                ld.dma(q.dma_start(out=sl16[:], in_=consts["t_sel16"].ap()))
            consts = dict(consts)
            consts["sel16"] = sl16

        @block.sync
        def _(q):
            ld.dma(q.dma_start(out=biasr[:], in_=bias_d.ap()))
            ld.dma(q.dma_start(out=ident[:], in_=id_d.ap()))

        @block.gpsimd
        def _(g):
            for i in range(DT):
                ld.dma(g.dma_start(out=rtb[i][:], in_=rt_d.ap()[i * 128:(i + 1) * 128, :]))

        @block.vector
        def _(v):
            ld.wait(v)
            v.memset(ones16[:], 1.0)
            ins = v.memset(neg1[:], -1.0)
            vec.inc(ins)

        @block.tensor
        def _(t):
            ld.wait(t)
            ins = t.matmul(psT[0:128, 0:NE], consts["ones_row_f32"][:], biasr[:], start=True, stop=True)
            pe.inc(ins)

        @block.scalar
        def _(s):
            pe.wait(s)
            ins = s.copy(biasb[:], psT[0:128, 0:NE])
            sca.inc(ins)

        for half in range(2):
            hs_tok = slice(half * HALF, (half + 1) * HALF)

            @block.gpsimd
            def _(g):
                pe.wait(g)
                for i in range(DT):
                    ld.dma(g.dma_start(out=x4bf[i][:], in_=x4_d.ap()[i * 128:(i + 1) * 128, hs_tok]))

            @block.vector
            def _(v):
                ld.wait(v)
                sto.wait(v)
                for i in range(DT):
                    ins = v.memset(racc[i][:], 0.0)
                vec.inc(ins)

            for c2 in range(HALF // CH):
                cs = slice(c2 * CH, (c2 + 1) * CH)

                @block.tensor
                def _(t):
                    ld.wait(t)
                    sca.wait(t)
                    for i in range(DT):
                        ins = t.matmul(psC[:, 0:CH], rtb[i][:], x4bf[i][:, cs],
                                       start=(i == 0), stop=(i == DT - 1))
                    pe.inc(ins)

                @block.scalar
                def _(s):
                    pe.wait(s)
                    ins = s.activation(out=scoresT[0:NE, cs], in_=psC[:, 0:CH], func=AF.Sigmoid)
                    sca.inc(ins)

            for tt in range(HALF // 128):
                @block.tensor
                def _(t):
                    sca.wait(t)
                    vec.wait(t)
                    ins = t.transpose(psT[0:128, 0:NE], scoresT[0:NE, tt * 128:(tt + 1) * 128], ident[0:NE, 0:NE])
                    pe.inc(ins)

                @block.vector
                def _(v):
                    def E(ins):
                        vec.inc(ins)
                        vec.wait(v)
                    pe.wait(v)
                    E(v.tensor_copy(stok[:], psT[0:128, 0:NE]))
                    E(v.tensor_add(sbv[:], stok[:], biasb[:]))
                    E(v.memset(rank16[:], 0.0))
                    for g in range(NG):
                        gsl = slice(g * EPG, (g + 1) * EPG)
                        for e2 in range(EPG):
                            col = sbv[:, g * EPG + e2:g * EPG + e2 + 1]
                            v.scalar_tensor_tensor(out=rank16[:, gsl], in0=sbv[:, gsl], scalar=col,
                                                   in1=rank16[:, gsl], op0=ALU.is_lt, op1=ALU.add)
                            if e2 < EPG - 1:
                                t2sl = slice(g * EPG + e2 + 1, (g + 1) * EPG)
                                v.scalar_tensor_tensor(out=rank16[:, t2sl], in0=sbv[:, t2sl], scalar=col,
                                                       in1=rank16[:, t2sl], op0=ALU.is_equal, op1=ALU.add)
                    E(v.memset(thrc[:], 2.0))
                    E(v.tensor_single_scalar(m01[:], rank16[:], thrc[:], ALU.is_lt))
                    E(v.tensor_mul(m01[:], m01[:], sbv[:]))
                    for g in range(NG):
                        E(v.tensor_reduce(gsc[:, g:g + 1], m01[:, g * EPG:(g + 1) * EPG],
                                          axis=mybir.AxisListType.X, op=ALU.add))
                    E(v.memset(grank[:], 0.0))
                    for g2 in range(NG):
                        col = gsc[:, g2:g2 + 1]
                        v.scalar_tensor_tensor(out=grank[:], in0=gsc[:], scalar=col,
                                               in1=grank[:], op0=ALU.is_lt, op1=ALU.add)
                        if g2 < NG - 1:
                            t2sl = slice(g2 + 1, NG)
                            v.scalar_tensor_tensor(out=grank[:, t2sl], in0=gsc[:, t2sl], scalar=col,
                                                   in1=grank[:, t2sl], op0=ALU.is_equal, op1=ALU.add)
                    E(v.engine_nop())
                    E(v.tensor_single_scalar(gm01[:], grank[:], thrc[:], ALU.is_lt))
                    for g in range(NG):
                        E(v.tensor_scalar_mul(emask[:, g * EPG:(g + 1) * EPG], ones16[:, 0:EPG],
                                              gm01[:, g:g + 1]))
                    E(v.tensor_scalar_add(m01[:], sbv[:], 1.0))
                    E(v.tensor_mul(m01[:], m01[:], emask[:]))
                    E(v.tensor_scalar_add(sbv[:], m01[:], -1.0))
                    E(v.memset(rank16[:], 0.0))
                    for e2 in range(NE):
                        col = sbv[:, e2:e2 + 1]
                        v.scalar_tensor_tensor(out=rank16[:], in0=sbv[:], scalar=col,
                                               in1=rank16[:], op0=ALU.is_lt, op1=ALU.add)
                        if e2 < NE - 1:
                            t2sl = slice(e2 + 1, NE)
                            v.scalar_tensor_tensor(out=rank16[:, t2sl], in0=sbv[:, t2sl], scalar=col,
                                                   in1=rank16[:, t2sl], op0=ALU.is_equal, op1=ALU.add)
                    E(v.memset(thrc[:], float(TOPK)))
                    E(v.tensor_single_scalar(m01[:], rank16[:], thrc[:], ALU.is_lt))
                    E(v.tensor_mul(m01[:], m01[:], stok[:]))
                    E(v.tensor_reduce(den[:], m01[:], axis=mybir.AxisListType.X, op=ALU.add))
                    E(v.tensor_scalar_add(den[:], den[:], 1e-20))
                    E(v.reciprocal(den[:], den[:]))
                    E(v.tensor_scalar(out=stok[:], in0=m01[:], scalar1=den[:],
                                      scalar2=MSCALE, op0=ALU.mult, op1=ALU.mult))

                @block.tensor
                def _(t):
                    vec.wait(t)
                    sca.wait(t)
                    ins = t.transpose(psT[0:NE, 0:128], stok[:, 0:NE], ident[:])
                    pe.inc(ins)

                @block.scalar
                def _(s):
                    pe.wait(s)
                    ins = s.copy(combT[0:NE, tt * 128:(tt + 1) * 128], psT[0:NE, 0:128])
                    sca.inc(ins)

            phist = [pe.n, pe.n]
            tgi = [0]
            for e in range(NE):
                for wsrc, tag in ((gw_d, "g"), (uw_d, "u"), (dw_d, "d")):
                    wbuf = wbuf2[tgi[0] % 2]

                    @block.gpsimd
                    def _(g):
                        pe.wait(g, phist[-2])
                        for i in range(DT):
                            wld.dma(g.dma_start(out=wbuf[i][:, 0:1024],
                                                in_=wsrc.ap()[e, i * 128:(i + 1) * 128, :]))

                    if tag == "g":
                        for tch in range(HALF // CH):
                            ts = slice(tch * CH, (tch + 1) * CH)
                            for fj in range(DT):
                                ps = psA if fj % 2 == 0 else psB

                                @block.tensor
                                def _(t):
                                    wld.wait(t)
                                    sca.wait_relax(t, tch * DT + fj)
                                    vec.wait(t)
                                    for i in range(DT):
                                        ins = t.matmul(ps[:], wbuf[i][:, fj * 128:(fj + 1) * 128],
                                                       x4bf[i][:, ts], start=(i == 0), stop=(i == DT - 1))
                                    pe.inc(ins)

                                @block.scalar
                                def _(s):
                                    pe.wait(s)
                                    ins = s.activation(out=hs[fj][:, ts], in_=ps[:], func=AF.Silu)
                                    sca.inc(ins)
                    elif tag == "u":
                        for tch in range(HALF // CH):
                            ts = slice(tch * CH, (tch + 1) * CH)
                            for fj in range(DT):
                                ps = psA if fj % 2 == 0 else psB

                                @block.tensor
                                def _(t):
                                    wld.wait(t)
                                    sca.wait(t)
                                    vec.wait(t)
                                    for i in range(DT):
                                        ins = t.matmul(ps[:], wbuf[i][:, fj * 128:(fj + 1) * 128],
                                                       x4bf[i][:, ts], start=(i == 0), stop=(i == DT - 1))
                                    pe.inc(ins)

                                @block.scalar
                                def _(s):
                                    pe.wait(s)
                                    ins = s.copy(ut[:], ps[:])
                                    sca.inc(ins)

                                @block.vector
                                def _(v):
                                    sca.wait(v)
                                    ins = v.tensor_mul(hs[fj][:, ts], hs[fj][:, ts], ut[:])
                                    vec.inc(ins)
                    else:
                        for tch in range(HALF // CH):
                            ts = slice(tch * CH, (tch + 1) * CH)

                            @block.tensor
                            def _(t):
                                sca.wait(t)
                                vec.wait(t)
                                ins = t.matmul(psT[:, 0:CH], consts["sel16"][0:NE, e * 128:(e + 1) * 128],
                                               combT[0:NE, ts], start=True, stop=True)
                                pe.inc(ins)

                            @block.scalar
                            def _(s):
                                pe.wait(s)
                                ins = s.copy(cb[:], psT[:, 0:CH])
                                sca.inc(ins)

                            for dj in range(DT):
                                ps = psA if dj % 2 == 0 else psB

                                @block.tensor
                                def _(t):
                                    wld.wait(t)
                                    sca.wait(t)
                                    vec.wait(t)
                                    for fj in range(DT):
                                        ins = t.matmul(ps[:], wbuf[fj][:, dj * 128:(dj + 1) * 128],
                                                       hs[fj][:, ts], start=(fj == 0), stop=(fj == DT - 1))
                                    pe.inc(ins)

                                @block.vector
                                def _(v):
                                    pe.wait(v)
                                    sca.wait(v)
                                    v.tensor_tensor(tmoe[:], ps[:], cb[:], ALU.mult)
                                    ins = v.tensor_add(racc[dj][:, ts], racc[dj][:, ts], tmoe[:])
                                    vec.inc(ins)
                    phist.append(pe.n)
                    tgi[0] += 1

            pe_sh0 = pe.n
            for phase in ("g", "u", "d"):
                wbuf = wbuf2[{"g": 0, "u": 1, "d": 0}[phase]]

                @block.gpsimd
                def _(g):
                    if phase == "g":
                        pe.wait(g, phist[-1])
                        pe.wait(g)
                    elif phase == "u":
                        pe.wait(g, pe_sh0)
                    else:
                        pe.wait(g)
                    if phase in ("g", "u"):
                        src = shg_d if phase == "g" else shu_d
                        for i in range(DT):
                            wld.dma(g.dma_start(out=wbuf[i][:], in_=src.ap()[i * 128:(i + 1) * 128, :]))
                    else:
                        for k in range(2 * DT):
                            wld.dma(g.dma_start(out=wbuf[k // 2][:, (k % 2) * 1024:(k % 2 + 1) * 1024],
                                                in_=shd_d.ap()[k * 128:(k + 1) * 128, :]))

                if phase in ("g", "u"):
                    for tch in range(HALF // CH):
                        ts = slice(tch * CH, (tch + 1) * CH)
                        for fj in range(2 * DT):
                            ps = psA if fj % 2 == 0 else psB

                            @block.tensor
                            def _(t):
                                wld.wait(t)
                                sca.wait_relax(t, tch * 2 * DT + fj)
                                vec.wait(t)
                                for i in range(DT):
                                    ins = t.matmul(ps[:], wbuf[i][:, fj * 128:(fj + 1) * 128],
                                                   x4bf[i][:, ts], start=(i == 0), stop=(i == DT - 1))
                                pe.inc(ins)

                            if phase == "g":
                                @block.scalar
                                def _(s):
                                    pe.wait(s)
                                    ins = s.activation(out=hush[fj][:, ts], in_=ps[:], func=AF.Silu)
                                    sca.inc(ins)
                            else:
                                @block.scalar
                                def _(s):
                                    pe.wait(s)
                                    ins = s.copy(ut[:], ps[:])
                                    sca.inc(ins)

                                @block.vector
                                def _(v):
                                    sca.wait(v)
                                    ins = v.tensor_mul(hush[fj][:, ts], hush[fj][:, ts], ut[:])
                                    vec.inc(ins)
                else:
                    for tch in range(HALF // CH):
                        ts = slice(tch * CH, (tch + 1) * CH)
                        for dj in range(DT):
                            ps = psA if dj % 2 == 0 else psB

                            @block.tensor
                            def _(t):
                                wld.wait(t)
                                sca.wait(t)
                                vec.wait(t)
                                for k in range(2 * DT):
                                    ins = t.matmul(ps[:], wbuf[k // 2][:, (k % 2) * 1024 + dj * 128:
                                                                       (k % 2) * 1024 + (dj + 1) * 128],
                                                   hush[k][:, ts], start=(k == 0), stop=(k == 2 * DT - 1))
                                pe.inc(ins)

                            @block.vector
                            def _(v):
                                pe.wait(v)
                                ins = v.tensor_add(racc[dj][:, ts], racc[dj][:, ts], ps[:])
                                vec.inc(ins)

            for tch in range(HALF // CH):
                ts = slice(tch * CH, (tch + 1) * CH)
                gts = slice(half * HALF + tch * CH, half * HALF + (tch + 1) * CH)

                @block.sync
                def _(q):
                    vec.wait(q)
                    for i in range(DT):
                        ld.dma(q.dma_start(out=xch[i][:], in_=x4_d.ap()[i * 128:(i + 1) * 128, gts]))

                @block.vector
                def _(v):
                    ld.wait(v)
                    sto.wait(v, max(0, sto.n - 16 * DT))
                    for i in range(DT):
                        ins = v.tensor_add(x5p[i][:], xch[i][:], racc[i][:, ts])
                    vec.inc(ins)

                ln_plain_T(nc, block, ctrs, x5p, x5p, DT, CH, consts, lnb)

                @block.sync
                def _(q):
                    vec.wait(q)
                    for i in range(DT):
                        sto.dma(q.dma_start(out=o_d.ap()[i * 128:(i + 1) * 128, gts], in_=x5p[i][:]))

        @block.sync
        def _(q):
            sto.wait(q)
    return nc


def run_moe(x4, inputs):
    nc = build_moe()
    base = {
        "routerT": np.ascontiguousarray(inputs["L1_ffn_router_w"].T),
        "bias16": np.ascontiguousarray(inputs["L1_ffn_bias"].reshape(1, NE)),
        "gate_w": np.ascontiguousarray(inputs["L1_ffn_gate_w"]),
        "up_w": np.ascontiguousarray(inputs["L1_ffn_up_w"]),
        "down_w": np.ascontiguousarray(inputs["L1_ffn_down_w"]),
        "shg": np.ascontiguousarray(inputs["L1_ffn_sh_gate_w"]),
        "shu": np.ascontiguousarray(inputs["L1_ffn_sh_up_w"]),
        "shd": np.ascontiguousarray(inputs["L1_ffn_sh_down_w"]),
        "ident128": np.eye(128, dtype=np.float32),
    }
    in_maps = []
    for c in range(NCORES):
        m = dict(base)
        m["x4T"] = _shard_tok_T(x4, c)
        in_maps.append(m)
    res = run_spmd(nc, in_maps, core_ids=list(range(NCORES)))
    out = np.empty((B, L, D), np.float32)
    for c in range(NCORES):
        b, h = c // 2, c % 2
        out[b, h * TK:(h + 1) * TK] = res.results[c]["outT"].T
    return out


def emit_scan_fused(F):
    nc, block = F["nc"], F["block"]
    ld, ctrs, sto = F["ld"], F["ctrs"], F["sto"]
    pe, sca, vec = ctrs["pe"], ctrs["sca"], ctrs["vec"]
    consts = F["consts"]
    NDT = 8
    NSC = L // SCH
    al_d = {d: nc.dram_tensor(f"al_{d}", [1024, DS], F32, kind="ExternalInput") for d in ("f", "b")}

    sl16 = sb(nc, "c_sel16_scan", 16, 16 * 128, F32)
    A = sb_tiles(nc, "sc_A", NDT, 128, DS, F32)
    hstate = sb_tiles(nc, "sc_hstate", NDT, 128, DS, F32)
    bmr = sb(nc, "sc_bmr", DS, L, F32)
    cmr = sb(nc, "sc_cmr", DS, L, F32)
    bbc = sb(nc, "sc_bbc", 128, SW, F32)
    cbc = sb(nc, "sc_cbc", 128, SW, F32)
    bstg = sb(nc, "sc_bstg", 128, SCH, F32)
    dtc = [sb(nc, f"sc_dtc{k}", 128, SCH, F32) for k in range(2)]
    xsc = [sb(nc, f"sc_xsc{k}", 128, SCH, F32) for k in range(2)]
    dtr = sb(nc, "sc_dtr", 128, SCH, F32)
    xsr = sb(nc, "sc_xsr", 128, SCH, F32)
    wt = sb(nc, "sc_wt", 128, SCH, F32)
    da2 = [sb(nc, f"sc_da{k}", 128, SW, F32) for k in range(2)]
    dbx = sb(nc, "sc_dbx", 128, SW, F32)
    hh = sb(nc, "sc_hh", 128, SW, F32)
    yt = [sb(nc, f"sc_yt{k}", 128, SCH, F32) for k in range(2)]
    ytr = sb(nc, "sc_ytr", 128, SCH, F32)
    psb = psum(nc, "sc_psb", 128, 2 * SCH)
    psc = psum(nc, "sc_psc", 128, 2 * SCH)

    @block.sync
    def _(q):
        ld.dma(q.dma_start(out=sl16[:], in_=consts["t_sel16"].ap()))

    for d in ("f", "b"):
        rev = d == "b"
        sh_dt, sh_xs = F["io"][f"sh_dt_{d}"], F["io"][f"sh_xs_{d}"]
        sh_bm, sh_cm = F["io"][f"sh_Bm_{d}"], F["io"][f"sh_Cm_{d}"]
        sh_y = F["io"][f"sh_y_{d}"]

        @block.sync
        def _(q):
            vec.wait(q)
            sca.wait(q)
            for i in range(NDT):
                ld.dma(q.dma_start(out=A[i][:],
                                   in_=al_d[d].ap()[i * 128:(i + 1) * 128, :]))
            ld.dma(q.dma_start(out=bmr[:], in_=sh_bm.ap()))
            ld.dma(q.dma_start(out=cmr[:], in_=sh_cm.ap()))

        @block.vector
        def _(v):
            ld.wait(v)
            for i in range(NDT):
                ins = v.memset(hstate[i][:], 0.0)
            ctrs["vec"].inc(ins)

        @block.scalar
        def _(s):
            ld.wait(s)
            vec.wait(s)
            for i in range(NDT):
                ins = s.activation(out=A[i][:], in_=A[i][:], func=AF.Exp)
            sca.inc(ins)

        @block.vector
        def _(v):
            sca.wait(v)
            for i in range(NDT):
                v.tensor_scalar_mul(A[i][:], A[i][:], -1.0)
            for k in range(2):
                slots = bass.AP(da2[k], 0, [list(da2[k][:].ap[0]), [SCH + 1, DS]])
                ins = v.memset(slots, 0.0)
            vec.inc(ins)
        vhist = []

        for c in range(NSC):
            rc0 = (L - (c + 1) * SCH) if rev else (c * SCH)
            tc = slice(rc0, rc0 + SCH)
            for sp in range(DS // 2):
                s0, s1 = 2 * sp, 2 * sp + 1
                reg0 = slice(s0 * (SCH + 1) + 1, s0 * (SCH + 1) + 1 + SCH)
                reg1 = slice(s1 * (SCH + 1) + 1, s1 * (SCH + 1) + 1 + SCH)
                h0, h1 = slice(0, SCH), slice(SCH, 2 * SCH)

                @block.tensor
                def _(t):
                    ld.wait(t)
                    vec.wait(t)
                    sca.wait(t)
                    sel0 = sl16[0:DS, s0 * 128:(s0 + 1) * 128]
                    sel1 = sl16[0:DS, s1 * 128:(s1 + 1) * 128]
                    t.matmul(psb[:, h0], sel0, bmr[0:DS, tc], start=True, stop=True)
                    t.matmul(psb[:, h1], sel1, bmr[0:DS, tc], start=True, stop=True)
                    t.matmul(psc[:, h0], sel0, cmr[0:DS, tc], start=True, stop=True)
                    ins = t.matmul(psc[:, h1], sel1, cmr[0:DS, tc], start=True, stop=True)
                    pe.inc(ins)

                if not rev:
                    @block.scalar
                    def _(s_):
                        pe.wait(s_)
                        s_.copy(bbc[:, reg0], psb[:, h0])
                        s_.copy(bbc[:, reg1], psb[:, h1])
                        s_.copy(cbc[:, reg0], psc[:, h0])
                        ins = s_.copy(cbc[:, reg1], psc[:, h1])
                        sca.inc(ins)
                else:
                    @block.scalar
                    def _(s_):
                        pe.wait(s_)
                        s_.copy(bstg[:], psb[:, h0])
                        s_.copy(dtr[:], psb[:, h1])
                        s_.copy(xsr[:], psc[:, h0])
                        ins = s_.copy(ytr[:], psc[:, h1])
                        sca.inc(ins)

                    @block.vector
                    def _(v):
                        sca.wait(v)
                        v.tensor_copy(bbc[:, reg0], bstg[:, ::-1])
                        v.tensor_copy(bbc[:, reg1], dtr[:, ::-1])
                        v.tensor_copy(cbc[:, reg0], xsr[:, ::-1])
                        ins = v.tensor_copy(cbc[:, reg1], ytr[:, ::-1])
                        vec.inc(ins)

            for i in range(NDT):
                buf = (c * NDT + i) % 2

                @block.gpsimd
                def _(g):
                    vec.wait_relax(g, c * NDT + i, back=(4 if rev else 1))

                    def _lr(base):
                        ro = slice(base + i * 128, base + (i + 1) * 128)
                        ld.dma(g.dma_start(out=dtc[buf][:], in_=sh_dt.ap()[ro, tc]))
                        ld.dma(g.dma_start(out=xsc[buf][:], in_=sh_xs.ap()[ro, tc]))
                    par_branch(g, lambda: _lr(0), lambda: _lr(1024), bal=(ld,))

                if rev:
                    @block.vector
                    def _(v):
                        ld.wait(v)
                        sca.wait(v)
                        v.tensor_copy(dtr[:], dtc[buf][:, ::-1])
                        ins = v.tensor_copy(xsr[:], xsc[buf][:, ::-1])
                        vec.inc(ins)
                    dts, xss = dtr, xsr
                else:
                    dts, xss = dtc[buf], xsc[buf]

                ig = c * NDT + i
                da = da2[ig % 2]

                @block.scalar
                def _(s_):
                    ld.wait(s_)
                    if rev or len(vhist) < 2:
                        vec.wait(s_)
                    else:
                        vec.wait(s_, vhist[-2])
                    for s in range(DS):
                        reg = slice(s * (SCH + 1) + 1, s * (SCH + 1) + 1 + SCH)
                        ins = s_.activation(out=da[:, reg], in_=dts[:], func=AF.Exp,
                                            scale=A[i][:, s:s + 1])
                    sca.inc(ins)

                @block.vector
                def _(v):
                    ld.wait(v)
                    sca.wait(v)
                    v.tensor_mul(wt[:], dts[:], xss[:])
                    slots = bass.AP(dbx, 0, [list(dbx[:].ap[0]), [SCH + 1, DS]])
                    v.tensor_copy(slots, hstate[i][:])
                    for s in range(DS):
                        reg = slice(s * (SCH + 1) + 1, s * (SCH + 1) + 1 + SCH)
                        v.tensor_tensor(dbx[:, reg], wt[:], bbc[:, reg], ALU.mult)
                    v.tensor_tensor_scan(out=hh[:], data0=da[:], data1=dbx[:],
                                         initial=0.0, op0=ALU.mult, op1=ALU.add)
                    laststates = bass.AP(hh, SCH, [list(hh[:].ap[0]), [SCH + 1, DS]])
                    v.tensor_copy(hstate[i][:], laststates)
                    for s in range(DS):
                        reg = slice(s * (SCH + 1) + 1, s * (SCH + 1) + 1 + SCH)
                        v.tensor_tensor(dbx[:, reg], hh[:, reg], cbc[:, reg], ALU.mult)
                    sto.wait(v, max(0, sto.n - 16))
                    cyv = bass.AP(dbx, 1, [list(dbx[:].ap[0]), [1, SCH], [SCH + 1, DS]])
                    ins = v.tensor_reduce(yt[i % 2][:], cyv, axis=mybir.AxisListType.X, op=ALU.add)
                    if rev:
                        vec.inc(ins)
                        vec.wait(v)
                        ins = v.tensor_copy(ytr[:], yt[i % 2][:, ::-1])
                        vec.inc(ins)
                        vec.wait(v)
                        ins = v.tensor_copy(yt[i % 2][:], ytr[:])
                    vec.inc(ins)

                vhist.append(vec.n)

                @block.gpsimd
                def _(g):
                    vec.wait(g)

                    def _sr(base):
                        ro = slice(base + i * 128, base + (i + 1) * 128)
                        sto.dma(g.dma_start(out=sh_y.ap()[ro, tc], in_=yt[i % 2][:]))
                    par_branch(g, lambda: _sr(0), lambda: _sr(1024), bal=(sto,))

    @block.gpsimd
    def _(g):
        sto.wait(g)


I32 = mybir.dt.int32
FUSED = True


def _scope(pfx):
    import contextlib

    class _S:
        def __enter__(self):
            self.st = contextlib.ExitStack()
            self.old = dict(_ALLOC)
            _ALLOC["stack"] = self.st
            _ALLOC["pfx"] = pfx
            self.st.__enter__()
            return self

        def __exit__(self, *a):
            r = self.st.__exit__(*a)
            _ALLOC.update(self.old)
            return r
    return _S()


def build_fused():
    nc = bass.Bass("TRN2", target_bir_lowering=False, debug=False, num_devices=8)
    consts = make_consts(nc)

    def shd(name, shape, dt):
        return nc.dram_tensor(name, shape, dt, kind="Internal", addr_space="Shared")

    def locd(name, shape, dt):
        return nc.dram_tensor(name, shape, dt, kind="Internal")
    io = {}
    for d in ("f", "b"):
        io[f"sh_xs_{d}"] = shd(f"sh_xs_{d}", [DI, L], F32)
        io[f"sh_dt_{d}"] = shd(f"sh_dt_{d}", [DI, L], F32)
        io[f"sh_Bm_{d}"] = shd(f"sh_Bm_{d}", [DS, L], F32)
        io[f"sh_Cm_{d}"] = shd(f"sh_Cm_{d}", [DS, L], F32)
        io[f"sh_y_{d}"] = shd(f"sh_y_{d}", [DI, L], F32)
        io[f"zs_{d}"] = locd(f"zs_{d}", [DI, TK], F32)
    io["x2T"] = locd("x2T", [D, TK], F32)
    io["x2c"] = shd("sh_x2c", [D, 2], F32)
    io["x3T"] = locd("x3T", [D, TK], F32)
    io["sh_x3"] = shd("sh_x3", [D, L], BF16)
    io["x4T"] = locd("x4T", [D, TK], F32)
    sh_flag = shd("sh_flag", [1, 2], I32)
    salt_d = nc.dram_tensor("salt", [1, 8], I32, kind="ExternalInput")

    saltt = sb(nc, "saltt", 1, 8, I32)
    fread = sb(nc, "fread", 1, 1, I32)

    import contextlib
    with contextlib.ExitStack() as TOP:
        hs = {n: TOP.enter_context(nc.semaphore(n)) for n in
              ("ld", "wld", "pe", "sca", "vec", "sto", "xc", "fl")}
        block = TOP.enter_context(nc.Block())
        ld, wld = Ctr(hs["ld"]), Ctr(hs["wld"])
        ctrs = {"pe": Ctr(hs["pe"]), "sca": Ctr(hs["sca"]), "vec": Ctr(hs["vec"])}
        sto = Ctr(hs["sto"])
        xcc = Ctr(hs["xc"])
        flc = Ctr(hs["fl"])
        load_consts(nc, block, consts, ld, need_sel=False)

        @block.sync
        def _(q):
            ld.dma(q.dma_start(out=saltt[:], in_=salt_d.ap()))

        F = {"nc": nc, "block": block, "consts": consts, "ld": ld, "wld": wld,
             "ctrs": ctrs, "sto": sto, "io": io}
        allc = [ld, wld, sto, xcc, flc] + list(ctrs.values())

        def barrier():
            for engname in ("sync", "gpsimd", "tensor", "scalar", "vector"):
                def go(e):
                    for ct in allc:
                        ct.wait(e)
                getattr(block, engname)(go)

        def sync_point(k):
            @block.gpsimd
            def _(g):
                for ct in [ld, wld, sto] + list(ctrs.values()):
                    ct.wait(g)
                par_branch(g,
                    lambda: flc.dma(g.dma_start(out=sh_flag.ap()[0:1, 0:1], in_=saltt[0:1, k - 1:k])),
                    lambda: flc.dma(g.dma_start(out=sh_flag.ap()[0:1, 1:2], in_=saltt[0:1, k - 1:k])),
                    bal=(flc,))
                with g.register(f"flr{k}") as reg, g.register(f"tg{k}") as tgt:
                    g.reg_load(tgt, saltt[0:1, k - 1:k])

                    def cond():
                        par_branch(g,
                            lambda: flc.dma(g.dma_start(out=fread[:], in_=sh_flag.ap()[0:1, 1:2])),
                            lambda: flc.dma(g.dma_start(out=fread[:], in_=sh_flag.ap()[0:1, 0:1])),
                            bal=(flc,))
                        g.wait_ge(hs["fl"], flc.n)
                        g.reg_load(reg, fread[0:1, 0:1])
                        return g.scalar_reg_alu(ALU.not_equal, reg, tgt)
                    with g.While(cond):
                        g.engine_nop()
                xcc.inc(g.engine_nop())
            barrier()

        with _scope("P1_"):
            build_mamba_pre(F)
        sync_point(1)
        with _scope("P2_"):
            emit_scan_fused(F)
        sync_point(2)
        with _scope("P3_"):
            build_mamba_post(F)
        sync_point(3)
        with _scope("P4_"):
            build_rwkv(F)
        sync_point(4)
        with _scope("P5_"):
            build_attn(F)
        barrier()
        with _scope("P6_"):
            build_moe(F)
    return nc


def kernel_fused(inputs):
    if "fused" not in _PROGS:
        _PROGS["fused"] = build_fused()
    nc = _PROGS["fused"]
    x = np.ascontiguousarray(inputs["L0_attn_x"], dtype=np.float32)
    base = {}
    for d, pref in (("f", "L0_attn_f_"), ("b", "L0_attn_b_")):
        base[f"inWT_{d}"] = np.ascontiguousarray(inputs[pref + "inW"].T)
        base[f"convW_{d}"] = np.ascontiguousarray(inputs[pref + "convW"].reshape(DI, 4))
        base[f"convb_{d}"] = np.ascontiguousarray(inputs[pref + "convb"].reshape(DI, 1))
        xp = inputs[pref + "xprojW"]
        base[f"xpdtT_{d}"] = np.ascontiguousarray(xp[0:RK].T)
        base[f"xpBT_{d}"] = np.ascontiguousarray(xp[RK:RK + DS].T)
        base[f"xpCT_{d}"] = np.ascontiguousarray(xp[RK + DS:RK + 2 * DS].T)
        base[f"dtWT_{d}"] = np.ascontiguousarray(inputs[pref + "dtW"].T)
        base[f"dtb_{d}"] = np.ascontiguousarray(inputs[pref + "dtb"].reshape(DI, 1))
        base[f"Dsk_{d}"] = np.ascontiguousarray(inputs[pref + "Dskip"].reshape(DI, 1))
        base[f"outWT_{d}"] = np.ascontiguousarray(inputs[pref + "outW"].T)
    base["lnw"] = np.ascontiguousarray(inputs["L0_attn_ln_w"].reshape(D, 1))
    base["lnb"] = np.ascontiguousarray(inputs["L0_attn_ln_b"].reshape(D, 1))
    pw = inputs["L0_attn_proj_W"]
    base["projT_f"] = np.ascontiguousarray(pw[:, 0:D].T)
    base["projT_b"] = np.ascontiguousarray(pw[:, D:2 * D].T)
    base["projb"] = np.ascontiguousarray(inputs["L0_attn_proj_b"].reshape(D, 1))
    base["wrT"] = np.ascontiguousarray(inputs["L0_ffn_W_r"].T)
    base["wkT"] = np.ascontiguousarray(inputs["L0_ffn_W_k"].T)
    base["wvT"] = np.ascontiguousarray(inputs["L0_ffn_W_v"].T)
    base["mur"] = np.ascontiguousarray(inputs["L0_ffn_mu_r"].reshape(D, 1))
    base["muk"] = np.ascontiguousarray(inputs["L0_ffn_mu_k"].reshape(D, 1))
    base["Wq"] = np.ascontiguousarray(inputs["L1_attn_Wq"])
    base["Wk"] = np.ascontiguousarray(inputs["L1_attn_Wk"])
    base["Wv"] = np.ascontiguousarray(inputs["L1_attn_Wv"])
    base["Wo"] = np.ascontiguousarray(inputs["L1_attn_Wo"])
    base["bo"] = np.ascontiguousarray(inputs["L1_attn_bo"].reshape(D, 1))
    base["ropeP"] = _rope_perm()
    ck, sk = _rope_tables(0, L, 1.0)
    base["cosk"], base["sink"] = ck, sk
    base["routerT"] = np.ascontiguousarray(inputs["L1_ffn_router_w"].T)
    base["bias16"] = np.ascontiguousarray(inputs["L1_ffn_bias"].reshape(1, NE))
    base["gate_w"] = np.ascontiguousarray(inputs["L1_ffn_gate_w"])
    base["up_w"] = np.ascontiguousarray(inputs["L1_ffn_up_w"])
    base["down_w"] = np.ascontiguousarray(inputs["L1_ffn_down_w"])
    base["shg"] = np.ascontiguousarray(inputs["L1_ffn_sh_gate_w"])
    base["shu"] = np.ascontiguousarray(inputs["L1_ffn_sh_up_w"])
    base["shd"] = np.ascontiguousarray(inputs["L1_ffn_sh_down_w"])
    base["ident128"] = np.eye(128, dtype=np.float32)
    salt = int(np.random.randint(1, 2**28))
    base["salt"] = (salt + np.arange(1, 9, dtype=np.int64)).astype(np.int32).reshape(1, 8)
    in_maps = []
    for c in range(NCORES):
        b, h = c // 2, c % 2
        m = dict(base)
        m["xhT"] = _shard_tok_T(x, c, halo_left=3, halo_right=3)
        m["lmask"] = np.full((128, 1), 0.0 if h == 0 else 1.0, np.float32)
        m["rmask"] = np.full((128, 1), 0.0 if h == 1 else 1.0, np.float32)
        m["pmask"] = np.full((128, 1), 0.0 if h == 0 else 1.0, np.float32)
        for d, pref in (("f", "L0_attn_f_"), ("b", "L0_attn_b_")):
            m[f"al_{d}"] = np.ascontiguousarray(inputs[pref + "A_log"][h * 1024:(h + 1) * 1024])
        m["xT"] = _shard_tok_T(x, c)
        cq, sq = _rope_tables(h * TK, TK, ASCALE)
        m["cosq"], m["sinq"] = cq, sq
        in_maps.append(m)
    res = run_spmd(nc, in_maps, core_ids=list(range(NCORES)))
    out = np.empty((B, L, D), np.float32)
    for c in range(NCORES):
        b, h = c // 2, c % 2
        out[b, h * TK:(h + 1) * TK] = res.results[c]["outT"].T
    return out


# version 08
# speedup vs baseline: 1.6335x; 1.6335x over previous
import numpy as np
import concourse.bass as bass
import concourse.mybir as mybir
from concourse.bass_utils import run_bass_kernel_spmd

F32 = mybir.dt.float32
BF16 = mybir.dt.bfloat16
AF = mybir.ActivationFunctionType
ALU = mybir.AluOpType

NCORES = 8
B, L, D = 4, 4096, 1024
TK = 2048
DT = D // 128
DI = 2048
DS = 16
RK = 64
CH = 512
NCH = TK // CH
EPS = 1e-5


def run_spmd(nc, in_maps, core_ids):
    import concourse.mybir as _mb
    want = {}
    for alloc in nc.m.functions[0].allocations:
        if isinstance(alloc, _mb.MemoryLocationSet) and alloc.kind == "ExternalInput":
            name = alloc.memorylocations[0].name
            if nc.partition_id_tensor is not None and name == nc.partition_id_tensor.name:
                continue
            want[name] = (tuple(alloc.tensor_shape), _mb.dt.np(alloc.dtype))
    for im in in_maps:
        for name, (shp, dty) in want.items():
            assert name in im, f"missing input {name}"
            a = im[name]
            assert tuple(a.shape) == shp, f"{name}: got {a.shape}, want {shp}"
            assert a.dtype == dty, f"{name}: got {a.dtype}, want {dty}"
        extra = set(im) - set(want)
        assert not extra, f"extra inputs: {extra}"
    return run_bass_kernel_spmd(nc, in_maps, core_ids=core_ids)


class Ctr:

    def __init__(self, handle):
        self.h = handle
        self.n = 0

    def inc(self, inst, k=1):
        inst.then_inc(self.h, k)
        self.n += k
        return self.n

    def dma(self, inst):
        return self.inc(inst, 16)

    def wait(self, eng, val=None):
        eng.wait_ge(self.h, self.n if val is None else val)

    def wait_relax(self, eng, j, back=1, first=2):
        eng.wait_ge(self.h, self.n if j < first else max(0, self.n - back))


_ALLOC = {"stack": None, "pfx": ""}


def sb(nc, name, p, f, dtype):
    name = _ALLOC["pfx"] + name
    if _ALLOC["stack"] is None:
        return nc.alloc_sbuf_tensor(name, [p, f], dtype)
    return _ALLOC["stack"].enter_context(nc.sbuf_tensor(name, [p, f], dtype))


def sb_tiles(nc, name, ntiles, p, f, dtype):
    return [sb(nc, f"{name}{i}", p, f, dtype) for i in range(ntiles)]


def psum(nc, name, p, f, dtype=F32):
    name = _ALLOC["pfx"] + name
    if _ALLOC["stack"] is None:
        return nc.alloc_psum_tensor(name, [p, f], dtype)
    return _ALLOC["stack"].enter_context(nc.psum_tensor(name, [p, f], dtype))


def ln_alloc(nc, tag, nt, Tloc, xb=None, xsq=None):
    return {
        "xb": xb if xb is not None else sb_tiles(nc, f"{tag}_lnxb", nt, 128, CH, BF16),
        "xsq": xsq if xsq is not None else sb_tiles(nc, f"{tag}_lnxsq", nt, 128, CH, BF16),
        "row_mu": sb(nc, f"{tag}_row_mu", 1, Tloc, F32),
        "row_ms": sb(nc, f"{tag}_row_ms", 1, Tloc, F32),
        "row_rs": sb(nc, f"{tag}_row_rs", 1, Tloc, F32),
        "row_d": sb(nc, f"{tag}_row_d", 1, Tloc, F32),
        "tmp": sb(nc, f"{tag}_lntmp", 128, CH, F32),
        "ps_sum": psum(nc, f"{tag}_ps_sum", 1, CH),
        "ps_sq": psum(nc, f"{tag}_ps_sq", 1, CH),
        "ps_mu": psum(nc, f"{tag}_ps_mu", 128, CH),
        "ps_rs": psum(nc, f"{tag}_ps_rs", 128, CH),
    }


def ln_plain_T(nc, block, ctrs, x_tiles, out_tiles, nt, Tloc, consts, bufs,
               w_cols=None, b_cols=None, out_dtype_note=None):
    t_ctr, s_ctr, v_ctr = ctrs["pe"], ctrs["sca"], ctrs["vec"]
    xb, xsq = bufs["xb"], bufs["xsq"]
    row_mu, row_ms, row_rs = bufs["row_mu"], bufs["row_ms"], bufs["row_rs"]
    ps_sum, ps_sq = bufs["ps_sum"], bufs["ps_sq"]
    ps_mu, ps_rs = bufs["ps_mu"], bufs["ps_rs"]
    c_starts = [i * CH for i in range(Tloc // CH)]
    if Tloc % CH:
        c_starts.append(Tloc - CH)

    for c0 in c_starts:
        w = CH
        cs = slice(c0, c0 + w)

        @block.scalar
        def _(s):
            v_ctr.wait(s)
            t_ctr.wait(s)
            for i in range(nt):
                s.copy(xb[i][:, 0:w], x_tiles[i][:, cs])
                ins = s.activation(out=xsq[i][:, 0:w], in_=x_tiles[i][:, cs], func=AF.Square)
            s_ctr.inc(ins)

        @block.tensor
        def _(t):
            s_ctr.wait(t)
            for i in range(nt):
                t.matmul(ps_sum[0:1, 0:w], consts["ones_col_bf"][:], xb[i][:, 0:w],
                         start=(i == 0), stop=(i == nt - 1))
            for i in range(nt):
                ins = t.matmul(ps_sq[0:1, 0:w], consts["ones_col_bf"][:], xsq[i][:, 0:w],
                               start=(i == 0), stop=(i == nt - 1))
            t_ctr.inc(ins)

        @block.scalar
        def _(s):
            t_ctr.wait(s)
            s.activation(out=row_mu[0:1, cs], in_=ps_sum[0:1, 0:w], func=AF.Copy,
                         scale=1.0 / (nt * 128.0))
            ins = s.activation(out=row_ms[0:1, cs], in_=ps_sq[0:1, 0:w], func=AF.Copy,
                               scale=1.0 / (nt * 128.0))
            s_ctr.inc(ins)

    @block.vector
    def _(v):
        s_ctr.wait(v)
        v.tensor_tensor(bufs["row_d"][:], row_mu[:], row_mu[:], ALU.mult)
        ins = v.tensor_sub(bufs["row_d"][:], row_ms[:], bufs["row_d"][:])
        v_ctr.inc(ins)

    @block.scalar
    def _(s):
        v_ctr.wait(s)
        ins = s.activation(out=row_rs[:], in_=bufs["row_d"][:], func=AF.Ln, bias=consts["eps11"][:])
        s_ctr.inc(ins)
        s_ctr.wait(s)
        ins = s.activation(out=row_rs[:], in_=row_rs[:], func=AF.Exp, scale=-0.5)
        s_ctr.inc(ins)

    for c0 in c_starts:
        w = CH
        cs = slice(c0, c0 + w)

        @block.tensor
        def _(t):
            s_ctr.wait(t)
            v_ctr.wait(t)
            t.matmul(ps_mu[:, 0:w], consts["ones_row_f32"][:], row_mu[0:1, cs], start=True, stop=True)
            ins = t.matmul(ps_rs[:, 0:w], consts["ones_row_f32"][:], row_rs[0:1, cs], start=True, stop=True)
            t_ctr.inc(ins)

        @block.vector
        def _(v):
            t_ctr.wait(v)
            for i in range(nt):
                v.tensor_sub(bufs["tmp"][:, 0:w], x_tiles[i][:, cs], ps_mu[:, 0:w])
                if w_cols is None:
                    ins = v.tensor_mul(out_tiles[i][:, cs], bufs["tmp"][:, 0:w], ps_rs[:, 0:w])
                else:
                    v.tensor_mul(bufs["tmp"][:, 0:w], bufs["tmp"][:, 0:w], ps_rs[:, 0:w])
                    ins = v.tensor_scalar(out=out_tiles[i][:, cs], in0=bufs["tmp"][:, 0:w],
                                          scalar1=w_cols[i], scalar2=b_cols[i],
                                          op0=ALU.mult, op1=ALU.add)
            v_ctr.inc(ins)


def make_consts(nc):
    import ml_dtypes
    ones_col_bf = nc.inline_tensor(np.ones((128, 1), dtype=ml_dtypes.bfloat16), "ones_col_bf")
    ones_row_f32 = nc.inline_tensor(np.ones((1, 128), dtype=np.float32), "ones_row_f32")
    eps11 = nc.inline_tensor(np.full((1, 1), EPS, dtype=np.float32), "eps11")
    ones_col_f32 = nc.inline_tensor(np.ones((128, 1), dtype=np.float32), "ones_col_f32")
    sel = np.zeros((16, 16 * 128), dtype=np.float32)
    for s_ in range(16):
        sel[s_, s_ * 128:(s_ + 1) * 128] = 1.0
    sel16 = nc.inline_tensor(sel, "sel16")
    return {"t_ones_col_bf": ones_col_bf, "t_ones_row_f32": ones_row_f32, "t_eps11": eps11,
            "t_ones_col_f32": ones_col_f32, "t_sel16": sel16}


def load_consts(nc, block, consts, ld, need_sel=True):
    cb = sb(nc, "c_ones_col_bf", 128, 1, BF16)
    rf = sb(nc, "c_ones_row_f32", 1, 128, F32)
    ep = sb(nc, "c_eps11", 1, 1, F32)
    cf = sb(nc, "c_ones_col_f32", 128, 1, F32)
    sl = sb(nc, "c_sel16", 16, 16 * 128, F32) if need_sel else None

    @block.sync
    def _(q):
        ld.dma(q.dma_start(out=cb[:], in_=consts["t_ones_col_bf"].ap()))
        ld.dma(q.dma_start(out=rf[:], in_=consts["t_ones_row_f32"].ap()))
        ld.dma(q.dma_start(out=ep[:], in_=consts["t_eps11"].ap()))
        ld.dma(q.dma_start(out=cf[:], in_=consts["t_ones_col_f32"].ap()))
        if need_sel:
            ld.dma(q.dma_start(out=sl[:], in_=consts["t_sel16"].ap()))
    consts["ones_col_bf"] = cb
    consts["ones_row_f32"] = rf
    consts["eps11"] = ep
    consts["ones_col_f32"] = cf
    consts["sel16"] = sl


def build_rwkv(F=None):
    standalone = F is None
    nc = bass.Bass("TRN2", target_bir_lowering=False, debug=False, num_devices=8) if standalone else F["nc"]
    w_d = {w: nc.dram_tensor(w, [D, D], F32, kind="ExternalInput") for w in ("wrT", "wkT", "wvT")}
    mur_d = nc.dram_tensor("mur", [D, 1], F32, kind="ExternalInput")
    muk_d = nc.dram_tensor("muk", [D, 1], F32, kind="ExternalInput")
    if standalone:
        xh_d = nc.dram_tensor("xh", [D, TK + 1], F32, kind="ExternalInput")
        o_d = nc.dram_tensor("x3T", [D, TK], F32, kind="ExternalOutput")
    else:
        o_d = F["io"]["x3T"]
        pm_d = nc.dram_tensor("pmask", [128, 1], F32, kind="ExternalInput")

    xh = sb_tiles(nc, "xh", DT, 128, TK + 1, F32)
    wr = sb_tiles(nc, "wr", DT, 128, D, BF16)
    wk = sb_tiles(nc, "wk", DT, 128, D, BF16)
    wv = sb_tiles(nc, "wv", DT, 128, D, BF16)
    mur = sb_tiles(nc, "mur", DT, 128, 1, F32)
    muk = sb_tiles(nc, "muk", DT, 128, 1, F32)
    mur1 = sb_tiles(nc, "mur1", DT, 128, 1, F32)
    muk1 = sb_tiles(nc, "muk1", DT, 128, 1, F32)
    xr = sb_tiles(nc, "xr", DT, 128, CH, BF16)
    xk = sb_tiles(nc, "xk", DT, 128, CH, BF16)
    rg = sb_tiles(nc, "rg", DT, 128, CH, F32)
    kact = sb_tiles(nc, "kact", DT, 128, CH, BF16)
    x3p = sb_tiles(nc, "x3p", DT, 128, CH, F32)
    lnb = ln_alloc(nc, "r", DT, CH)
    ktmp = sb(nc, "ktmp", 128, CH, F32)

    psA = psum(nc, "psA", 128, CH)
    psB = psum(nc, "psB", 128, CH)

    import contextlib
    es = contextlib.ExitStack()
    block, consts, ld, wld, ctrs, sto = _f_ctx(
        F, nc, es, ["ld", "pe", "sca", "vec", "sto"], need_sel=False)
    with es:
        if standalone:
            @block.sync
            def _(q):
                for i in range(DT):
                    ld.dma(q.dma_start(out=xh[i][:], in_=xh_d.ap()[i * 128:(i + 1) * 128, :]))
        else:
            pmc = sb(nc, "pmc", 128, 1, F32)

            @block.sync
            def _(q):
                ld.dma(q.dma_start(out=pmc[:], in_=pm_d.ap()))
                for i in range(DT):
                    r = slice(i * 128, (i + 1) * 128)
                    ld.dma(q.dma_start(out=xh[i][:, 1:TK + 1], in_=F["io"]["x2T"].ap()[r, :]))

                def _hl(sl):
                    for i in range(DT):
                        r = slice(i * 128, (i + 1) * 128)
                        ld.dma(q.dma_start(out=xh[i][:, 0:1], in_=F["io"]["x2c"].ap()[r, sl],
                                           allow_slow_non_contiguous=True))
                par_branch(q, lambda: _hl(slice(1, 2)), lambda: _hl(slice(0, 1)), bal=(ld,))

            @block.vector
            def _(v):
                ld.wait(v)
                for i in range(DT):
                    ins = v.tensor_scalar_mul(xh[i][:, 0:1], xh[i][:, 0:1], pmc[:])
                ctrs["vec"].inc(ins)

        @block.sync
        def _(q):
            for i in range(DT):
                ld.dma(q.dma_start(out=mur[i][:], in_=mur_d.ap()[i * 128:(i + 1) * 128, :]))
                ld.dma(q.dma_start(out=muk[i][:], in_=muk_d.ap()[i * 128:(i + 1) * 128, :]))

        @block.gpsimd
        def _(g):
            for i in range(DT):
                ld.dma(g.dma_start(out=wr[i][:], in_=w_d["wrT"].ap()[i * 128:(i + 1) * 128, :]))
                ld.dma(g.dma_start(out=wk[i][:], in_=w_d["wkT"].ap()[i * 128:(i + 1) * 128, :]))
                ld.dma(g.dma_start(out=wv[i][:], in_=w_d["wvT"].ap()[i * 128:(i + 1) * 128, :]))

        @block.vector
        def _(v):
            ld.wait(v)
            for i in range(DT):
                v.tensor_scalar(out=mur1[i][:], in0=mur[i][:], scalar1=-1.0, scalar2=1.0,
                                op0=ALU.mult, op1=ALU.add)
                ins = v.tensor_scalar(out=muk1[i][:], in0=muk[i][:], scalar1=-1.0, scalar2=1.0,
                                      op0=ALU.mult, op1=ALU.add)
            ctrs["vec"].inc(ins)

        for c in range(NCH):
            cur = slice(1 + c * CH, 1 + (c + 1) * CH)
            prv = slice(c * CH, (c + 1) * CH)

            @block.vector
            def _(v):
                ctrs["pe"].wait(v)
                for i in range(DT):
                    v.tensor_scalar_mul(lnb["tmp"][:, 0:CH], xh[i][:, cur], mur[i][:])
                    v.scalar_tensor_tensor(out=xr[i][:], in0=xh[i][:, prv], scalar=mur1[i][:],
                                           in1=lnb["tmp"][:, 0:CH], op0=ALU.mult, op1=ALU.add)
                    v.tensor_scalar_mul(lnb["tmp"][:, 0:CH], xh[i][:, cur], muk[i][:])
                    ins = v.scalar_tensor_tensor(out=xk[i][:], in0=xh[i][:, prv], scalar=muk1[i][:],
                                                 in1=lnb["tmp"][:, 0:CH], op0=ALU.mult, op1=ALU.add)
                ctrs["vec"].inc(ins)

            for j in range(DT):
                ps = psA if j % 2 == 0 else psB
                jc = slice(j * 128, (j + 1) * 128)

                @block.tensor
                def _(t):
                    ctrs["vec"].wait(t)
                    ctrs["sca"].wait_relax(t, j)
                    for i in range(DT):
                        ins = t.matmul(ps[:], wr[i][:, jc], xr[i][:], start=(i == 0), stop=(i == DT - 1))
                    ctrs["pe"].inc(ins)

                @block.scalar
                def _(s):
                    ctrs["pe"].wait(s)
                    ins = s.activation(out=rg[j][:], in_=ps[:], func=AF.Sigmoid)
                    ctrs["sca"].inc(ins)
            for j in range(DT):
                ps = psA if j % 2 == 0 else psB
                jc = slice(j * 128, (j + 1) * 128)

                @block.tensor
                def _(t):
                    ctrs["sca"].wait_relax(t, j, back=2)
                    for i in range(DT):
                        ins = t.matmul(ps[:], wk[i][:, jc], xk[i][:], start=(i == 0), stop=(i == DT - 1))
                    ctrs["pe"].inc(ins)

                @block.scalar
                def _(s):
                    ctrs["pe"].wait(s)
                    ins = s.activation(out=ktmp[:], in_=ps[:], func=AF.Relu)
                    ctrs["sca"].inc(ins)
                    ctrs["sca"].wait(s)
                    ins = s.activation(out=kact[j][:], in_=ktmp[:], func=AF.Square)
                    ctrs["sca"].inc(ins)
            for j in range(DT):
                ps = psA if j % 2 == 0 else psB
                jc = slice(j * 128, (j + 1) * 128)

                @block.tensor
                def _(t):
                    ctrs["sca"].wait(t)
                    ctrs["vec"].wait_relax(t, j)
                    for i in range(DT):
                        ins = t.matmul(ps[:], wv[i][:, jc], kact[i][:], start=(i == 0), stop=(i == DT - 1))
                    ctrs["pe"].inc(ins)

                @block.vector
                def _(v):
                    ctrs["pe"].wait(v)
                    sto.wait(v)
                    v.tensor_mul(lnb["tmp"][:, 0:CH], rg[j][:], ps[:])
                    ins = v.tensor_add(x3p[j][:], xh[j][:, cur], lnb["tmp"][:, 0:CH])
                    ctrs["vec"].inc(ins)

            ln_plain_T(nc, block, ctrs, x3p, x3p, DT, CH, consts, lnb)

            @block.sync
            def _(q):
                ctrs["vec"].wait(q)
                for i in range(DT):
                    sto.dma(q.dma_start(out=o_d.ap()[i * 128:(i + 1) * 128, c * CH:(c + 1) * CH],
                                        in_=x3p[i][:]))

            if not standalone:
                @block.gpsimd
                def _(g):
                    ctrs["vec"].wait(g)

                    def _sx(off):
                        for i in range(DT):
                            sto.dma(g.dma_start(
                                out=F["io"]["sh_x3"].ap()[i * 128:(i + 1) * 128,
                                                          off + c * CH:off + (c + 1) * CH],
                                in_=x3p[i][:]))
                    par_branch(g, lambda: _sx(0), lambda: _sx(TK), bal=(sto,))

        @block.sync
        def _(q):
            sto.wait(q)
    return nc


def _shard_tok_T(a_bld, core, halo_left=0, halo_right=0):
    b, h = core // 2, core % 2
    s = h * TK
    aT = np.ascontiguousarray(np.transpose(a_bld[b], (1, 0)))
    if halo_left == 0 and halo_right == 0:
        return np.ascontiguousarray(aT[:, s:s + TK])
    out = np.zeros((aT.shape[0], halo_left + TK + halo_right), np.float32)
    lo = max(0, s - halo_left)
    hi = min(L, s + TK + halo_right)
    out[:, halo_left - (s - lo):halo_left + (hi - s)] = aT[:, lo:hi]
    return out


def run_rwkv(x2, inputs):
    nc = build_rwkv()
    wrT = np.ascontiguousarray(inputs["L0_ffn_W_r"].T)
    wkT = np.ascontiguousarray(inputs["L0_ffn_W_k"].T)
    wvT = np.ascontiguousarray(inputs["L0_ffn_W_v"].T)
    mur = np.ascontiguousarray(inputs["L0_ffn_mu_r"].reshape(D, 1))
    muk = np.ascontiguousarray(inputs["L0_ffn_mu_k"].reshape(D, 1))
    in_maps = []
    for c in range(NCORES):
        in_maps.append({
            "xh": _shard_tok_T(x2, c, halo_left=1),
            "wrT": wrT, "wkT": wkT, "wvT": wvT, "mur": mur, "muk": muk,
        })
    res = run_spmd(nc, in_maps, core_ids=list(range(NCORES)))
    x3 = np.empty((B, L, D), np.float32)
    for c in range(NCORES):
        b, h = c // 2, c % 2
        x3[b, h * TK:(h + 1) * TK] = res.results[c]["x3T"].T
    return x3


_PROGS = {}
_PAR_REGS = {}


def _parity_reg(e):
    key = (id(e.bass), str(e.engine))
    if key not in _PAR_REGS:
        pid = e.partition_id()
        preg = e.alloc_register("parreg")
        e.reg_alu(preg, e.to_reg(pid), 2, ALU.mod)
        _PAR_REGS[key] = preg
    return _PAR_REGS[key]


def par_branch(e, emit0, emit1, bal=()):
    preg = _parity_reg(e)
    marks = [c.n for c in bal]
    with e.If_eq(preg, 0):
        emit0()
    after = [c.n for c in bal]
    for c, m in zip(bal, marks):
        c.n = m
    with e.Else():
        emit1()
    for c, a in zip(bal, after):
        assert c.n == a, "unbalanced branch sem counts"


def _f_ctx(F, nc, es, semnames, need_sel, selname=None):
    if F is None:
        consts = make_consts(nc)
        hs = {n: es.enter_context(nc.semaphore(n)) for n in semnames}
        block = es.enter_context(nc.Block())
        ld = Ctr(hs["ld"])
        wld = Ctr(hs["wld"]) if "wld" in hs else None
        ctrs = {"pe": Ctr(hs["pe"]), "sca": Ctr(hs["sca"]), "vec": Ctr(hs["vec"])}
        sto = Ctr(hs["sto"])
        load_consts(nc, block, consts, ld, need_sel=need_sel)
        return block, consts, ld, wld, ctrs, sto
    return F["block"], F["consts"], F["ld"], F["wld"], F["ctrs"], F["sto"]


def kernel(**inputs):
    if FUSED:
        return kernel_fused(inputs)
    x = np.ascontiguousarray(inputs["L0_attn_x"], dtype=np.float32)
    pre = run_mamba_pre(x, inputs)
    ys = run_scan(pre, inputs)
    x2 = run_mamba_post(x, pre, ys, inputs)
    x3 = run_rwkv(x2, inputs)
    x4 = run_attn(x3, inputs)
    out = run_moe(x4, inputs)
    return out


def ln_stream_T(nc, block, ctrs, ld, x_dram, nt, Tloc, out_tiles, consts, bufs,
                xch, w_cols=None, b_cols=None):
    t_ctr, s_ctr, v_ctr = ctrs["pe"], ctrs["sca"], ctrs["vec"]
    xb, xsq = bufs["xb"], bufs["xsq"]
    row_mu, row_ms, row_rs = bufs["row_mu"], bufs["row_ms"], bufs["row_rs"]
    ps_sum, ps_sq = bufs["ps_sum"], bufs["ps_sq"]
    ps_mu, ps_rs = bufs["ps_mu"], bufs["ps_rs"]
    c_starts = [i * CH for i in range(Tloc // CH)]
    if Tloc % CH:
        c_starts.append(Tloc - CH)

    for c0 in c_starts:
        w = CH
        cs = slice(c0, c0 + w)

        @block.sync
        def _(q):
            s_ctr.wait(q)
            for i in range(nt):
                ld.dma(q.dma_start(out=xch[i][:, 0:w], in_=x_dram.ap()[i * 128:(i + 1) * 128, cs]))

        @block.scalar
        def _(s):
            ld.wait(s)
            t_ctr.wait(s)
            for i in range(nt):
                s.copy(xb[i][:, 0:w], xch[i][:, 0:w])
                ins = s.activation(out=xsq[i][:, 0:w], in_=xch[i][:, 0:w], func=AF.Square)
            s_ctr.inc(ins)

        @block.tensor
        def _(t):
            s_ctr.wait(t)
            for i in range(nt):
                t.matmul(ps_sum[0:1, 0:w], consts["ones_col_bf"][:], xb[i][:, 0:w],
                         start=(i == 0), stop=(i == nt - 1))
            for i in range(nt):
                ins = t.matmul(ps_sq[0:1, 0:w], consts["ones_col_bf"][:], xsq[i][:, 0:w],
                               start=(i == 0), stop=(i == nt - 1))
            t_ctr.inc(ins)

        @block.scalar
        def _(s):
            t_ctr.wait(s)
            s.activation(out=row_mu[0:1, cs], in_=ps_sum[0:1, 0:w], func=AF.Copy,
                         scale=1.0 / (nt * 128.0))
            ins = s.activation(out=row_ms[0:1, cs], in_=ps_sq[0:1, 0:w], func=AF.Copy,
                               scale=1.0 / (nt * 128.0))
            s_ctr.inc(ins)

    @block.vector
    def _(v):
        s_ctr.wait(v)
        v.tensor_tensor(bufs["row_d"][:], row_mu[:], row_mu[:], ALU.mult)
        ins = v.tensor_sub(bufs["row_d"][:], row_ms[:], bufs["row_d"][:])
        v_ctr.inc(ins)

    @block.scalar
    def _(s):
        v_ctr.wait(s)
        ins = s.activation(out=row_rs[:], in_=bufs["row_d"][:], func=AF.Ln, bias=consts["eps11"][:])
        s_ctr.inc(ins)
        s_ctr.wait(s)
        ins = s.activation(out=row_rs[:], in_=row_rs[:], func=AF.Exp, scale=-0.5)
        s_ctr.inc(ins)

    for c0 in c_starts:
        w = CH
        cs = slice(c0, c0 + w)

        @block.sync
        def _(q):
            v_ctr.wait(q)
            for i in range(nt):
                ld.dma(q.dma_start(out=xch[i][:, 0:w], in_=x_dram.ap()[i * 128:(i + 1) * 128, cs]))

        @block.tensor
        def _(t):
            s_ctr.wait(t)
            v_ctr.wait(t)
            t.matmul(ps_mu[:, 0:w], consts["ones_row_f32"][:], row_mu[0:1, cs], start=True, stop=True)
            ins = t.matmul(ps_rs[:, 0:w], consts["ones_row_f32"][:], row_rs[0:1, cs], start=True, stop=True)
            t_ctr.inc(ins)

        @block.vector
        def _(v):
            t_ctr.wait(v)
            ld.wait(v)
            for i in range(nt):
                v.tensor_sub(bufs["tmp"][:, 0:w], xch[i][:, 0:w], ps_mu[:, 0:w])
                if w_cols is None:
                    ins = v.tensor_mul(out_tiles[i][:, cs], bufs["tmp"][:, 0:w], ps_rs[:, 0:w])
                else:
                    v.tensor_mul(bufs["tmp"][:, 0:w], bufs["tmp"][:, 0:w], ps_rs[:, 0:w])
                    ins = v.tensor_scalar(out=out_tiles[i][:, cs], in0=bufs["tmp"][:, 0:w],
                                          scalar1=w_cols[i], scalar2=b_cols[i],
                                          op0=ALU.mult, op1=ALU.add)
            v_ctr.inc(ins)


def build_mamba_pre(F=None):
    standalone = F is None
    nc = bass.Bass("TRN2", target_bir_lowering=False, debug=False, num_devices=8) if standalone else F["nc"]
    TE = TK + 6
    DIT = DI // 128
    xh_d = nc.dram_tensor("xhT", [D, TE], F32, kind="ExternalInput")
    lnw_d = nc.dram_tensor("lnw", [D, 1], F32, kind="ExternalInput")
    lnbb_d = nc.dram_tensor("lnb", [D, 1], F32, kind="ExternalInput")
    lm_d = nc.dram_tensor("lmask", [128, 1], F32, kind="ExternalInput")
    rm_d = nc.dram_tensor("rmask", [128, 1], F32, kind="ExternalInput")
    dirs = ("f", "b")
    din = {}
    dout = {}
    for d in dirs:
        din[d] = {
            "inWT": nc.dram_tensor(f"inWT_{d}", [D, 2 * DI], F32, kind="ExternalInput"),
            "convW": nc.dram_tensor(f"convW_{d}", [DI, 4], F32, kind="ExternalInput"),
            "convb": nc.dram_tensor(f"convb_{d}", [DI, 1], F32, kind="ExternalInput"),
            "xpdtT": nc.dram_tensor(f"xpdtT_{d}", [DI, RK], F32, kind="ExternalInput"),
            "xpBT": nc.dram_tensor(f"xpBT_{d}", [DI, DS], F32, kind="ExternalInput"),
            "xpCT": nc.dram_tensor(f"xpCT_{d}", [DI, DS], F32, kind="ExternalInput"),
            "dtWT": nc.dram_tensor(f"dtWT_{d}", [RK, DI], F32, kind="ExternalInput"),
            "dtb": nc.dram_tensor(f"dtb_{d}", [DI, 1], F32, kind="ExternalInput"),
        }
        if standalone:
            dout[d] = {
                "xs": nc.dram_tensor(f"xs_{d}", [DI, TK], F32, kind="ExternalOutput"),
                "zs": nc.dram_tensor(f"zs_{d}", [DI, TK], F32, kind="ExternalOutput"),
                "dt": nc.dram_tensor(f"dt_{d}", [DI, TK], F32, kind="ExternalOutput"),
                "Bm": nc.dram_tensor(f"Bm_{d}", [DS, TK], F32, kind="ExternalOutput"),
                "Cm": nc.dram_tensor(f"Cm_{d}", [DS, TK], F32, kind="ExternalOutput"),
            }
        else:
            dout[d] = {
                "xs": F["io"][f"sh_xs_{d}"], "dt": F["io"][f"sh_dt_{d}"],
                "Bm": F["io"][f"sh_Bm_{d}"], "Cm": F["io"][f"sh_Cm_{d}"],
                "zs": F["io"][f"zs_{d}"],
            }

    normed = sb_tiles(nc, "normed", DT, 128, TE, BF16)
    xch = sb_tiles(nc, "xch", DT, 128, CH, F32)
    wz = sb_tiles(nc, "wz", DT, 128, DI, BF16)
    xspre = sb_tiles(nc, "xspre", DIT, 128, CH + 6, BF16)
    xs_ch = sb_tiles(nc, "xs_ch", DIT, 128, CH, BF16)
    lncols = sb_tiles(nc, "lncol", DT, 128, 1, F32)
    lnbcols = sb_tiles(nc, "lnbcol", DT, 128, 1, F32)
    lmask = sb(nc, "lmask_s", 128, 1, F32)
    rmask = sb(nc, "rmask_s", 128, 1, F32)
    convw = {d: sb_tiles(nc, f"convw_{d}", DIT, 128, 4, F32) for d in dirs}
    convb = {d: sb_tiles(nc, f"convb_{d}", DIT, 128, 1, F32) for d in dirs}
    dtb = {d: sb_tiles(nc, f"dtb_{d}", DIT, 128, 1, F32) for d in dirs}
    xpdt = {d: sb_tiles(nc, f"xpdt_{d}", DIT, 128, RK, BF16) for d in dirs}
    xpB = {d: sb_tiles(nc, f"xpB_{d}", DIT, 128, DS, BF16) for d in dirs}
    xpC = {d: sb_tiles(nc, f"xpC_{d}", DIT, 128, DS, BF16) for d in dirs}
    dtw = {d: sb(nc, f"dtw_{d}", RK, DI, BF16) for d in dirs}
    zstage = [sb(nc, f"zstage{k}", 128, CH, F32) for k in range(2)]
    xsstage = [sb(nc, f"xsstage{k}", 128, CH, F32) for k in range(2)]
    acc = [sb(nc, f"acc{k}", 128, CH, F32) for k in range(2)]
    dtraw = sb(nc, "dtraw", RK, CH, BF16)
    bstage = sb(nc, "bstage", DS, CH, F32)
    cstage = sb(nc, "cstage", DS, CH, F32)
    sp1 = sb(nc, "sp1", 128, CH, F32)
    sp2 = sb(nc, "sp2", 128, CH, F32)
    sp3 = sb(nc, "sp3", 128, CH, F32)
    lnbufs = ln_alloc(nc, "m", DT, TE)

    psA = psum(nc, "psA", 128, CH)
    psB = psum(nc, "psB", 128, CH)
    psH = psum(nc, "psH", 128, 8)
    psDT = psum(nc, "psDT", RK, CH)

    import contextlib
    es = contextlib.ExitStack()
    block, consts, ld, wld, ctrs, sto = _f_ctx(
        F, nc, es, ["ld", "wld", "pe", "sca", "vec", "sto"], need_sel=False)
    with es:

        @block.sync
        def _(q):
            for i in range(DT):
                ld.dma(q.dma_start(out=lncols[i][:], in_=lnw_d.ap()[i * 128:(i + 1) * 128, :]))
                ld.dma(q.dma_start(out=lnbcols[i][:], in_=lnbb_d.ap()[i * 128:(i + 1) * 128, :]))
            ld.dma(q.dma_start(out=lmask[:], in_=lm_d.ap()))
            ld.dma(q.dma_start(out=rmask[:], in_=rm_d.ap()))

        @block.gpsimd
        def _(g):
            for d in dirs:
                for i in range(DIT):
                    r = slice(i * 128, (i + 1) * 128)
                    ld.dma(g.dma_start(out=convw[d][i][:], in_=din[d]["convW"].ap()[r, :]))
                    ld.dma(g.dma_start(out=convb[d][i][:], in_=din[d]["convb"].ap()[r, :]))
                    ld.dma(g.dma_start(out=dtb[d][i][:], in_=din[d]["dtb"].ap()[r, :]))
                    ld.dma(g.dma_start(out=xpdt[d][i][:], in_=din[d]["xpdtT"].ap()[r, :]))
                    ld.dma(g.dma_start(out=xpB[d][i][:], in_=din[d]["xpBT"].ap()[r, :]))
                    ld.dma(g.dma_start(out=xpC[d][i][:], in_=din[d]["xpCT"].ap()[r, :]))
                ld.dma(g.dma_start(out=dtw[d][:], in_=din[d]["dtWT"].ap()))

        ln_stream_T(nc, block, ctrs, ld, xh_d, DT, TE, normed, consts, lnbufs, xch,
                    w_cols=[lncols[i][:] for i in range(DT)],
                    b_cols=[lnbcols[i][:] for i in range(DT)])

        @block.vector
        def _(v):
            ld.wait(v)
            for i in range(DT):
                v.tensor_scalar_mul(normed[i][:, 0:3], normed[i][:, 0:3], lmask[:])
                ins = v.tensor_scalar_mul(normed[i][:, TE - 3:TE], normed[i][:, TE - 3:TE], rmask[:])
            ctrs["vec"].inc(ins)

        for d in dirs:
            @block.gpsimd
            def _(g):
                ctrs["pe"].wait(g)
                for i in range(DT):
                    wld.dma(g.dma_start(out=wz[i][:], in_=din[d]["inWT"].ap()[i * 128:(i + 1) * 128, DI:2 * DI]))

            for c in range(NCH):
                tok = slice(3 + c * CH, 3 + (c + 1) * CH)
                for j in range(DIT):
                    ps = psA if j % 2 == 0 else psB
                    jc = slice(j * 128, (j + 1) * 128)

                    @block.tensor
                    def _(t):
                        wld.wait(t)
                        ctrs["vec"].wait(t)
                        ctrs["sca"].wait_relax(t, c * DIT + j)
                        for i in range(DT):
                            ins = t.matmul(ps[:], wz[i][:, jc], normed[i][:, tok],
                                           start=(i == 0), stop=(i == DT - 1))
                        ctrs["pe"].inc(ins)

                    @block.scalar
                    def _(s):
                        ctrs["pe"].wait(s)
                        sto.wait(s, sto.n - 16 if sto.n >= 16 else 0)
                        zst = zstage[j % 2]
                        ins = s.activation(out=zst[:], in_=ps[:], func=AF.Silu)
                        ctrs["sca"].inc(ins)

                    @block.sync
                    def _(q):
                        ctrs["sca"].wait(q)
                        sto.dma(q.dma_start(out=dout[d]["zs"].ap()[jc, c * CH:(c + 1) * CH],
                                            in_=zstage[j % 2][:]))

            @block.gpsimd
            def _(g):
                ctrs["pe"].wait(g)
                for i in range(DT):
                    wld.dma(g.dma_start(out=wz[i][:], in_=din[d]["inWT"].ap()[i * 128:(i + 1) * 128, 0:DI]))

            for c in range(NCH):
                ext = slice(c * CH, c * CH + CH)
                ext2 = slice(c * CH + CH, c * CH + CH + 6)

                for j in range(DIT):
                    ps = psA if j % 2 == 0 else psB
                    jc = slice(j * 128, (j + 1) * 128)

                    @block.tensor
                    def _(t):
                        wld.wait(t)
                        ctrs["sca"].wait(t)
                        for i in range(DT):
                            t.matmul(ps[:], wz[i][:, jc], normed[i][:, ext],
                                     start=(i == 0), stop=(i == DT - 1))
                        for i in range(DT):
                            ins = t.matmul(psH[:, 0:6], wz[i][:, jc], normed[i][:, ext2],
                                           start=(i == 0), stop=(i == DT - 1))
                        ctrs["pe"].inc(ins)

                    @block.scalar
                    def _(s):
                        ctrs["pe"].wait(s)
                        ctrs["vec"].wait(s)
                        s.copy(xspre[j][:, 0:CH], ps[:])
                        ins = s.copy(xspre[j][:, CH:CH + 6], psH[:, 0:6])
                        ctrs["sca"].inc(ins)

                for j in range(DIT):
                    a0, a1 = acc[0], acc[1]

                    @block.vector
                    def _(v):
                        ctrs["sca"].wait(v)
                        shifts = (0, 1, 2, 3) if d == "f" else (6, 5, 4, 3)
                        v.tensor_scalar_mul(a0[:], xspre[j][:, shifts[0]:shifts[0] + CH],
                                            convw[d][j][:, 0:1])
                        for tap in (1, 2, 3):
                            sh = shifts[tap]
                            v.tensor_scalar_mul(a1[:], xspre[j][:, sh:sh + CH], convw[d][j][:, tap:tap + 1])
                            ins = v.tensor_add(a0[:], a0[:], a1[:])
                        ctrs["vec"].inc(ins)

                    @block.scalar
                    def _(s):
                        ctrs["vec"].wait(s)
                        ctrs["pe"].wait(s)
                        sto.wait(s, max(0, sto.n - 16))
                        xst = xsstage[j % 2]
                        s.activation(out=xst[:], in_=a0[:], func=AF.Silu, bias=convb[d][j][:])
                        ins = s.activation(out=xs_ch[j][:], in_=a0[:], func=AF.Silu, bias=convb[d][j][:])
                        ctrs["sca"].inc(ins)

                    @block.sync
                    def _(q):
                        ctrs["sca"].wait(q)
                        if standalone:
                            sto.dma(q.dma_start(out=dout[d]["xs"].ap()[j * 128:(j + 1) * 128, c * CH:(c + 1) * CH],
                                                in_=xsstage[j % 2][:]))
                        else:
                            par_branch(q,
                                lambda: sto.dma(q.dma_start(out=dout[d]["xs"].ap()[j * 128:(j + 1) * 128, c * CH:(c + 1) * CH], in_=xsstage[j % 2][:])),
                                lambda: sto.dma(q.dma_start(out=dout[d]["xs"].ap()[j * 128:(j + 1) * 128, TK + c * CH:TK + (c + 1) * CH], in_=xsstage[j % 2][:])),
                                bal=(sto,))

                @block.tensor
                def _(t):
                    ctrs["sca"].wait(t)
                    for i in range(DIT):
                        ins = t.matmul(psDT[:], xpdt[d][i][:], xs_ch[i][:],
                                       start=(i == 0), stop=(i == DIT - 1))
                    ctrs["pe"].inc(ins)

                @block.scalar
                def _(s):
                    ctrs["pe"].wait(s)
                    ins = s.copy(dtraw[:], psDT[:])
                    ctrs["sca"].inc(ins)

                @block.tensor
                def _(t):
                    ctrs["sca"].wait(t)
                    for i in range(DIT):
                        ins = t.matmul(psDT[0:DS, :], xpB[d][i][:], xs_ch[i][:],
                                       start=(i == 0), stop=(i == DIT - 1))
                    ctrs["pe"].inc(ins)

                @block.scalar
                def _(s):
                    ctrs["pe"].wait(s)
                    sto.wait(s, max(0, sto.n - 32))
                    ins = s.copy(bstage[:], psDT[0:DS, :])
                    ctrs["sca"].inc(ins)

                @block.sync
                def _(q):
                    ctrs["sca"].wait(q)
                    if standalone:
                        sto.dma(q.dma_start(out=dout[d]["Bm"].ap()[:, c * CH:(c + 1) * CH], in_=bstage[:]))
                    else:
                        par_branch(q,
                            lambda: sto.dma(q.dma_start(out=dout[d]["Bm"].ap()[:, c * CH:(c + 1) * CH], in_=bstage[:])),
                            lambda: sto.dma(q.dma_start(out=dout[d]["Bm"].ap()[:, TK + c * CH:TK + (c + 1) * CH], in_=bstage[:])),
                            bal=(sto,))

                @block.tensor
                def _(t):
                    ctrs["sca"].wait(t)
                    for i in range(DIT):
                        ins = t.matmul(psDT[0:DS, :], xpC[d][i][:], xs_ch[i][:],
                                       start=(i == 0), stop=(i == DIT - 1))
                    ctrs["pe"].inc(ins)

                @block.scalar
                def _(s):
                    ctrs["pe"].wait(s)
                    sto.wait(s, max(0, sto.n - 32))
                    ins = s.copy(cstage[:], psDT[0:DS, :])
                    ctrs["sca"].inc(ins)

                @block.sync
                def _(q):
                    ctrs["sca"].wait(q)
                    if standalone:
                        sto.dma(q.dma_start(out=dout[d]["Cm"].ap()[:, c * CH:(c + 1) * CH], in_=cstage[:]))
                    else:
                        par_branch(q,
                            lambda: sto.dma(q.dma_start(out=dout[d]["Cm"].ap()[:, c * CH:(c + 1) * CH], in_=cstage[:])),
                            lambda: sto.dma(q.dma_start(out=dout[d]["Cm"].ap()[:, TK + c * CH:TK + (c + 1) * CH], in_=cstage[:])),
                            bal=(sto,))

                for j in range(DIT):
                    ps = psA if j % 2 == 0 else psB
                    jc = slice(j * 128, (j + 1) * 128)

                    @block.tensor
                    def _(t):
                        ctrs["sca"].wait(t)
                        ctrs["vec"].wait(t)
                        ins = t.matmul(ps[:], dtw[d][0:RK, jc], dtraw[:], start=True, stop=True)
                        ctrs["pe"].inc(ins)

                    @block.vector
                    def _(v):
                        ctrs["pe"].wait(v)
                        sto.wait(v)
                        ins = v.tensor_scalar_add(sp1[:], ps[:], dtb[d][j][:])
                        ctrs["vec"].inc(ins)

                    @block.scalar
                    def _(s):
                        ctrs["vec"].wait(s)
                        ins = s.activation(out=sp2[:], in_=sp1[:], func=AF.Abs)
                        ctrs["sca"].inc(ins)
                        ctrs["sca"].wait(s)
                        ins = s.activation(out=sp3[:], in_=sp2[:], func=AF.Exp, scale=-1.0)
                        ctrs["sca"].inc(ins)
                        ctrs["sca"].wait(s)
                        ins = s.activation(out=sp2[:], in_=sp3[:], func=AF.Ln, bias=consts["ones_col_f32"][:])
                        ctrs["sca"].inc(ins)

                    @block.vector
                    def _(v):
                        ctrs["sca"].wait(v)
                        v.tensor_scalar_max(sp3[:], sp1[:], 0.0)
                        ins = v.tensor_add(sp1[:], sp3[:], sp2[:])
                        ctrs["vec"].inc(ins)

                    @block.sync
                    def _(q):
                        ctrs["vec"].wait(q)
                        if standalone:
                            sto.dma(q.dma_start(out=dout[d]["dt"].ap()[jc, c * CH:(c + 1) * CH], in_=sp1[:]))
                        else:
                            par_branch(q,
                                lambda: sto.dma(q.dma_start(out=dout[d]["dt"].ap()[jc, c * CH:(c + 1) * CH], in_=sp1[:])),
                                lambda: sto.dma(q.dma_start(out=dout[d]["dt"].ap()[jc, TK + c * CH:TK + (c + 1) * CH], in_=sp1[:])),
                                bal=(sto,))

        @block.sync
        def _(q):
            sto.wait(q)
    return nc


def run_mamba_pre(x, inputs):
    nc = build_mamba_pre()
    base = {}
    for d, pref in (("f", "L0_attn_f_"), ("b", "L0_attn_b_")):
        base[f"inWT_{d}"] = np.ascontiguousarray(inputs[pref + "inW"].T)
        base[f"convW_{d}"] = np.ascontiguousarray(inputs[pref + "convW"].reshape(DI, 4))
        base[f"convb_{d}"] = np.ascontiguousarray(inputs[pref + "convb"].reshape(DI, 1))
        xp = inputs[pref + "xprojW"]
        base[f"xpdtT_{d}"] = np.ascontiguousarray(xp[0:RK].T)
        base[f"xpBT_{d}"] = np.ascontiguousarray(xp[RK:RK + DS].T)
        base[f"xpCT_{d}"] = np.ascontiguousarray(xp[RK + DS:RK + 2 * DS].T)
        base[f"dtWT_{d}"] = np.ascontiguousarray(inputs[pref + "dtW"].T)
        base[f"dtb_{d}"] = np.ascontiguousarray(inputs[pref + "dtb"].reshape(DI, 1))
    base["lnw"] = np.ascontiguousarray(inputs["L0_attn_ln_w"].reshape(D, 1))
    base["lnb"] = np.ascontiguousarray(inputs["L0_attn_ln_b"].reshape(D, 1))
    import ml_dtypes
    in_maps = []
    for c in range(NCORES):
        m = dict(base)
        b, h = c // 2, c % 2
        m["xhT"] = _shard_tok_T(x, c, halo_left=3, halo_right=3)
        m["lmask"] = np.full((128, 1), 0.0 if h == 0 else 1.0, np.float32)
        m["rmask"] = np.full((128, 1), 0.0 if h == 1 else 1.0, np.float32)
        in_maps.append(m)
    res = run_spmd(nc, in_maps, core_ids=list(range(NCORES)))
    out = {}
    for name, rows in (("xs", DI), ("zs", DI), ("dt", DI), ("Bm", DS), ("Cm", DS)):
        for d in ("f", "b"):
            a = np.empty((B, L, rows), np.float32)
            for c in range(NCORES):
                b_, h = c // 2, c % 2
                a[b_, h * TK:(h + 1) * TK] = res.results[c][f"{name}_{d}"].T
            out[f"{name}_{d}"] = a
    return out


SCH = 256
SW = DS * (SCH + 1)


def build_scan():
    nc = bass.Bass("TRN2", target_bir_lowering=False, debug=False, num_devices=8)
    DIT = DI // 128
    NSC = L // SCH
    dt_d = nc.dram_tensor("dt", [DI, L], F32, kind="ExternalInput")
    xs_d = nc.dram_tensor("xs", [DI, L], F32, kind="ExternalInput")
    bm_d = nc.dram_tensor("bm", [DS, L], F32, kind="ExternalInput")
    cm_d = nc.dram_tensor("cm", [DS, L], F32, kind="ExternalInput")
    al_d = nc.dram_tensor("A_log", [DI, DS], F32, kind="ExternalInput")
    y_d = nc.dram_tensor("y", [DI, L], F32, kind="ExternalOutput")

    A = sb_tiles(nc, "A", DIT, 128, DS, F32)
    hstate = sb_tiles(nc, "hstate", DIT, 128, DS, F32)
    bmr = sb(nc, "bmr", DS, L, F32)
    cmr = sb(nc, "cmr", DS, L, F32)
    bbc = sb(nc, "bbc", 128, SW, F32)
    cbc = sb(nc, "cbc", 128, SW, F32)
    dtc = [sb(nc, f"dtc{k}", 128, SCH, F32) for k in range(2)]
    xsc = [sb(nc, f"xsc{k}", 128, SCH, F32) for k in range(2)]
    wt = sb(nc, "wt", 128, SCH, F32)
    da = sb(nc, "da", 128, SW, F32)
    dbx = sb(nc, "dbx", 128, SW, F32)
    hh = sb(nc, "hh", 128, SW, F32)
    yt = [sb(nc, f"yt{k}", 128, SCH, F32) for k in range(2)]
    psb = psum(nc, "psb", 128, SCH)
    psc = psum(nc, "psc", 128, SCH)

    with (
        nc.semaphore("ld") as ld_h,
        nc.semaphore("pe") as pe_h,
        nc.semaphore("sca") as sca_h,
        nc.semaphore("vec") as vec_h,
        nc.semaphore("sto") as sto_h,
        nc.Block() as block,
    ):
        ld = Ctr(ld_h)
        ctrs = {"pe": Ctr(pe_h), "sca": Ctr(sca_h), "vec": Ctr(vec_h)}
        pe, sca, vec = ctrs["pe"], ctrs["sca"], ctrs["vec"]
        sto = Ctr(sto_h)
        consts = make_consts(nc)
        load_consts(nc, block, consts, ld)

        @block.sync
        def _(q):
            for i in range(DIT):
                ld.dma(q.dma_start(out=A[i][:], in_=al_d.ap()[i * 128:(i + 1) * 128, :]))
            ld.dma(q.dma_start(out=bmr[:], in_=bm_d.ap()))
            ld.dma(q.dma_start(out=cmr[:], in_=cm_d.ap()))

        @block.vector
        def _(v):
            ld.wait(v)
            for i in range(DIT):
                ins = v.memset(hstate[i][:], 0.0)
            vec.inc(ins)

        @block.scalar
        def _(s):
            ld.wait(s)
            for i in range(DIT):
                ins = s.activation(out=A[i][:], in_=A[i][:], func=AF.Exp)
            sca.inc(ins)

        @block.vector
        def _(v):
            sca.wait(v)
            for i in range(DIT):
                ins = v.tensor_scalar_mul(A[i][:], A[i][:], -1.0)
            vec.inc(ins)

        @block.vector
        def _(v):
            slots = bass.AP(da, 0, [list(da[:].ap[0]), [SCH + 1, DS]])
            ins = v.memset(slots, 0.0)
            vec.inc(ins)

        for c in range(NSC):
            tc = slice(c * SCH, (c + 1) * SCH)
            for s in range(DS):
                reg = slice(s * (SCH + 1) + 1, s * (SCH + 1) + 1 + SCH)

                @block.tensor
                def _(t):
                    ld.wait(t)
                    vec.wait(t)
                    sca.wait(t)
                    selp = consts["sel16"][0:DS, s * 128:(s + 1) * 128]
                    t.matmul(psb[:], selp, bmr[0:DS, tc], start=True, stop=True)
                    ins = t.matmul(psc[:], selp, cmr[0:DS, tc], start=True, stop=True)
                    pe.inc(ins)

                @block.scalar
                def _(s_):
                    pe.wait(s_)
                    s_.copy(bbc[:, reg], psb[:])
                    ins = s_.copy(cbc[:, reg], psc[:])
                    sca.inc(ins)

            for i in range(DIT):
                buf = c % 2

                @block.sync
                def _(q):
                    vec.wait(q)
                    ld.dma(q.dma_start(out=dtc[buf][:], in_=dt_d.ap()[i * 128:(i + 1) * 128, tc]))
                    ld.dma(q.dma_start(out=xsc[buf][:], in_=xs_d.ap()[i * 128:(i + 1) * 128, tc]))

                @block.scalar
                def _(s_):
                    ld.wait(s_)
                    vec.wait(s_)
                    for s in range(DS):
                        reg = slice(s * (SCH + 1) + 1, s * (SCH + 1) + 1 + SCH)
                        ins = s_.activation(out=da[:, reg], in_=dtc[buf][:], func=AF.Exp,
                                            scale=A[i][:, s:s + 1])
                    sca.inc(ins)

                @block.vector
                def _(v):
                    ld.wait(v)
                    sca.wait(v)
                    v.tensor_mul(wt[:], dtc[buf][:], xsc[buf][:])
                    slots = bass.AP(dbx, 0, [list(dbx[:].ap[0]), [SCH + 1, DS]])
                    v.tensor_copy(slots, hstate[i][:])
                    for s in range(DS):
                        reg = slice(s * (SCH + 1) + 1, s * (SCH + 1) + 1 + SCH)
                        v.tensor_tensor(dbx[:, reg], wt[:], bbc[:, reg], ALU.mult)
                    v.tensor_tensor_scan(out=hh[:], data0=da[:], data1=dbx[:],
                                         initial=0.0, op0=ALU.mult, op1=ALU.add)
                    laststates = bass.AP(hh, SCH, [list(hh[:].ap[0]), [SCH + 1, DS]])
                    v.tensor_copy(hstate[i][:], laststates)
                    for s in range(DS):
                        reg = slice(s * (SCH + 1) + 1, s * (SCH + 1) + 1 + SCH)
                        v.tensor_tensor(dbx[:, reg], hh[:, reg], cbc[:, reg], ALU.mult)
                    sto.wait(v, max(0, sto.n - 16))
                    cyv = bass.AP(dbx, 1, [list(dbx[:].ap[0]), [1, SCH], [SCH + 1, DS]])
                    ins = v.tensor_reduce(yt[i % 2][:], cyv, axis=mybir.AxisListType.X, op=ALU.add)
                    vec.inc(ins)

                @block.sync
                def _(q):
                    vec.wait(q)
                    sto.dma(q.dma_start(out=y_d.ap()[i * 128:(i + 1) * 128, tc], in_=yt[i % 2][:]))

        @block.sync
        def _(q):
            sto.wait(q)
    return nc


def run_scan(pre, inputs):
    nc = build_scan()
    in_maps = []
    for c in range(NCORES):
        b, d = c % 4, ("f" if c < 4 else "b")
        rev = d == "b"

        def prep(a):
            a = a[::-1] if rev else a
            return np.ascontiguousarray(a.T)
        m = {
            "dt": prep(pre[f"dt_{d}"][b]),
            "xs": prep(pre[f"xs_{d}"][b]),
            "A_log": np.ascontiguousarray(inputs[f"L0_attn_{d}_A_log"]),
        }
        m["bm"] = prep(pre[f"Bm_{d}"][b])
        m["cm"] = prep(pre[f"Cm_{d}"][b])
        in_maps.append(m)
    res = run_spmd(nc, in_maps, core_ids=list(range(NCORES)))
    out = {}
    for d in ("f", "b"):
        y = np.empty((B, L, DI), np.float32)
        for b in range(4):
            c = b if d == "f" else b + 4
            yy = res.results[c]["y"].T
            y[b] = yy[::-1] if d == "b" else yy
        out[f"y_{d}"] = y
    return out


MPOST_PHASES = 3

def build_mamba_post(F=None):
    standalone = F is None
    nc = bass.Bass("TRN2", target_bir_lowering=False, debug=False, num_devices=8) if standalone else F["nc"]
    DIT = DI // 128
    dirs = ("f", "b")
    din = {}
    for d in dirs:
        din[d] = {
            "Dsk": nc.dram_tensor(f"Dsk_{d}", [DI, 1], F32, kind="ExternalInput"),
            "outWT": nc.dram_tensor(f"outWT_{d}", [DI, D], F32, kind="ExternalInput"),
            "projT": nc.dram_tensor(f"projT_{d}", [D, D], F32, kind="ExternalInput"),
        }
        if standalone:
            din[d]["y"] = nc.dram_tensor(f"y_{d}", [DI, TK], F32, kind="ExternalInput")
            din[d]["xs"] = nc.dram_tensor(f"xs2_{d}", [DI, TK], F32, kind="ExternalInput")
            din[d]["zs"] = nc.dram_tensor(f"zs2_{d}", [DI, TK], F32, kind="ExternalInput")
        else:
            din[d]["y"] = F["io"][f"sh_y_{d}"]
            din[d]["xs"] = F["io"][f"sh_xs_{d}"]
            din[d]["zs"] = F["io"][f"zs_{d}"]
    projb_d = nc.dram_tensor("projb", [D, 1], F32, kind="ExternalInput")
    xT_d = nc.dram_tensor("xT", [D, TK], F32, kind="ExternalInput")
    if standalone:
        o_d = nc.dram_tensor("x2T", [D, TK], F32, kind="ExternalOutput")
    else:
        o_d = F["io"]["x2T"]

    wbig = sb_tiles(nc, "wbig", DIT, 128, D, BF16)
    fo = sb_tiles(nc, "fo", DT, 128, TK, BF16)
    bo = sb_tiles(nc, "bo", DT, 128, TK, BF16)
    ystr = [sb(nc, f"ystr{k}", 128, CH, F32) for k in range(2)]
    xsstr = [sb(nc, f"xsstr{k}", 128, CH, F32) for k in range(2)]
    zsstr = [sb(nc, f"zsstr{k}", 128, CH, F32) for k in range(2)]
    dcol = {d: sb_tiles(nc, f"dcol_{d}", DIT, 128, 1, F32) for d in dirs}
    pbcol = sb_tiles(nc, "pbcol", DT, 128, 1, F32)
    yfin = sb_tiles(nc, "yfin", DIT, 128, CH, BF16)
    xch = sb_tiles(nc, "xch", DT, 128, CH, F32)
    x2p = sb_tiles(nc, "x2p", DT, 128, CH, F32)
    t1 = sb(nc, "t1", 128, CH, F32)
    lnb = ln_alloc(nc, "p", DT, CH)
    psA = psum(nc, "psA", 128, CH)
    psB = psum(nc, "psB", 128, CH)

    import contextlib
    es = contextlib.ExitStack()
    block, consts, ld, wld, ctrs, sto = _f_ctx(
        F, nc, es, ["ld", "wld", "pe", "sca", "vec", "sto"], need_sel=False)
    with es:
        pe, sca, vec = ctrs["pe"], ctrs["sca"], ctrs["vec"]


        @block.gpsimd
        def _(g):
            for d in dirs:
                for i in range(DIT):
                    ld.dma(g.dma_start(out=dcol[d][i][:], in_=din[d]["Dsk"].ap()[i * 128:(i + 1) * 128, :]))
            for i in range(DT):
                ld.dma(g.dma_start(out=pbcol[i][:], in_=projb_d.ap()[i * 128:(i + 1) * 128, :]))

        if MPOST_PHASES < 1:
            @block.sync
            def _(q):
                ld.wait(q)
            return nc
        for d in dirs:
            dst = fo if d == "f" else bo

            @block.gpsimd
            def _(g):
                pe.wait(g)
                for i in range(DIT):
                    wld.dma(g.dma_start(out=wbig[i][:], in_=din[d]["outWT"].ap()[i * 128:(i + 1) * 128, :]))

            for c in range(NCH):
                tc = slice(c * CH, (c + 1) * CH)
                for i in range(DIT):
                    bu = i % 2

                    @block.sync
                    def _(q):
                        vec.wait_relax(q, i)
                        r = slice(i * 128, (i + 1) * 128)
                        if standalone:
                            ld.dma(q.dma_start(out=ystr[bu][:], in_=din[d]["y"].ap()[r, tc]))
                            ld.dma(q.dma_start(out=xsstr[bu][:], in_=din[d]["xs"].ap()[r, tc]))
                        else:
                            def _l(off):
                                cs2 = slice(off + c * CH, off + (c + 1) * CH)
                                ld.dma(q.dma_start(out=ystr[bu][:], in_=din[d]["y"].ap()[r, cs2]))
                                ld.dma(q.dma_start(out=xsstr[bu][:], in_=din[d]["xs"].ap()[r, cs2]))
                            par_branch(q, lambda: _l(0), lambda: _l(TK), bal=(ld,))
                        ld.dma(q.dma_start(out=zsstr[bu][:], in_=din[d]["zs"].ap()[r, tc]))

                    @block.vector
                    def _(v):
                        ld.wait(v)
                        pe.wait(v)
                        v.tensor_scalar_mul(t1[:], xsstr[bu][:], dcol[d][i][:])
                        v.tensor_add(t1[:], ystr[bu][:], t1[:])
                        ins = v.tensor_mul(yfin[i][:], t1[:], zsstr[bu][:])
                        vec.inc(ins)

                for j in range(DT):
                    ps = psA if j % 2 == 0 else psB
                    jc = slice(j * 128, (j + 1) * 128)

                    @block.tensor
                    def _(t):
                        wld.wait(t)
                        vec.wait(t)
                        sca.wait(t)
                        for i in range(DIT):
                            ins = t.matmul(ps[:], wbig[i][:, jc], yfin[i][:],
                                           start=(i == 0), stop=(i == DIT - 1))
                        pe.inc(ins)

                    @block.scalar
                    def _(s):
                        pe.wait(s)
                        ins = s.copy(dst[j][:, tc], ps[:])
                        sca.inc(ins)

        if MPOST_PHASES < 2:
            @block.sync
            def _(q):
                sto.wait(q)
            return nc
        @block.gpsimd
        def _(g):
            pe.wait(g)
            sca.wait(g)
            for i in range(DT):
                wld.dma(g.dma_start(out=wbig[i][:], in_=din["f"]["projT"].ap()[i * 128:(i + 1) * 128, :]))
                wld.dma(g.dma_start(out=wbig[DT + i][:], in_=din["b"]["projT"].ap()[i * 128:(i + 1) * 128, :]))

        for c in range(NCH):
            tc = slice(c * CH, (c + 1) * CH)

            @block.sync
            def _(q):
                vec.wait(q)
                for i in range(DT):
                    ld.dma(q.dma_start(out=xch[i][:], in_=xT_d.ap()[i * 128:(i + 1) * 128, tc]))

            for j in range(DT):
                ps = psA if j % 2 == 0 else psB
                jc = slice(j * 128, (j + 1) * 128)

                @block.tensor
                def _(t):
                    wld.wait(t)
                    vec.wait_relax(t, j)
                    sca.wait(t)
                    for i in range(DT):
                        t.matmul(ps[:], wbig[i][:, jc], fo[i][:, tc], start=(i == 0), stop=False)
                    for i in range(DT):
                        ins = t.matmul(ps[:], wbig[DT + i][:, jc], bo[i][:, tc],
                                       start=False, stop=(i == DT - 1))
                    pe.inc(ins)

                @block.vector
                def _(v):
                    pe.wait(v)
                    ld.wait(v)
                    v.scalar_tensor_tensor(out=t1[:], in0=xch[j][:], scalar=2.0, in1=ps[:],
                                           op0=ALU.mult, op1=ALU.add)
                    ins = v.tensor_scalar_add(x2p[j][:], t1[:], pbcol[j][:])
                    vec.inc(ins)

            if MPOST_PHASES >= 3:
                ln_plain_T(nc, block, ctrs, x2p, x2p, DT, CH, consts, lnb)

            @block.sync
            def _(q):
                vec.wait(q)
                for i in range(DT):
                    sto.dma(q.dma_start(out=o_d.ap()[i * 128:(i + 1) * 128, tc], in_=x2p[i][:]))
                if (not standalone) and c == NCH - 1:
                    def _st(sl):
                        for i in range(DT):
                            sto.dma(q.dma_start(
                                out=F["io"]["x2c"].ap()[i * 128:(i + 1) * 128, sl],
                                in_=x2p[i][:, CH - 1:CH],
                                allow_slow_non_contiguous=True))
                    par_branch(q, lambda: _st(slice(0, 1)), lambda: _st(slice(1, 2)), bal=(sto,))

        @block.sync
        def _(q):
            sto.wait(q)
    return nc


def run_mamba_post(x, pre, ys, inputs):
    nc = build_mamba_post()
    base = {"projb": np.ascontiguousarray(inputs["L0_attn_proj_b"].reshape(D, 1))}
    pw = inputs["L0_attn_proj_W"]
    for d, pref in (("f", "L0_attn_f_"), ("b", "L0_attn_b_")):
        base[f"Dsk_{d}"] = np.ascontiguousarray(inputs[pref + "Dskip"].reshape(DI, 1))
        base[f"outWT_{d}"] = np.ascontiguousarray(inputs[pref + "outW"].T)
    base["projT_f"] = np.ascontiguousarray(pw[:, 0:D].T)
    base["projT_b"] = np.ascontiguousarray(pw[:, D:2 * D].T)
    in_maps = []
    for c in range(NCORES):
        b, h = c // 2, c % 2
        m = dict(base)
        m["xT"] = _shard_tok_T(x, c)
        for d in ("f", "b"):
            m[f"y_{d}"] = np.ascontiguousarray(ys[f"y_{d}"][b, h * TK:(h + 1) * TK].T)
            m[f"xs2_{d}"] = np.ascontiguousarray(pre[f"xs_{d}"][b, h * TK:(h + 1) * TK].T)
            m[f"zs2_{d}"] = np.ascontiguousarray(pre[f"zs_{d}"][b, h * TK:(h + 1) * TK].T)
        in_maps.append(m)
    res = run_spmd(nc, in_maps, core_ids=list(range(NCORES)))
    x2 = np.empty((B, L, D), np.float32)
    for c in range(NCORES):
        b, h = c // 2, c % 2
        x2[b, h * TK:(h + 1) * TK] = res.results[c]["x2T"].T
    return x2


HEADS, KVH, DH, ROT = 16, 8, 64, 32
ASCALE = DH ** (-0.5)


def _rope_tables(n0, n, scale):
    pos = np.arange(n0, n0 + n, dtype=np.float64)
    inv = 1.0 / 10000.0 ** (np.arange(0, ROT, 2, dtype=np.float64) / ROT)
    cos = np.full((128, n), scale, np.float64)
    sin = np.zeros((128, n), np.float64)
    for half in (0, 64):
        for r in range(ROT):
            f = inv[r // 2]
            cos[half + r] = np.cos(pos * f) * scale
            sin[half + r] = np.sin(pos * f) * scale
    return cos.astype(np.float32), sin.astype(np.float32)


def _rope_perm():
    P = np.zeros((128, 128), np.float32)
    for half in (0, 64):
        for r in range(0, ROT, 2):
            P[half + r + 1, half + r] = -1.0
            P[half + r, half + r + 1] = 1.0
    return P


def build_attn(F=None):
    standalone = F is None
    nc = bass.Bass("TRN2", target_bir_lowering=False, debug=False, num_devices=8) if standalone else F["nc"]
    NKT = L // 128
    if standalone:
        x3f_d = nc.dram_tensor("x3bf", [D, L], F32, kind="ExternalInput")
        x3o_d = nc.dram_tensor("x3own", [D, TK], F32, kind="ExternalInput")
    else:
        x3f_d = F["io"]["sh_x3"]
        x3o_d = F["io"]["x3T"]
    wq_d = nc.dram_tensor("Wq", [D, D], F32, kind="ExternalInput")
    wk_d = nc.dram_tensor("Wk", [D, KVH * DH], F32, kind="ExternalInput")
    wv_d = nc.dram_tensor("Wv", [D, KVH * DH], F32, kind="ExternalInput")
    wo_d = nc.dram_tensor("Wo", [D, D], F32, kind="ExternalInput")
    bo_d = nc.dram_tensor("bo", [D, 1], F32, kind="ExternalInput")
    cosq_d = nc.dram_tensor("cosq", [128, TK], F32, kind="ExternalInput")
    sinq_d = nc.dram_tensor("sinq", [128, TK], F32, kind="ExternalInput")
    cosk_d = nc.dram_tensor("cosk", [128, L], F32, kind="ExternalInput")
    sink_d = nc.dram_tensor("sink", [128, L], F32, kind="ExternalInput")
    ropeP_d = nc.dram_tensor("ropeP", [128, 128], F32, kind="ExternalInput")
    if standalone:
        o_d = nc.dram_tensor("x4T", [D, TK], F32, kind="ExternalOutput")
    else:
        o_d = F["io"]["x4T"]

    KD = KVH * DH
    x3c = sb_tiles(nc, "x3c", DT, 128, CH, BF16)
    wqb = sb_tiles(nc, "wqb", DT, 128, D, BF16)
    wkb = sb_tiles(nc, "wkb", DT, 128, KD, BF16)
    wvb = sb_tiles(nc, "wvb", DT, 128, KD, BF16)
    wob = sb_tiles(nc, "wob", DT, 128, D, BF16)
    bocol = sb_tiles(nc, "bocol", DT, 128, 1, F32)
    ropeP = sb(nc, "ropeP_s", 128, 128, F32)
    ctab = sb(nc, "ctab", 128, CH, F32)
    stab = sb(nc, "stab", 128, CH, F32)
    krope = [sb(nc, f"krope{i}", 128, L, BF16) for i in range(KD // 128)]
    vaug = [sb(nc, f"vaug{i}", 128, KVH * (DH + 1), BF16) for i in range(NKT)]
    qrope = sb_tiles(nc, "qrope", DT, 128, CH, F32)
    qaug = sb(nc, "qaug", DH + 1, CH, BF16)
    kaug = sb(nc, "kaug", DH + 1, L, BF16)
    pT = [sb(nc, f"pT{k}", 128, CH, BF16) for k in range(2)]
    sqv = sb(nc, "sqv", 128, CH, BF16)
    nrow = sb(nc, "nrow", 1, CH, F32)
    brow = sb(nc, "brow", 1, CH, F32)
    krow = sb(nc, "krow", 1, CH, F32)
    kmax = sb(nc, "kmax", 1, 1, F32)
    kcur = sb(nc, "kcur", 1, 1, F32)
    kmaxs = sb_tiles(nc, "kmaxs", KVH, 1, 1, F32)
    orow = sb(nc, "orow", 1, CH, F32)
    of32 = sb(nc, "of32", DH, CH, F32)
    otb = sb_tiles(nc, "otb", DT, 128, CH, BF16)
    x4p = sb_tiles(nc, "x4p", DT, 128, CH, F32)
    t1 = sb(nc, "at1", 128, CH, F32)
    t2 = sb(nc, "at2", 128, CH, F32)
    lnb = ln_alloc(nc, "a", DT, CH, xb=x3c, xsq=otb)
    psA = psum(nc, "psA", 128, CH)
    psB = psum(nc, "psB", 128, CH)
    psO = psum(nc, "psO", DH + 1, CH)
    psR = psum(nc, "psR", DH, CH)

    import contextlib
    es = contextlib.ExitStack()
    block, consts, ld, wld, ctrs, sto = _f_ctx(
        F, nc, es, ["ld", "pe", "sca", "vec", "sto"], need_sel=False)
    with es:
        pe, sca, vec = ctrs["pe"], ctrs["sca"], ctrs["vec"]

        @block.gpsimd
        def _(g):
            for i in range(DT):
                r = slice(i * 128, (i + 1) * 128)
                ld.dma(g.dma_start(out=wqb[i][:], in_=wq_d.ap()[r, :]))
                ld.dma(g.dma_start(out=wkb[i][:], in_=wk_d.ap()[r, :]))
                ld.dma(g.dma_start(out=wvb[i][:], in_=wv_d.ap()[r, :]))
                ld.dma(g.dma_start(out=wob[i][:], in_=wo_d.ap()[r, :]))
                ld.dma(g.dma_start(out=bocol[i][:], in_=bo_d.ap()[r, :]))
            ld.dma(g.dma_start(out=ropeP[:], in_=ropeP_d.ap()))

        for c in range(L // CH):
            tc = slice(c * CH, (c + 1) * CH)

            @block.gpsimd
            def _(g):
                pe.wait(g)
                for i in range(DT):
                    ld.dma(g.dma_start(out=x3c[i][:], in_=x3f_d.ap()[i * 128:(i + 1) * 128, tc]))

            @block.sync
            def _(q):
                vec.wait(q)
                ld.dma(q.dma_start(out=ctab[:], in_=cosk_d.ap()[:, tc]))
                ld.dma(q.dma_start(out=stab[:], in_=sink_d.ap()[:, tc]))

            for j in range(KD // 128):
                ps = psA if j % 2 == 0 else psB
                rps = psB if j % 2 == 0 else psA
                jc = slice(j * 128, (j + 1) * 128)

                @block.tensor
                def _(t):
                    ld.wait(t)
                    sca.wait(t)
                    vec.wait(t)
                    for i in range(DT):
                        ins = t.matmul(ps[:], wkb[i][:, jc], x3c[i][:], start=(i == 0), stop=(i == DT - 1))
                    pe.inc(ins)

                @block.scalar
                def _(s):
                    pe.wait(s)
                    ins = s.copy(t1[:], ps[:])
                    sca.inc(ins)

                @block.tensor
                def _(t):
                    sca.wait(t)
                    ins = t.matmul(rps[:], ropeP[:], t1[:], start=True, stop=True)
                    pe.inc(ins)

                @block.vector
                def _(v):
                    pe.wait(v)
                    sca.wait(v)
                    v.tensor_tensor(t2[:], t1[:], ctab[:], ALU.mult)
                    v.tensor_tensor(t1[:], rps[:], stab[:], ALU.mult)
                    ins = v.tensor_add(krope[j][:, tc], t2[:], t1[:])
                    vec.inc(ins)

            for tt in range(CH // 128):
                kt = c * (CH // 128) + tt
                ps = psA if tt % 2 == 0 else psB

                @block.tensor
                def _(t):
                    vec.wait(t)
                    sca.wait(t)
                    for i in range(DT):
                        ins = t.matmul(ps[:], x3c[i][:, tt * 128:(tt + 1) * 128], wvb[i][:],
                                       start=(i == 0), stop=(i == DT - 1))
                    pe.inc(ins)

                @block.scalar
                def _(s):
                    pe.wait(s)
                    for h2 in range(KVH):
                        ins = s.copy(vaug[kt][:, h2 * (DH + 1):h2 * (DH + 1) + DH],
                                     ps[:, h2 * DH:(h2 + 1) * DH])
                    sca.inc(ins)

        @block.vector
        def _(v):
            sca.wait(v)
            for kt in range(NKT):
                oc = bass.AP(vaug[kt], DH, [list(vaug[kt][:].ap[0]), [DH + 1, KVH]])
                ins = v.memset(oc, 1.0)
            vec.inc(ins)

        for h2 in range(KVH):
            ti, ro = h2 // 2, (h2 % 2) * 64

            for c in range(L // CH):
                @block.scalar
                def _(s):
                    pe.wait(s)
                    vec.wait(s)
                    ins = s.activation(out=sqv[0:DH, 0:CH], in_=krope[ti][ro:ro + DH, c * CH:(c + 1) * CH],
                                       func=AF.Square)
                    sca.inc(ins)

                @block.tensor
                def _(t):
                    sca.wait(t)
                    vec.wait(t)
                    ins = t.matmul(psA[0:1, 0:CH], consts["ones_col_bf"][0:DH, :], sqv[0:DH, 0:CH],
                                   start=True, stop=True)
                    pe.inc(ins)

                @block.vector
                def _(v):
                    pe.wait(v)
                    v.tensor_reduce(kcur[:], psA[0:1, 0:CH], axis=mybir.AxisListType.X, op=ALU.max)
                    if c == 0:
                        ins = v.tensor_copy(kmax[:], kcur[:])
                    else:
                        ins = v.tensor_max(kmax[:], kmax[:], kcur[:])
                    vec.inc(ins)

            @block.scalar
            def _(s):
                vec.wait(s)
                ins = s.activation(out=kmaxs[h2][:], in_=kmax[:], func=AF.Sqrt)
                sca.inc(ins)

        for c in range(NCH):
            tc = slice(c * CH, (c + 1) * CH)

            @block.gpsimd
            def _(g):
                pe.wait(g)
                for i in range(DT):
                    ld.dma(g.dma_start(out=x3c[i][:], in_=x3o_d.ap()[i * 128:(i + 1) * 128, tc]))

            @block.sync
            def _(q):
                vec.wait(q)
                ld.dma(q.dma_start(out=ctab[:], in_=cosq_d.ap()[:, tc]))
                ld.dma(q.dma_start(out=stab[:], in_=sinq_d.ap()[:, tc]))

            for j in range(DT):
                ps = psA if j % 2 == 0 else psB
                rps = psB if j % 2 == 0 else psA
                jc = slice(j * 128, (j + 1) * 128)

                @block.tensor
                def _(t):
                    ld.wait(t)
                    vec.wait(t)
                    sca.wait(t)
                    for i in range(DT):
                        ins = t.matmul(ps[:], wqb[i][:, jc], x3c[i][:], start=(i == 0), stop=(i == DT - 1))
                    pe.inc(ins)

                @block.scalar
                def _(s):
                    pe.wait(s)
                    ins = s.copy(t1[:], ps[:])
                    sca.inc(ins)

                @block.tensor
                def _(t):
                    sca.wait(t)
                    ins = t.matmul(rps[:], ropeP[:], t1[:], start=True, stop=True)
                    pe.inc(ins)

                @block.vector
                def _(v):
                    pe.wait(v)
                    sca.wait(v)
                    v.tensor_tensor(t2[:], t1[:], ctab[:], ALU.mult)
                    v.tensor_tensor(t1[:], rps[:], stab[:], ALU.mult)
                    ins = v.tensor_add(qrope[j][:], t2[:], t1[:])
                    vec.inc(ins)

            for h in range(HEADS):
                ti, ro = h // 2, (h % 2) * 64
                h2 = h // 2

                @block.scalar
                def _(s):
                    vec.wait(s)
                    pe.wait(s)
                    s.copy(qaug[0:DH, :], qrope[ti][ro:ro + DH, :])
                    ins = s.activation(out=sqv[0:DH, 0:CH], in_=qrope[ti][ro:ro + DH, :], func=AF.Square)
                    sca.inc(ins)

                @block.tensor
                def _(t):
                    sca.wait(t)
                    vec.wait(t)
                    ins = t.matmul(psO[0:1, 0:CH], consts["ones_col_bf"][0:DH, :], sqv[0:DH, 0:CH],
                                   start=True, stop=True)
                    pe.inc(ins)

                @block.scalar
                def _(s):
                    pe.wait(s)
                    ins = s.activation(out=nrow[:], in_=psO[0:1, 0:CH], func=AF.Sqrt)
                    sca.inc(ins)

                @block.vector
                def _(v):
                    sca.wait(v)
                    v.tensor_scalar(out=brow[:], in0=nrow[:], scalar1=kmaxs[h2][:],
                                    scalar2=-1.02, op0=ALU.mult, op1=ALU.mult)
                    ins = v.tensor_copy(qaug[DH:DH + 1, :], brow[:])
                    vec.inc(ins)

                if h % 2 == 0:
                    kti, kro = h2 // 2, (h2 % 2) * 64
                    @block.vector
                    def _(v):
                        pe.wait(v)
                        v.tensor_copy(kaug[0:DH, :], krope[kti][kro:kro + DH, :])
                        ins = v.memset(kaug[DH:DH + 1, :], 1.0)
                        vec.inc(ins)

                for kt in range(NKT):
                    ps = psA if kt % 2 == 0 else psB

                    @block.tensor
                    def _(t):
                        vec.wait(t)
                        sca.wait_relax(t, kt)
                        ins = t.matmul(ps[:], kaug[0:DH + 1, kt * 128:(kt + 1) * 128],
                                       qaug[0:DH + 1, :], start=True, stop=True)
                        pe.inc(ins)

                    @block.scalar
                    def _(s):
                        pe.wait(s)
                        ins = s.activation(out=pT[kt % 2][:], in_=ps[:], func=AF.Exp)
                        sca.inc(ins)

                    @block.tensor
                    def _(t):
                        sca.wait(t)
                        ins = t.matmul(psO[0:DH + 1, 0:CH], vaug[kt][:, h2 * (DH + 1):(h2 + 1) * (DH + 1)],
                                       pT[kt % 2][:], start=(kt == 0), stop=(kt == NKT - 1))
                        pe.inc(ins)

                @block.vector
                def _(v):
                    pe.wait(v)
                    ins = v.reciprocal(orow[:], psO[DH:DH + 1, 0:CH])
                    vec.inc(ins)

                @block.scalar
                def _(s):
                    pe.wait(s)
                    ins = s.copy(of32[:], psO[0:DH, 0:CH])
                    sca.inc(ins)

                @block.tensor
                def _(t):
                    vec.wait(t)
                    sca.wait(t)
                    ins = t.matmul(psR[0:DH, 0:CH], consts["ones_row_f32"][0:1, 0:DH], orow[:],
                                   start=True, stop=True)
                    pe.inc(ins)

                @block.vector
                def _(v):
                    pe.wait(v)
                    sca.wait(v)
                    ins = v.tensor_tensor(otb[ti][ro:ro + DH, :], of32[:], psR[0:DH, 0:CH], ALU.mult)
                    vec.inc(ins)

            xoch = qrope
            @block.sync
            def _(q):
                vec.wait(q)
                sca.wait(q)
                for i in range(DT):
                    ld.dma(q.dma_start(out=xoch[i][:], in_=x3o_d.ap()[i * 128:(i + 1) * 128, tc]))

            for j in range(DT):
                ps = psA if j % 2 == 0 else psB
                jc = slice(j * 128, (j + 1) * 128)

                @block.tensor
                def _(t):
                    vec.wait(t)
                    sca.wait(t)
                    for i in range(DT):
                        ins = t.matmul(ps[:], wob[i][:, jc], otb[i][:], start=(i == 0), stop=(i == DT - 1))
                    pe.inc(ins)

                @block.vector
                def _(v):
                    pe.wait(v)
                    ld.wait(v)
                    v.tensor_scalar_add(t1[:], ps[:], bocol[j][:])
                    ins = v.tensor_add(x4p[j][:], xoch[j][:], t1[:])
                    vec.inc(ins)

            ln_plain_T(nc, block, ctrs, x4p, x4p, DT, CH, consts, lnb)

            @block.sync
            def _(q):
                vec.wait(q)
                for i in range(DT):
                    sto.dma(q.dma_start(out=o_d.ap()[i * 128:(i + 1) * 128, tc], in_=x4p[i][:]))

        @block.sync
        def _(q):
            sto.wait(q)
    return nc


def run_attn(x3, inputs):
    nc = build_attn()
    base = {
        "Wq": np.ascontiguousarray(inputs["L1_attn_Wq"]),
        "Wk": np.ascontiguousarray(inputs["L1_attn_Wk"]),
        "Wv": np.ascontiguousarray(inputs["L1_attn_Wv"]),
        "Wo": np.ascontiguousarray(inputs["L1_attn_Wo"]),
        "bo": np.ascontiguousarray(inputs["L1_attn_bo"].reshape(D, 1)),
        "ropeP": _rope_perm(),
    }
    ck, sk = _rope_tables(0, L, 1.0)
    base["cosk"], base["sink"] = ck, sk
    in_maps = []
    for c in range(NCORES):
        b, h = c // 2, c % 2
        m = dict(base)
        x3T = np.ascontiguousarray(x3[b].T)
        m["x3bf"] = x3T
        m["x3own"] = np.ascontiguousarray(x3T[:, h * TK:(h + 1) * TK])
        cq, sq = _rope_tables(h * TK, TK, ASCALE)
        m["cosq"], m["sinq"] = cq, sq
        in_maps.append(m)
    res = run_spmd(nc, in_maps, core_ids=list(range(NCORES)))
    x4 = np.empty((B, L, D), np.float32)
    for c in range(NCORES):
        b, h = c // 2, c % 2
        x4[b, h * TK:(h + 1) * TK] = res.results[c]["x4T"].T
    return x4


NE = 16
NG, EPG, TKG, TOPK = 4, 4, 2, 4
MSCALE = 2.5
HALF = TK // 2


def build_moe(F=None):
    standalone = F is None
    nc = bass.Bass("TRN2", target_bir_lowering=False, debug=False, num_devices=8) if standalone else F["nc"]
    x4_d = nc.dram_tensor("x4T", [D, TK], F32, kind="ExternalInput") if standalone else F["io"]["x4T"]
    rt_d = nc.dram_tensor("routerT", [D, NE], F32, kind="ExternalInput")
    bias_d = nc.dram_tensor("bias16", [1, NE], F32, kind="ExternalInput")
    gw_d = nc.dram_tensor("gate_w", [NE, D, 1024], F32, kind="ExternalInput")
    uw_d = nc.dram_tensor("up_w", [NE, D, 1024], F32, kind="ExternalInput")
    dw_d = nc.dram_tensor("down_w", [NE, 1024, D], F32, kind="ExternalInput")
    shg_d = nc.dram_tensor("shg", [D, 2048], F32, kind="ExternalInput")
    shu_d = nc.dram_tensor("shu", [D, 2048], F32, kind="ExternalInput")
    shd_d = nc.dram_tensor("shd", [2048, D], F32, kind="ExternalInput")
    id_d = nc.dram_tensor("ident128", [128, 128], F32, kind="ExternalInput")
    o_d = nc.dram_tensor("outT", [D, TK], F32, kind="ExternalOutput")

    x4bf = sb_tiles(nc, "m_x4bf", DT, 128, HALF, BF16)
    racc = sb_tiles(nc, "m_racc", DT, 128, HALF, F32)
    wbuf2 = [sb_tiles(nc, f"m_wbuf{k}", DT, 128, 2048, BF16) for k in range(2)]
    hush = sb_tiles(nc, "m_hush", 2 * DT, 128, HALF, BF16)
    hs = hush[0:DT]
    rtb = sb_tiles(nc, "m_rtb", DT, 128, NE, BF16)
    biasr = sb(nc, "m_biasr", 1, NE, F32)
    biasb = sb(nc, "m_biasb", 128, NE, F32)
    ident = sb(nc, "m_ident", 128, 128, F32)
    scoresT = sb(nc, "m_scoresT", NE, HALF, F32)
    combT = sb(nc, "m_combT", NE, HALF, F32)
    stok = sb(nc, "m_stok", 128, NE, F32)
    sbv = sb(nc, "m_sb", 128, NE, F32)
    rank16 = sb(nc, "m_rank16", 128, NE, F32)
    m01 = sb(nc, "m_m01", 128, NE, F32)
    gsc = sb(nc, "m_gsc", 128, NG, F32)
    grank = sb(nc, "m_grank", 128, NG, F32)
    gm01 = sb(nc, "m_gm01", 128, NG, F32)
    emask = sb(nc, "m_emask", 128, NE, F32)
    ones16 = sb(nc, "m_ones16", 128, NE, F32)
    neg1 = sb(nc, "m_neg1", 128, NE, F32)
    den = sb(nc, "m_den", 128, 1, F32)
    thrc = sb(nc, "m_thrc", 128, 1, F32)
    ut = sb(nc, "m_ut", 128, CH, BF16)
    cb = sb(nc, "m_cb", 128, CH, F32)
    xch = sb_tiles(nc, "m_xch", DT, 128, CH, F32)
    x5p = sb_tiles(nc, "m_x5p", DT, 128, CH, F32)
    lnb = ln_alloc(nc, "m5", DT, CH, xb=hush[0:DT], xsq=hush[DT:2 * DT])
    tmoe = lnb["tmp"]
    psA = psum(nc, "psA", 128, CH)
    psB = psum(nc, "psB", 128, CH)
    psC = psum(nc, "psC", NE, CH)
    psT = psum(nc, "psT", 128, CH)

    import contextlib
    es = contextlib.ExitStack()
    block, consts, ld, wld, ctrs, sto = _f_ctx(
        F, nc, es, ["ld", "wld", "pe", "sca", "vec", "sto"], need_sel=True)
    with es:
        pe, sca, vec = ctrs["pe"], ctrs["sca"], ctrs["vec"]
        if F is not None:
            sl16 = sb(nc, "c_sel16_moe", 16, 16 * 128, F32)

            @block.sync
            def _(q):
                ld.dma(q.dma_start(out=sl16[:], in_=consts["t_sel16"].ap()))
            consts = dict(consts)
            consts["sel16"] = sl16

        @block.sync
        def _(q):
            ld.dma(q.dma_start(out=biasr[:], in_=bias_d.ap()))
            ld.dma(q.dma_start(out=ident[:], in_=id_d.ap()))

        @block.gpsimd
        def _(g):
            for i in range(DT):
                ld.dma(g.dma_start(out=rtb[i][:], in_=rt_d.ap()[i * 128:(i + 1) * 128, :]))

        @block.vector
        def _(v):
            ld.wait(v)
            v.memset(ones16[:], 1.0)
            ins = v.memset(neg1[:], -1.0)
            vec.inc(ins)

        @block.tensor
        def _(t):
            ld.wait(t)
            ins = t.matmul(psT[0:128, 0:NE], consts["ones_row_f32"][:], biasr[:], start=True, stop=True)
            pe.inc(ins)

        @block.scalar
        def _(s):
            pe.wait(s)
            ins = s.copy(biasb[:], psT[0:128, 0:NE])
            sca.inc(ins)

        for half in range(2):
            hs_tok = slice(half * HALF, (half + 1) * HALF)

            @block.gpsimd
            def _(g):
                pe.wait(g)
                for i in range(DT):
                    ld.dma(g.dma_start(out=x4bf[i][:], in_=x4_d.ap()[i * 128:(i + 1) * 128, hs_tok]))

            @block.vector
            def _(v):
                ld.wait(v)
                sto.wait(v)
                for i in range(DT):
                    ins = v.memset(racc[i][:], 0.0)
                vec.inc(ins)

            for c2 in range(HALF // CH):
                cs = slice(c2 * CH, (c2 + 1) * CH)

                @block.tensor
                def _(t):
                    ld.wait(t)
                    sca.wait(t)
                    for i in range(DT):
                        ins = t.matmul(psC[:, 0:CH], rtb[i][:], x4bf[i][:, cs],
                                       start=(i == 0), stop=(i == DT - 1))
                    pe.inc(ins)

                @block.scalar
                def _(s):
                    pe.wait(s)
                    ins = s.activation(out=scoresT[0:NE, cs], in_=psC[:, 0:CH], func=AF.Sigmoid)
                    sca.inc(ins)

            for tt in range(HALF // 128):
                @block.tensor
                def _(t):
                    sca.wait(t)
                    vec.wait(t)
                    ins = t.transpose(psT[0:128, 0:NE], scoresT[0:NE, tt * 128:(tt + 1) * 128], ident[0:NE, 0:NE])
                    pe.inc(ins)

                @block.vector
                def _(v):
                    def E(ins):
                        vec.inc(ins)
                        vec.wait(v)
                    pe.wait(v)
                    E(v.tensor_copy(stok[:], psT[0:128, 0:NE]))
                    E(v.tensor_add(sbv[:], stok[:], biasb[:]))
                    E(v.memset(rank16[:], 0.0))
                    for g in range(NG):
                        gsl = slice(g * EPG, (g + 1) * EPG)
                        for e2 in range(EPG):
                            col = sbv[:, g * EPG + e2:g * EPG + e2 + 1]
                            v.scalar_tensor_tensor(out=rank16[:, gsl], in0=sbv[:, gsl], scalar=col,
                                                   in1=rank16[:, gsl], op0=ALU.is_lt, op1=ALU.add)
                            if e2 < EPG - 1:
                                t2sl = slice(g * EPG + e2 + 1, (g + 1) * EPG)
                                v.scalar_tensor_tensor(out=rank16[:, t2sl], in0=sbv[:, t2sl], scalar=col,
                                                       in1=rank16[:, t2sl], op0=ALU.is_equal, op1=ALU.add)
                    E(v.memset(thrc[:], 2.0))
                    E(v.tensor_single_scalar(m01[:], rank16[:], thrc[:], ALU.is_lt))
                    E(v.tensor_mul(m01[:], m01[:], sbv[:]))
                    for g in range(NG):
                        E(v.tensor_reduce(gsc[:, g:g + 1], m01[:, g * EPG:(g + 1) * EPG],
                                          axis=mybir.AxisListType.X, op=ALU.add))
                    E(v.memset(grank[:], 0.0))
                    for g2 in range(NG):
                        col = gsc[:, g2:g2 + 1]
                        v.scalar_tensor_tensor(out=grank[:], in0=gsc[:], scalar=col,
                                               in1=grank[:], op0=ALU.is_lt, op1=ALU.add)
                        if g2 < NG - 1:
                            t2sl = slice(g2 + 1, NG)
                            v.scalar_tensor_tensor(out=grank[:, t2sl], in0=gsc[:, t2sl], scalar=col,
                                                   in1=grank[:, t2sl], op0=ALU.is_equal, op1=ALU.add)
                    E(v.engine_nop())
                    E(v.tensor_single_scalar(gm01[:], grank[:], thrc[:], ALU.is_lt))
                    for g in range(NG):
                        E(v.tensor_scalar_mul(emask[:, g * EPG:(g + 1) * EPG], ones16[:, 0:EPG],
                                              gm01[:, g:g + 1]))
                    E(v.tensor_scalar_add(m01[:], sbv[:], 1.0))
                    E(v.tensor_mul(m01[:], m01[:], emask[:]))
                    E(v.tensor_scalar_add(sbv[:], m01[:], -1.0))
                    E(v.memset(rank16[:], 0.0))
                    for e2 in range(NE):
                        col = sbv[:, e2:e2 + 1]
                        v.scalar_tensor_tensor(out=rank16[:], in0=sbv[:], scalar=col,
                                               in1=rank16[:], op0=ALU.is_lt, op1=ALU.add)
                        if e2 < NE - 1:
                            t2sl = slice(e2 + 1, NE)
                            v.scalar_tensor_tensor(out=rank16[:, t2sl], in0=sbv[:, t2sl], scalar=col,
                                                   in1=rank16[:, t2sl], op0=ALU.is_equal, op1=ALU.add)
                    E(v.memset(thrc[:], float(TOPK)))
                    E(v.tensor_single_scalar(m01[:], rank16[:], thrc[:], ALU.is_lt))
                    E(v.tensor_mul(m01[:], m01[:], stok[:]))
                    E(v.tensor_reduce(den[:], m01[:], axis=mybir.AxisListType.X, op=ALU.add))
                    E(v.tensor_scalar_add(den[:], den[:], 1e-20))
                    E(v.reciprocal(den[:], den[:]))
                    E(v.tensor_scalar(out=stok[:], in0=m01[:], scalar1=den[:],
                                      scalar2=MSCALE, op0=ALU.mult, op1=ALU.mult))

                @block.tensor
                def _(t):
                    vec.wait(t)
                    sca.wait(t)
                    ins = t.transpose(psT[0:NE, 0:128], stok[:, 0:NE], ident[:])
                    pe.inc(ins)

                @block.scalar
                def _(s):
                    pe.wait(s)
                    ins = s.copy(combT[0:NE, tt * 128:(tt + 1) * 128], psT[0:NE, 0:128])
                    sca.inc(ins)

            phist = [pe.n, pe.n]
            tgi = [0]
            for e in range(NE):
                for wsrc, tag in ((gw_d, "g"), (uw_d, "u"), (dw_d, "d")):
                    wbuf = wbuf2[tgi[0] % 2]

                    @block.gpsimd
                    def _(g):
                        pe.wait(g, phist[-2])
                        for i in range(DT):
                            wld.dma(g.dma_start(out=wbuf[i][:, 0:1024],
                                                in_=wsrc.ap()[e, i * 128:(i + 1) * 128, :]))

                    if tag == "g":
                        for tch in range(HALF // CH):
                            ts = slice(tch * CH, (tch + 1) * CH)
                            for fj in range(DT):
                                ps = psA if fj % 2 == 0 else psB

                                @block.tensor
                                def _(t):
                                    wld.wait(t)
                                    sca.wait_relax(t, tch * DT + fj)
                                    vec.wait(t)
                                    for i in range(DT):
                                        ins = t.matmul(ps[:], wbuf[i][:, fj * 128:(fj + 1) * 128],
                                                       x4bf[i][:, ts], start=(i == 0), stop=(i == DT - 1))
                                    pe.inc(ins)

                                @block.scalar
                                def _(s):
                                    pe.wait(s)
                                    ins = s.activation(out=hs[fj][:, ts], in_=ps[:], func=AF.Silu)
                                    sca.inc(ins)
                    elif tag == "u":
                        for tch in range(HALF // CH):
                            ts = slice(tch * CH, (tch + 1) * CH)
                            for fj in range(DT):
                                ps = psA if fj % 2 == 0 else psB

                                @block.tensor
                                def _(t):
                                    wld.wait(t)
                                    sca.wait(t)
                                    vec.wait(t)
                                    for i in range(DT):
                                        ins = t.matmul(ps[:], wbuf[i][:, fj * 128:(fj + 1) * 128],
                                                       x4bf[i][:, ts], start=(i == 0), stop=(i == DT - 1))
                                    pe.inc(ins)

                                @block.scalar
                                def _(s):
                                    pe.wait(s)
                                    ins = s.copy(ut[:], ps[:])
                                    sca.inc(ins)

                                @block.vector
                                def _(v):
                                    sca.wait(v)
                                    ins = v.tensor_mul(hs[fj][:, ts], hs[fj][:, ts], ut[:])
                                    vec.inc(ins)
                    else:
                        for tch in range(HALF // CH):
                            ts = slice(tch * CH, (tch + 1) * CH)

                            @block.tensor
                            def _(t):
                                sca.wait(t)
                                vec.wait(t)
                                ins = t.matmul(psT[:, 0:CH], consts["sel16"][0:NE, e * 128:(e + 1) * 128],
                                               combT[0:NE, ts], start=True, stop=True)
                                pe.inc(ins)

                            @block.scalar
                            def _(s):
                                pe.wait(s)
                                ins = s.copy(cb[:], psT[:, 0:CH])
                                sca.inc(ins)

                            for dj in range(DT):
                                ps = psA if dj % 2 == 0 else psB

                                @block.tensor
                                def _(t):
                                    wld.wait(t)
                                    sca.wait(t)
                                    vec.wait(t)
                                    for fj in range(DT):
                                        ins = t.matmul(ps[:], wbuf[fj][:, dj * 128:(dj + 1) * 128],
                                                       hs[fj][:, ts], start=(fj == 0), stop=(fj == DT - 1))
                                    pe.inc(ins)

                                @block.vector
                                def _(v):
                                    pe.wait(v)
                                    sca.wait(v)
                                    v.tensor_tensor(tmoe[:], ps[:], cb[:], ALU.mult)
                                    ins = v.tensor_add(racc[dj][:, ts], racc[dj][:, ts], tmoe[:])
                                    vec.inc(ins)
                    phist.append(pe.n)
                    tgi[0] += 1

            pe_sh0 = pe.n
            for phase in ("g", "u", "d"):
                wbuf = wbuf2[{"g": 0, "u": 1, "d": 0}[phase]]

                @block.gpsimd
                def _(g):
                    if phase == "g":
                        pe.wait(g, phist[-1])
                        pe.wait(g)
                    elif phase == "u":
                        pe.wait(g, pe_sh0)
                    else:
                        pe.wait(g)
                    if phase in ("g", "u"):
                        src = shg_d if phase == "g" else shu_d
                        for i in range(DT):
                            wld.dma(g.dma_start(out=wbuf[i][:], in_=src.ap()[i * 128:(i + 1) * 128, :]))
                    else:
                        for k in range(2 * DT):
                            wld.dma(g.dma_start(out=wbuf[k // 2][:, (k % 2) * 1024:(k % 2 + 1) * 1024],
                                                in_=shd_d.ap()[k * 128:(k + 1) * 128, :]))

                if phase in ("g", "u"):
                    for tch in range(HALF // CH):
                        ts = slice(tch * CH, (tch + 1) * CH)
                        for fj in range(2 * DT):
                            ps = psA if fj % 2 == 0 else psB

                            @block.tensor
                            def _(t):
                                wld.wait(t)
                                sca.wait_relax(t, tch * 2 * DT + fj)
                                vec.wait(t)
                                for i in range(DT):
                                    ins = t.matmul(ps[:], wbuf[i][:, fj * 128:(fj + 1) * 128],
                                                   x4bf[i][:, ts], start=(i == 0), stop=(i == DT - 1))
                                pe.inc(ins)

                            if phase == "g":
                                @block.scalar
                                def _(s):
                                    pe.wait(s)
                                    ins = s.activation(out=hush[fj][:, ts], in_=ps[:], func=AF.Silu)
                                    sca.inc(ins)
                            else:
                                @block.scalar
                                def _(s):
                                    pe.wait(s)
                                    ins = s.copy(ut[:], ps[:])
                                    sca.inc(ins)

                                @block.vector
                                def _(v):
                                    sca.wait(v)
                                    ins = v.tensor_mul(hush[fj][:, ts], hush[fj][:, ts], ut[:])
                                    vec.inc(ins)
                else:
                    for tch in range(HALF // CH):
                        ts = slice(tch * CH, (tch + 1) * CH)
                        for dj in range(DT):
                            ps = psA if dj % 2 == 0 else psB

                            @block.tensor
                            def _(t):
                                wld.wait(t)
                                sca.wait(t)
                                vec.wait(t)
                                for k in range(2 * DT):
                                    ins = t.matmul(ps[:], wbuf[k // 2][:, (k % 2) * 1024 + dj * 128:
                                                                       (k % 2) * 1024 + (dj + 1) * 128],
                                                   hush[k][:, ts], start=(k == 0), stop=(k == 2 * DT - 1))
                                pe.inc(ins)

                            @block.vector
                            def _(v):
                                pe.wait(v)
                                ins = v.tensor_add(racc[dj][:, ts], racc[dj][:, ts], ps[:])
                                vec.inc(ins)

            for tch in range(HALF // CH):
                ts = slice(tch * CH, (tch + 1) * CH)
                gts = slice(half * HALF + tch * CH, half * HALF + (tch + 1) * CH)

                @block.sync
                def _(q):
                    vec.wait(q)
                    for i in range(DT):
                        ld.dma(q.dma_start(out=xch[i][:], in_=x4_d.ap()[i * 128:(i + 1) * 128, gts]))

                @block.vector
                def _(v):
                    ld.wait(v)
                    sto.wait(v, max(0, sto.n - 16 * DT))
                    for i in range(DT):
                        ins = v.tensor_add(x5p[i][:], xch[i][:], racc[i][:, ts])
                    vec.inc(ins)

                ln_plain_T(nc, block, ctrs, x5p, x5p, DT, CH, consts, lnb)

                @block.sync
                def _(q):
                    vec.wait(q)
                    for i in range(DT):
                        sto.dma(q.dma_start(out=o_d.ap()[i * 128:(i + 1) * 128, gts], in_=x5p[i][:]))

        @block.sync
        def _(q):
            sto.wait(q)
    return nc


def run_moe(x4, inputs):
    nc = build_moe()
    base = {
        "routerT": np.ascontiguousarray(inputs["L1_ffn_router_w"].T),
        "bias16": np.ascontiguousarray(inputs["L1_ffn_bias"].reshape(1, NE)),
        "gate_w": np.ascontiguousarray(inputs["L1_ffn_gate_w"]),
        "up_w": np.ascontiguousarray(inputs["L1_ffn_up_w"]),
        "down_w": np.ascontiguousarray(inputs["L1_ffn_down_w"]),
        "shg": np.ascontiguousarray(inputs["L1_ffn_sh_gate_w"]),
        "shu": np.ascontiguousarray(inputs["L1_ffn_sh_up_w"]),
        "shd": np.ascontiguousarray(inputs["L1_ffn_sh_down_w"]),
        "ident128": np.eye(128, dtype=np.float32),
    }
    in_maps = []
    for c in range(NCORES):
        m = dict(base)
        m["x4T"] = _shard_tok_T(x4, c)
        in_maps.append(m)
    res = run_spmd(nc, in_maps, core_ids=list(range(NCORES)))
    out = np.empty((B, L, D), np.float32)
    for c in range(NCORES):
        b, h = c // 2, c % 2
        out[b, h * TK:(h + 1) * TK] = res.results[c]["outT"].T
    return out


def emit_scan_fused(F):
    nc, block = F["nc"], F["block"]
    ld, ctrs, sto = F["ld"], F["ctrs"], F["sto"]
    pe, sca, vec = ctrs["pe"], ctrs["sca"], ctrs["vec"]
    consts = F["consts"]
    NDT = 8
    NSC = L // SCH
    al_d = {d: nc.dram_tensor(f"al_{d}", [1024, DS], F32, kind="ExternalInput") for d in ("f", "b")}

    sl16 = sb(nc, "c_sel16_scan", 16, 16 * 128, F32)
    A = sb_tiles(nc, "sc_A", NDT, 128, DS, F32)
    hstate = sb_tiles(nc, "sc_hstate", NDT, 128, DS, F32)
    bmr = sb(nc, "sc_bmr", DS, L, F32)
    cmr = sb(nc, "sc_cmr", DS, L, F32)
    bbc = sb(nc, "sc_bbc", 128, SW, F32)
    cbc = sb(nc, "sc_cbc", 128, SW, F32)
    bstg = sb(nc, "sc_bstg", 128, SCH, F32)
    dtc = [sb(nc, f"sc_dtc{k}", 128, SCH, F32) for k in range(2)]
    xsc = [sb(nc, f"sc_xsc{k}", 128, SCH, F32) for k in range(2)]
    dtr = sb(nc, "sc_dtr", 128, SCH, F32)
    xsr = sb(nc, "sc_xsr", 128, SCH, F32)
    wt = sb(nc, "sc_wt", 128, SCH, F32)
    da2 = [sb(nc, f"sc_da{k}", 128, SW, F32) for k in range(2)]
    dbx = sb(nc, "sc_dbx", 128, SW, F32)
    hh = sb(nc, "sc_hh", 128, SW, F32)
    yt = [sb(nc, f"sc_yt{k}", 128, SCH, F32) for k in range(2)]
    ytr = sb(nc, "sc_ytr", 128, SCH, F32)
    psb = psum(nc, "sc_psb", 128, 2 * SCH)
    psc = psum(nc, "sc_psc", 128, 2 * SCH)

    @block.sync
    def _(q):
        ld.dma(q.dma_start(out=sl16[:], in_=consts["t_sel16"].ap()))

    for d in ("f", "b"):
        rev = d == "b"
        sh_dt, sh_xs = F["io"][f"sh_dt_{d}"], F["io"][f"sh_xs_{d}"]
        sh_bm, sh_cm = F["io"][f"sh_Bm_{d}"], F["io"][f"sh_Cm_{d}"]
        sh_y = F["io"][f"sh_y_{d}"]

        @block.sync
        def _(q):
            vec.wait(q)
            sca.wait(q)
            for i in range(NDT):
                ld.dma(q.dma_start(out=A[i][:],
                                   in_=al_d[d].ap()[i * 128:(i + 1) * 128, :]))
            ld.dma(q.dma_start(out=bmr[:], in_=sh_bm.ap()))
            ld.dma(q.dma_start(out=cmr[:], in_=sh_cm.ap()))

        @block.vector
        def _(v):
            ld.wait(v)
            for i in range(NDT):
                ins = v.memset(hstate[i][:], 0.0)
            ctrs["vec"].inc(ins)

        @block.scalar
        def _(s):
            ld.wait(s)
            vec.wait(s)
            for i in range(NDT):
                ins = s.activation(out=A[i][:], in_=A[i][:], func=AF.Exp)
            sca.inc(ins)

        @block.vector
        def _(v):
            sca.wait(v)
            for i in range(NDT):
                v.tensor_scalar_mul(A[i][:], A[i][:], -1.0)
            for k in range(2):
                slots = bass.AP(da2[k], 0, [list(da2[k][:].ap[0]), [SCH + 1, DS]])
                ins = v.memset(slots, 0.0)
            vec.inc(ins)
        vhist = []

        for c in range(NSC):
            rc0 = (L - (c + 1) * SCH) if rev else (c * SCH)
            tc = slice(rc0, rc0 + SCH)
            for sp in range(DS // 2):
                s0, s1 = 2 * sp, 2 * sp + 1
                reg0 = slice(s0 * (SCH + 1) + 1, s0 * (SCH + 1) + 1 + SCH)
                reg1 = slice(s1 * (SCH + 1) + 1, s1 * (SCH + 1) + 1 + SCH)
                h0, h1 = slice(0, SCH), slice(SCH, 2 * SCH)

                @block.tensor
                def _(t):
                    ld.wait(t)
                    vec.wait(t)
                    sca.wait(t)
                    sel0 = sl16[0:DS, s0 * 128:(s0 + 1) * 128]
                    sel1 = sl16[0:DS, s1 * 128:(s1 + 1) * 128]
                    t.matmul(psb[:, h0], sel0, bmr[0:DS, tc], start=True, stop=True)
                    t.matmul(psb[:, h1], sel1, bmr[0:DS, tc], start=True, stop=True)
                    t.matmul(psc[:, h0], sel0, cmr[0:DS, tc], start=True, stop=True)
                    ins = t.matmul(psc[:, h1], sel1, cmr[0:DS, tc], start=True, stop=True)
                    pe.inc(ins)

                if not rev:
                    @block.scalar
                    def _(s_):
                        pe.wait(s_)
                        s_.copy(bbc[:, reg0], psb[:, h0])
                        s_.copy(bbc[:, reg1], psb[:, h1])
                        s_.copy(cbc[:, reg0], psc[:, h0])
                        ins = s_.copy(cbc[:, reg1], psc[:, h1])
                        sca.inc(ins)
                else:
                    @block.scalar
                    def _(s_):
                        pe.wait(s_)
                        s_.copy(bstg[:], psb[:, h0])
                        s_.copy(dtr[:], psb[:, h1])
                        s_.copy(xsr[:], psc[:, h0])
                        ins = s_.copy(ytr[:], psc[:, h1])
                        sca.inc(ins)

                    @block.vector
                    def _(v):
                        sca.wait(v)
                        v.tensor_copy(bbc[:, reg0], bstg[:, ::-1])
                        v.tensor_copy(bbc[:, reg1], dtr[:, ::-1])
                        v.tensor_copy(cbc[:, reg0], xsr[:, ::-1])
                        ins = v.tensor_copy(cbc[:, reg1], ytr[:, ::-1])
                        vec.inc(ins)

            for i in range(NDT):
                buf = (c * NDT + i) % 2

                @block.gpsimd
                def _(g):
                    vec.wait_relax(g, c * NDT + i, back=(4 if rev else 1))

                    def _lr(base):
                        ro = slice(base + i * 128, base + (i + 1) * 128)
                        ld.dma(g.dma_start(out=dtc[buf][:], in_=sh_dt.ap()[ro, tc]))
                        ld.dma(g.dma_start(out=xsc[buf][:], in_=sh_xs.ap()[ro, tc]))
                    par_branch(g, lambda: _lr(0), lambda: _lr(1024), bal=(ld,))

                if rev:
                    @block.vector
                    def _(v):
                        ld.wait(v)
                        sca.wait(v)
                        v.tensor_copy(dtr[:], dtc[buf][:, ::-1])
                        ins = v.tensor_copy(xsr[:], xsc[buf][:, ::-1])
                        vec.inc(ins)
                    dts, xss = dtr, xsr
                else:
                    dts, xss = dtc[buf], xsc[buf]

                ig = c * NDT + i
                da = da2[ig % 2]

                @block.scalar
                def _(s_):
                    ld.wait(s_)
                    if rev or len(vhist) < 2:
                        vec.wait(s_)
                    else:
                        vec.wait(s_, vhist[-2])
                    for s in range(DS):
                        reg = slice(s * (SCH + 1) + 1, s * (SCH + 1) + 1 + SCH)
                        ins = s_.activation(out=da[:, reg], in_=dts[:], func=AF.Exp,
                                            scale=A[i][:, s:s + 1])
                    sca.inc(ins)

                @block.vector
                def _(v):
                    ld.wait(v)
                    sca.wait(v)
                    v.tensor_mul(wt[:], dts[:], xss[:])
                    slots = bass.AP(dbx, 0, [list(dbx[:].ap[0]), [SCH + 1, DS]])
                    v.tensor_copy(slots, hstate[i][:])
                    for s in range(DS):
                        reg = slice(s * (SCH + 1) + 1, s * (SCH + 1) + 1 + SCH)
                        v.tensor_tensor(dbx[:, reg], wt[:], bbc[:, reg], ALU.mult)
                    v.tensor_tensor_scan(out=hh[:], data0=da[:], data1=dbx[:],
                                         initial=0.0, op0=ALU.mult, op1=ALU.add)
                    laststates = bass.AP(hh, SCH, [list(hh[:].ap[0]), [SCH + 1, DS]])
                    v.tensor_copy(hstate[i][:], laststates)
                    for s in range(DS):
                        reg = slice(s * (SCH + 1) + 1, s * (SCH + 1) + 1 + SCH)
                        v.tensor_tensor(dbx[:, reg], hh[:, reg], cbc[:, reg], ALU.mult)
                    sto.wait(v, max(0, sto.n - 16))
                    cyv = bass.AP(dbx, 1, [list(dbx[:].ap[0]), [1, SCH], [SCH + 1, DS]])
                    ins = v.tensor_reduce(yt[i % 2][:], cyv, axis=mybir.AxisListType.X, op=ALU.add)
                    if rev:
                        vec.inc(ins)
                        vec.wait(v)
                        ins = v.tensor_copy(ytr[:], yt[i % 2][:, ::-1])
                        vec.inc(ins)
                        vec.wait(v)
                        ins = v.tensor_copy(yt[i % 2][:], ytr[:])
                    vec.inc(ins)

                vhist.append(vec.n)

                @block.gpsimd
                def _(g):
                    vec.wait(g)

                    def _sr(base):
                        ro = slice(base + i * 128, base + (i + 1) * 128)
                        sto.dma(g.dma_start(out=sh_y.ap()[ro, tc], in_=yt[i % 2][:]))
                    par_branch(g, lambda: _sr(0), lambda: _sr(1024), bal=(sto,))

    @block.gpsimd
    def _(g):
        sto.wait(g)


I32 = mybir.dt.int32
FUSED = True


def _scope(pfx):
    import contextlib

    class _S:
        def __enter__(self):
            self.st = contextlib.ExitStack()
            self.old = dict(_ALLOC)
            _ALLOC["stack"] = self.st
            _ALLOC["pfx"] = pfx
            self.st.__enter__()
            return self

        def __exit__(self, *a):
            r = self.st.__exit__(*a)
            _ALLOC.update(self.old)
            return r
    return _S()


def build_fused():
    nc = bass.Bass("TRN2", target_bir_lowering=False, debug=False, num_devices=8)
    consts = make_consts(nc)

    def shd(name, shape, dt):
        return nc.dram_tensor(name, shape, dt, kind="Internal", addr_space="Shared")

    def locd(name, shape, dt):
        return nc.dram_tensor(name, shape, dt, kind="Internal")
    io = {}
    for d in ("f", "b"):
        io[f"sh_xs_{d}"] = shd(f"sh_xs_{d}", [DI, L], F32)
        io[f"sh_dt_{d}"] = shd(f"sh_dt_{d}", [DI, L], F32)
        io[f"sh_Bm_{d}"] = shd(f"sh_Bm_{d}", [DS, L], F32)
        io[f"sh_Cm_{d}"] = shd(f"sh_Cm_{d}", [DS, L], F32)
        io[f"sh_y_{d}"] = shd(f"sh_y_{d}", [DI, L], F32)
        io[f"zs_{d}"] = locd(f"zs_{d}", [DI, TK], F32)
    io["x2T"] = locd("x2T", [D, TK], F32)
    io["x2c"] = shd("sh_x2c", [D, 2], F32)
    io["x3T"] = locd("x3T", [D, TK], F32)
    io["sh_x3"] = shd("sh_x3", [D, L], BF16)
    io["x4T"] = locd("x4T", [D, TK], F32)
    sh_flag = shd("sh_flag", [1, 2], I32)
    salt_d = nc.dram_tensor("salt", [1, 8], I32, kind="ExternalInput")

    saltt = sb(nc, "saltt", 1, 8, I32)
    fread = sb(nc, "fread", 1, 1, I32)

    import contextlib
    with contextlib.ExitStack() as TOP:
        hs = {n: TOP.enter_context(nc.semaphore(n)) for n in
              ("ld", "wld", "pe", "sca", "vec", "sto", "xc", "fl")}
        block = TOP.enter_context(nc.Block())
        ld, wld = Ctr(hs["ld"]), Ctr(hs["wld"])
        ctrs = {"pe": Ctr(hs["pe"]), "sca": Ctr(hs["sca"]), "vec": Ctr(hs["vec"])}
        sto = Ctr(hs["sto"])
        xcc = Ctr(hs["xc"])
        flc = Ctr(hs["fl"])
        load_consts(nc, block, consts, ld, need_sel=False)

        @block.sync
        def _(q):
            ld.dma(q.dma_start(out=saltt[:], in_=salt_d.ap()))

        F = {"nc": nc, "block": block, "consts": consts, "ld": ld, "wld": wld,
             "ctrs": ctrs, "sto": sto, "io": io}
        allc = [ld, wld, sto, xcc, flc] + list(ctrs.values())

        def barrier():
            for engname in ("sync", "gpsimd", "tensor", "scalar", "vector"):
                def go(e):
                    for ct in allc:
                        ct.wait(e)
                getattr(block, engname)(go)

        def sync_point(k):
            @block.gpsimd
            def _(g):
                for ct in [ld, wld, sto] + list(ctrs.values()):
                    ct.wait(g)
                par_branch(g,
                    lambda: flc.dma(g.dma_start(out=sh_flag.ap()[0:1, 0:1], in_=saltt[0:1, k - 1:k])),
                    lambda: flc.dma(g.dma_start(out=sh_flag.ap()[0:1, 1:2], in_=saltt[0:1, k - 1:k])),
                    bal=(flc,))
                with g.register(f"flr{k}") as reg, g.register(f"tg{k}") as tgt:
                    g.reg_load(tgt, saltt[0:1, k - 1:k])

                    def cond():
                        par_branch(g,
                            lambda: flc.dma(g.dma_start(out=fread[:], in_=sh_flag.ap()[0:1, 1:2])),
                            lambda: flc.dma(g.dma_start(out=fread[:], in_=sh_flag.ap()[0:1, 0:1])),
                            bal=(flc,))
                        g.wait_ge(hs["fl"], flc.n)
                        g.reg_load(reg, fread[0:1, 0:1])
                        return g.scalar_reg_alu(ALU.not_equal, reg, tgt)
                    with g.While(cond):
                        g.engine_nop()
                xcc.inc(g.engine_nop())
            barrier()

        with _scope("P1_"):
            build_mamba_pre(F)
        sync_point(1)
        with _scope("P2_"):
            emit_scan_fused(F)
        sync_point(2)
        with _scope("P3_"):
            build_mamba_post(F)
        sync_point(3)
        with _scope("P4_"):
            build_rwkv(F)
        sync_point(4)
        with _scope("P5_"):
            build_attn(F)
        barrier()
        with _scope("P6_"):
            build_moe(F)
    return nc


def kernel_fused(inputs):
    if "fused" not in _PROGS:
        _PROGS["fused"] = build_fused()
    nc = _PROGS["fused"]
    x = np.ascontiguousarray(inputs["L0_attn_x"], dtype=np.float32)
    base = {}
    for d, pref in (("f", "L0_attn_f_"), ("b", "L0_attn_b_")):
        base[f"inWT_{d}"] = np.ascontiguousarray(inputs[pref + "inW"].T)
        base[f"convW_{d}"] = np.ascontiguousarray(inputs[pref + "convW"].reshape(DI, 4))
        base[f"convb_{d}"] = np.ascontiguousarray(inputs[pref + "convb"].reshape(DI, 1))
        xp = inputs[pref + "xprojW"]
        base[f"xpdtT_{d}"] = np.ascontiguousarray(xp[0:RK].T)
        base[f"xpBT_{d}"] = np.ascontiguousarray(xp[RK:RK + DS].T)
        base[f"xpCT_{d}"] = np.ascontiguousarray(xp[RK + DS:RK + 2 * DS].T)
        base[f"dtWT_{d}"] = np.ascontiguousarray(inputs[pref + "dtW"].T)
        base[f"dtb_{d}"] = np.ascontiguousarray(inputs[pref + "dtb"].reshape(DI, 1))
        base[f"Dsk_{d}"] = np.ascontiguousarray(inputs[pref + "Dskip"].reshape(DI, 1))
        base[f"outWT_{d}"] = np.ascontiguousarray(inputs[pref + "outW"].T)
    base["lnw"] = np.ascontiguousarray(inputs["L0_attn_ln_w"].reshape(D, 1))
    base["lnb"] = np.ascontiguousarray(inputs["L0_attn_ln_b"].reshape(D, 1))
    pw = inputs["L0_attn_proj_W"]
    base["projT_f"] = np.ascontiguousarray(pw[:, 0:D].T)
    base["projT_b"] = np.ascontiguousarray(pw[:, D:2 * D].T)
    base["projb"] = np.ascontiguousarray(inputs["L0_attn_proj_b"].reshape(D, 1))
    base["wrT"] = np.ascontiguousarray(inputs["L0_ffn_W_r"].T)
    base["wkT"] = np.ascontiguousarray(inputs["L0_ffn_W_k"].T)
    base["wvT"] = np.ascontiguousarray(inputs["L0_ffn_W_v"].T)
    base["mur"] = np.ascontiguousarray(inputs["L0_ffn_mu_r"].reshape(D, 1))
    base["muk"] = np.ascontiguousarray(inputs["L0_ffn_mu_k"].reshape(D, 1))
    base["Wq"] = np.ascontiguousarray(inputs["L1_attn_Wq"])
    base["Wk"] = np.ascontiguousarray(inputs["L1_attn_Wk"])
    base["Wv"] = np.ascontiguousarray(inputs["L1_attn_Wv"])
    base["Wo"] = np.ascontiguousarray(inputs["L1_attn_Wo"])
    base["bo"] = np.ascontiguousarray(inputs["L1_attn_bo"].reshape(D, 1))
    base["ropeP"] = _rope_perm()
    ck, sk = _rope_tables(0, L, 1.0)
    base["cosk"], base["sink"] = ck, sk
    base["routerT"] = np.ascontiguousarray(inputs["L1_ffn_router_w"].T)
    base["bias16"] = np.ascontiguousarray(inputs["L1_ffn_bias"].reshape(1, NE))
    base["gate_w"] = np.ascontiguousarray(inputs["L1_ffn_gate_w"])
    base["up_w"] = np.ascontiguousarray(inputs["L1_ffn_up_w"])
    base["down_w"] = np.ascontiguousarray(inputs["L1_ffn_down_w"])
    base["shg"] = np.ascontiguousarray(inputs["L1_ffn_sh_gate_w"])
    base["shu"] = np.ascontiguousarray(inputs["L1_ffn_sh_up_w"])
    base["shd"] = np.ascontiguousarray(inputs["L1_ffn_sh_down_w"])
    base["ident128"] = np.eye(128, dtype=np.float32)
    salt = int(np.random.randint(1, 2**28))
    base["salt"] = (salt + np.arange(1, 9, dtype=np.int64)).astype(np.int32).reshape(1, 8)
    in_maps = []
    for c in range(NCORES):
        b, h = c // 2, c % 2
        m = dict(base)
        m["xhT"] = _shard_tok_T(x, c, halo_left=3, halo_right=3)
        m["lmask"] = np.full((128, 1), 0.0 if h == 0 else 1.0, np.float32)
        m["rmask"] = np.full((128, 1), 0.0 if h == 1 else 1.0, np.float32)
        m["pmask"] = np.full((128, 1), 0.0 if h == 0 else 1.0, np.float32)
        for d, pref in (("f", "L0_attn_f_"), ("b", "L0_attn_b_")):
            m[f"al_{d}"] = np.ascontiguousarray(inputs[pref + "A_log"][h * 1024:(h + 1) * 1024])
        m["xT"] = _shard_tok_T(x, c)
        cq, sq = _rope_tables(h * TK, TK, ASCALE)
        m["cosq"], m["sinq"] = cq, sq
        in_maps.append(m)
    res = run_spmd(nc, in_maps, core_ids=list(range(NCORES)))
    out = np.empty((B, L, D), np.float32)
    for c in range(NCORES):
        b, h = c // 2, c % 2
        out[b, h * TK:(h + 1) * TK] = res.results[c]["outT"].T
    return out


# version 09 (final)
# speedup vs baseline: 1.7257x; 1.0564x over previous
import numpy as np
import concourse.bass as bass
import concourse.mybir as mybir
from concourse.bass_utils import run_bass_kernel_spmd

F32 = mybir.dt.float32
BF16 = mybir.dt.bfloat16
AF = mybir.ActivationFunctionType
ALU = mybir.AluOpType

NCORES = 8
B, L, D = 4, 4096, 1024
TK = 2048
DT = D // 128
DI = 2048
DS = 16
RK = 64
CH = 512
NCH = TK // CH
EPS = 1e-5


def run_spmd(nc, in_maps, core_ids):
    import concourse.mybir as _mb
    want = {}
    for alloc in nc.m.functions[0].allocations:
        if isinstance(alloc, _mb.MemoryLocationSet) and alloc.kind == "ExternalInput":
            name = alloc.memorylocations[0].name
            if nc.partition_id_tensor is not None and name == nc.partition_id_tensor.name:
                continue
            want[name] = (tuple(alloc.tensor_shape), _mb.dt.np(alloc.dtype))
    for im in in_maps:
        for name, (shp, dty) in want.items():
            assert name in im, f"missing input {name}"
            a = im[name]
            assert tuple(a.shape) == shp, f"{name}: got {a.shape}, want {shp}"
            assert a.dtype == dty, f"{name}: got {a.dtype}, want {dty}"
        extra = set(im) - set(want)
        assert not extra, f"extra inputs: {extra}"
    return run_bass_kernel_spmd(nc, in_maps, core_ids=core_ids)


class Ctr:

    def __init__(self, handle):
        self.h = handle
        self.n = 0

    def inc(self, inst, k=1):
        inst.then_inc(self.h, k)
        self.n += k
        return self.n

    def dma(self, inst):
        return self.inc(inst, 16)

    def wait(self, eng, val=None):
        eng.wait_ge(self.h, self.n if val is None else val)

    def wait_relax(self, eng, j, back=1, first=2):
        eng.wait_ge(self.h, self.n if j < first else max(0, self.n - back))


_ALLOC = {"stack": None, "pfx": ""}


def sb(nc, name, p, f, dtype):
    name = _ALLOC["pfx"] + name
    if _ALLOC["stack"] is None:
        return nc.alloc_sbuf_tensor(name, [p, f], dtype)
    return _ALLOC["stack"].enter_context(nc.sbuf_tensor(name, [p, f], dtype))


def sb_tiles(nc, name, ntiles, p, f, dtype):
    return [sb(nc, f"{name}{i}", p, f, dtype) for i in range(ntiles)]


def psum(nc, name, p, f, dtype=F32):
    name = _ALLOC["pfx"] + name
    if _ALLOC["stack"] is None:
        return nc.alloc_psum_tensor(name, [p, f], dtype)
    return _ALLOC["stack"].enter_context(nc.psum_tensor(name, [p, f], dtype))


def ln_alloc(nc, tag, nt, Tloc, xb=None, xsq=None):
    return {
        "xb": xb if xb is not None else sb_tiles(nc, f"{tag}_lnxb", nt, 128, CH, BF16),
        "xsq": xsq if xsq is not None else sb_tiles(nc, f"{tag}_lnxsq", nt, 128, CH, BF16),
        "row_mu": sb(nc, f"{tag}_row_mu", 1, Tloc, F32),
        "row_ms": sb(nc, f"{tag}_row_ms", 1, Tloc, F32),
        "row_rs": sb(nc, f"{tag}_row_rs", 1, Tloc, F32),
        "row_d": sb(nc, f"{tag}_row_d", 1, Tloc, F32),
        "tmp": sb(nc, f"{tag}_lntmp", 128, CH, F32),
        "ps_sum": psum(nc, f"{tag}_ps_sum", 1, CH),
        "ps_sq": psum(nc, f"{tag}_ps_sq", 1, CH),
        "ps_mu": psum(nc, f"{tag}_ps_mu", 128, CH),
        "ps_rs": psum(nc, f"{tag}_ps_rs", 128, CH),
    }


def ln_plain_T(nc, block, ctrs, x_tiles, out_tiles, nt, Tloc, consts, bufs,
               w_cols=None, b_cols=None, out_dtype_note=None):
    t_ctr, s_ctr, v_ctr = ctrs["pe"], ctrs["sca"], ctrs["vec"]
    xb, xsq = bufs["xb"], bufs["xsq"]
    row_mu, row_ms, row_rs = bufs["row_mu"], bufs["row_ms"], bufs["row_rs"]
    ps_sum, ps_sq = bufs["ps_sum"], bufs["ps_sq"]
    ps_mu, ps_rs = bufs["ps_mu"], bufs["ps_rs"]
    c_starts = [i * CH for i in range(Tloc // CH)]
    if Tloc % CH:
        c_starts.append(Tloc - CH)

    for c0 in c_starts:
        w = CH
        cs = slice(c0, c0 + w)

        @block.scalar
        def _(s):
            v_ctr.wait(s)
            t_ctr.wait(s)
            for i in range(nt):
                s.copy(xb[i][:, 0:w], x_tiles[i][:, cs])
                ins = s.activation(out=xsq[i][:, 0:w], in_=x_tiles[i][:, cs], func=AF.Square)
            s_ctr.inc(ins)

        @block.tensor
        def _(t):
            s_ctr.wait(t)
            for i in range(nt):
                t.matmul(ps_sum[0:1, 0:w], consts["ones_col_bf"][:], xb[i][:, 0:w],
                         start=(i == 0), stop=(i == nt - 1))
            for i in range(nt):
                ins = t.matmul(ps_sq[0:1, 0:w], consts["ones_col_bf"][:], xsq[i][:, 0:w],
                               start=(i == 0), stop=(i == nt - 1))
            t_ctr.inc(ins)

        @block.scalar
        def _(s):
            t_ctr.wait(s)
            s.activation(out=row_mu[0:1, cs], in_=ps_sum[0:1, 0:w], func=AF.Copy,
                         scale=1.0 / (nt * 128.0))
            ins = s.activation(out=row_ms[0:1, cs], in_=ps_sq[0:1, 0:w], func=AF.Copy,
                               scale=1.0 / (nt * 128.0))
            s_ctr.inc(ins)

    @block.vector
    def _(v):
        s_ctr.wait(v)
        v.tensor_tensor(bufs["row_d"][:], row_mu[:], row_mu[:], ALU.mult)
        ins = v.tensor_sub(bufs["row_d"][:], row_ms[:], bufs["row_d"][:])
        v_ctr.inc(ins)

    @block.scalar
    def _(s):
        v_ctr.wait(s)
        ins = s.activation(out=row_rs[:], in_=bufs["row_d"][:], func=AF.Ln, bias=consts["eps11"][:])
        s_ctr.inc(ins)
        s_ctr.wait(s)
        ins = s.activation(out=row_rs[:], in_=row_rs[:], func=AF.Exp, scale=-0.5)
        s_ctr.inc(ins)

    for c0 in c_starts:
        w = CH
        cs = slice(c0, c0 + w)

        @block.tensor
        def _(t):
            s_ctr.wait(t)
            v_ctr.wait(t)
            t.matmul(ps_mu[:, 0:w], consts["ones_row_f32"][:], row_mu[0:1, cs], start=True, stop=True)
            ins = t.matmul(ps_rs[:, 0:w], consts["ones_row_f32"][:], row_rs[0:1, cs], start=True, stop=True)
            t_ctr.inc(ins)

        @block.vector
        def _(v):
            t_ctr.wait(v)
            for i in range(nt):
                v.tensor_sub(bufs["tmp"][:, 0:w], x_tiles[i][:, cs], ps_mu[:, 0:w])
                if w_cols is None:
                    ins = v.tensor_mul(out_tiles[i][:, cs], bufs["tmp"][:, 0:w], ps_rs[:, 0:w])
                else:
                    v.tensor_mul(bufs["tmp"][:, 0:w], bufs["tmp"][:, 0:w], ps_rs[:, 0:w])
                    ins = v.tensor_scalar(out=out_tiles[i][:, cs], in0=bufs["tmp"][:, 0:w],
                                          scalar1=w_cols[i], scalar2=b_cols[i],
                                          op0=ALU.mult, op1=ALU.add)
            v_ctr.inc(ins)


def make_consts(nc):
    import ml_dtypes
    ones_col_bf = nc.inline_tensor(np.ones((128, 1), dtype=ml_dtypes.bfloat16), "ones_col_bf")
    ones_row_f32 = nc.inline_tensor(np.ones((1, 128), dtype=np.float32), "ones_row_f32")
    eps11 = nc.inline_tensor(np.full((1, 1), EPS, dtype=np.float32), "eps11")
    ones_col_f32 = nc.inline_tensor(np.ones((128, 1), dtype=np.float32), "ones_col_f32")
    sel = np.zeros((16, 16 * 128), dtype=np.float32)
    for s_ in range(16):
        sel[s_, s_ * 128:(s_ + 1) * 128] = 1.0
    sel16 = nc.inline_tensor(sel, "sel16")
    return {"t_ones_col_bf": ones_col_bf, "t_ones_row_f32": ones_row_f32, "t_eps11": eps11,
            "t_ones_col_f32": ones_col_f32, "t_sel16": sel16}


def load_consts(nc, block, consts, ld, need_sel=True):
    cb = sb(nc, "c_ones_col_bf", 128, 1, BF16)
    rf = sb(nc, "c_ones_row_f32", 1, 128, F32)
    ep = sb(nc, "c_eps11", 1, 1, F32)
    cf = sb(nc, "c_ones_col_f32", 128, 1, F32)
    sl = sb(nc, "c_sel16", 16, 16 * 128, F32) if need_sel else None

    @block.sync
    def _(q):
        ld.dma(q.dma_start(out=cb[:], in_=consts["t_ones_col_bf"].ap()))
        ld.dma(q.dma_start(out=rf[:], in_=consts["t_ones_row_f32"].ap()))
        ld.dma(q.dma_start(out=ep[:], in_=consts["t_eps11"].ap()))
        ld.dma(q.dma_start(out=cf[:], in_=consts["t_ones_col_f32"].ap()))
        if need_sel:
            ld.dma(q.dma_start(out=sl[:], in_=consts["t_sel16"].ap()))
    consts["ones_col_bf"] = cb
    consts["ones_row_f32"] = rf
    consts["eps11"] = ep
    consts["ones_col_f32"] = cf
    consts["sel16"] = sl


def build_rwkv(F=None):
    standalone = F is None
    nc = bass.Bass("TRN2", target_bir_lowering=False, debug=False, num_devices=8) if standalone else F["nc"]
    w_d = {w: nc.dram_tensor(w, [D, D], F32, kind="ExternalInput") for w in ("wrT", "wkT", "wvT")}
    mur_d = nc.dram_tensor("mur", [D, 1], F32, kind="ExternalInput")
    muk_d = nc.dram_tensor("muk", [D, 1], F32, kind="ExternalInput")
    if standalone:
        xh_d = nc.dram_tensor("xh", [D, TK + 1], F32, kind="ExternalInput")
        o_d = nc.dram_tensor("x3T", [D, TK], F32, kind="ExternalOutput")
    else:
        o_d = F["io"]["x3T"]
        pm_d = nc.dram_tensor("pmask", [128, 1], F32, kind="ExternalInput")

    xh = sb_tiles(nc, "xh", DT, 128, TK + 1, F32)
    wr = sb_tiles(nc, "wr", DT, 128, D, BF16)
    wk = sb_tiles(nc, "wk", DT, 128, D, BF16)
    wv = sb_tiles(nc, "wv", DT, 128, D, BF16)
    mur = sb_tiles(nc, "mur", DT, 128, 1, F32)
    muk = sb_tiles(nc, "muk", DT, 128, 1, F32)
    mur1 = sb_tiles(nc, "mur1", DT, 128, 1, F32)
    muk1 = sb_tiles(nc, "muk1", DT, 128, 1, F32)
    xr = sb_tiles(nc, "xr", DT, 128, CH, BF16)
    xk = sb_tiles(nc, "xk", DT, 128, CH, BF16)
    rg = sb_tiles(nc, "rg", DT, 128, CH, F32)
    kact = sb_tiles(nc, "kact", DT, 128, CH, BF16)
    x3p = sb_tiles(nc, "x3p", DT, 128, CH, F32)
    lnb = ln_alloc(nc, "r", DT, CH)
    ktmp = sb(nc, "ktmp", 128, CH, F32)

    psA = psum(nc, "psA", 128, CH)
    psB = psum(nc, "psB", 128, CH)

    import contextlib
    es = contextlib.ExitStack()
    block, consts, ld, wld, ctrs, sto = _f_ctx(
        F, nc, es, ["ld", "pe", "sca", "vec", "sto"], need_sel=False)
    with es:
        if standalone:
            @block.sync
            def _(q):
                for i in range(DT):
                    ld.dma(q.dma_start(out=xh[i][:], in_=xh_d.ap()[i * 128:(i + 1) * 128, :]))
        else:
            pmc = sb(nc, "pmc", 128, 1, F32)

            @block.sync
            def _(q):
                ld.dma(q.dma_start(out=pmc[:], in_=pm_d.ap()))
                for i in range(DT):
                    r = slice(i * 128, (i + 1) * 128)
                    ld.dma(q.dma_start(out=xh[i][:, 1:TK + 1], in_=F["io"]["x2T"].ap()[r, :]))

                def _hl(sl):
                    for i in range(DT):
                        r = slice(i * 128, (i + 1) * 128)
                        ld.dma(q.dma_start(out=xh[i][:, 0:1], in_=F["io"]["x2c"].ap()[r, sl],
                                           allow_slow_non_contiguous=True))
                par_branch(q, lambda: _hl(slice(1, 2)), lambda: _hl(slice(0, 1)), bal=(ld,))

            @block.vector
            def _(v):
                ld.wait(v)
                for i in range(DT):
                    ins = v.tensor_scalar_mul(xh[i][:, 0:1], xh[i][:, 0:1], pmc[:])
                ctrs["vec"].inc(ins)

        @block.sync
        def _(q):
            for i in range(DT):
                ld.dma(q.dma_start(out=mur[i][:], in_=mur_d.ap()[i * 128:(i + 1) * 128, :]))
                ld.dma(q.dma_start(out=muk[i][:], in_=muk_d.ap()[i * 128:(i + 1) * 128, :]))

        @block.gpsimd
        def _(g):
            for i in range(DT):
                ld.dma(g.dma_start(out=wr[i][:], in_=w_d["wrT"].ap()[i * 128:(i + 1) * 128, :]))
                ld.dma(g.dma_start(out=wk[i][:], in_=w_d["wkT"].ap()[i * 128:(i + 1) * 128, :]))
                ld.dma(g.dma_start(out=wv[i][:], in_=w_d["wvT"].ap()[i * 128:(i + 1) * 128, :]))

        @block.vector
        def _(v):
            ld.wait(v)
            for i in range(DT):
                v.tensor_scalar(out=mur1[i][:], in0=mur[i][:], scalar1=-1.0, scalar2=1.0,
                                op0=ALU.mult, op1=ALU.add)
                ins = v.tensor_scalar(out=muk1[i][:], in0=muk[i][:], scalar1=-1.0, scalar2=1.0,
                                      op0=ALU.mult, op1=ALU.add)
            ctrs["vec"].inc(ins)

        for c in range(NCH):
            cur = slice(1 + c * CH, 1 + (c + 1) * CH)
            prv = slice(c * CH, (c + 1) * CH)

            @block.vector
            def _(v):
                ctrs["pe"].wait(v)
                for i in range(DT):
                    v.tensor_scalar_mul(lnb["tmp"][:, 0:CH], xh[i][:, cur], mur[i][:])
                    v.scalar_tensor_tensor(out=xr[i][:], in0=xh[i][:, prv], scalar=mur1[i][:],
                                           in1=lnb["tmp"][:, 0:CH], op0=ALU.mult, op1=ALU.add)
                    v.tensor_scalar_mul(lnb["tmp"][:, 0:CH], xh[i][:, cur], muk[i][:])
                    ins = v.scalar_tensor_tensor(out=xk[i][:], in0=xh[i][:, prv], scalar=muk1[i][:],
                                                 in1=lnb["tmp"][:, 0:CH], op0=ALU.mult, op1=ALU.add)
                ctrs["vec"].inc(ins)

            for j in range(DT):
                ps = psA if j % 2 == 0 else psB
                jc = slice(j * 128, (j + 1) * 128)

                @block.tensor
                def _(t):
                    ctrs["vec"].wait(t)
                    ctrs["sca"].wait_relax(t, j)
                    for i in range(DT):
                        ins = t.matmul(ps[:], wr[i][:, jc], xr[i][:], start=(i == 0), stop=(i == DT - 1))
                    ctrs["pe"].inc(ins)

                @block.scalar
                def _(s):
                    ctrs["pe"].wait(s)
                    ins = s.activation(out=rg[j][:], in_=ps[:], func=AF.Sigmoid)
                    ctrs["sca"].inc(ins)
            for j in range(DT):
                ps = psA if j % 2 == 0 else psB
                jc = slice(j * 128, (j + 1) * 128)

                @block.tensor
                def _(t):
                    ctrs["sca"].wait_relax(t, j, back=2)
                    for i in range(DT):
                        ins = t.matmul(ps[:], wk[i][:, jc], xk[i][:], start=(i == 0), stop=(i == DT - 1))
                    ctrs["pe"].inc(ins)

                @block.scalar
                def _(s):
                    ctrs["pe"].wait(s)
                    ins = s.activation(out=ktmp[:], in_=ps[:], func=AF.Relu)
                    ctrs["sca"].inc(ins)
                    ctrs["sca"].wait(s)
                    ins = s.activation(out=kact[j][:], in_=ktmp[:], func=AF.Square)
                    ctrs["sca"].inc(ins)
            for j in range(DT):
                ps = psA if j % 2 == 0 else psB
                jc = slice(j * 128, (j + 1) * 128)

                @block.tensor
                def _(t):
                    ctrs["sca"].wait(t)
                    ctrs["vec"].wait_relax(t, j)
                    for i in range(DT):
                        ins = t.matmul(ps[:], wv[i][:, jc], kact[i][:], start=(i == 0), stop=(i == DT - 1))
                    ctrs["pe"].inc(ins)

                @block.vector
                def _(v):
                    ctrs["pe"].wait(v)
                    sto.wait(v)
                    v.tensor_mul(lnb["tmp"][:, 0:CH], rg[j][:], ps[:])
                    ins = v.tensor_add(x3p[j][:], xh[j][:, cur], lnb["tmp"][:, 0:CH])
                    ctrs["vec"].inc(ins)

            ln_plain_T(nc, block, ctrs, x3p, x3p, DT, CH, consts, lnb)

            @block.sync
            def _(q):
                ctrs["vec"].wait(q)
                for i in range(DT):
                    sto.dma(q.dma_start(out=o_d.ap()[i * 128:(i + 1) * 128, c * CH:(c + 1) * CH],
                                        in_=x3p[i][:]))

            if not standalone:
                @block.gpsimd
                def _(g):
                    ctrs["vec"].wait(g)

                    def _sx(off):
                        for i in range(DT):
                            sto.dma(g.dma_start(
                                out=F["io"]["sh_x3"].ap()[i * 128:(i + 1) * 128,
                                                          off + c * CH:off + (c + 1) * CH],
                                in_=x3p[i][:]))
                    par_branch(g, lambda: _sx(0), lambda: _sx(TK), bal=(sto,))

        @block.sync
        def _(q):
            sto.wait(q)
    return nc


def _shard_tok_T(a_bld, core, halo_left=0, halo_right=0):
    b, h = core // 2, core % 2
    s = h * TK
    aT = np.ascontiguousarray(np.transpose(a_bld[b], (1, 0)))
    if halo_left == 0 and halo_right == 0:
        return np.ascontiguousarray(aT[:, s:s + TK])
    out = np.zeros((aT.shape[0], halo_left + TK + halo_right), np.float32)
    lo = max(0, s - halo_left)
    hi = min(L, s + TK + halo_right)
    out[:, halo_left - (s - lo):halo_left + (hi - s)] = aT[:, lo:hi]
    return out


def run_rwkv(x2, inputs):
    nc = build_rwkv()
    wrT = np.ascontiguousarray(inputs["L0_ffn_W_r"].T)
    wkT = np.ascontiguousarray(inputs["L0_ffn_W_k"].T)
    wvT = np.ascontiguousarray(inputs["L0_ffn_W_v"].T)
    mur = np.ascontiguousarray(inputs["L0_ffn_mu_r"].reshape(D, 1))
    muk = np.ascontiguousarray(inputs["L0_ffn_mu_k"].reshape(D, 1))
    in_maps = []
    for c in range(NCORES):
        in_maps.append({
            "xh": _shard_tok_T(x2, c, halo_left=1),
            "wrT": wrT, "wkT": wkT, "wvT": wvT, "mur": mur, "muk": muk,
        })
    res = run_spmd(nc, in_maps, core_ids=list(range(NCORES)))
    x3 = np.empty((B, L, D), np.float32)
    for c in range(NCORES):
        b, h = c // 2, c % 2
        x3[b, h * TK:(h + 1) * TK] = res.results[c]["x3T"].T
    return x3


_PROGS = {}
_PAR_REGS = {}


def _parity_reg(e):
    key = (id(e.bass), str(e.engine))
    if key not in _PAR_REGS:
        pid = e.partition_id()
        preg = e.alloc_register("parreg")
        e.reg_alu(preg, e.to_reg(pid), 2, ALU.mod)
        _PAR_REGS[key] = preg
    return _PAR_REGS[key]


def par_branch(e, emit0, emit1, bal=()):
    preg = _parity_reg(e)
    marks = [c.n for c in bal]
    with e.If_eq(preg, 0):
        emit0()
    after = [c.n for c in bal]
    for c, m in zip(bal, marks):
        c.n = m
    with e.Else():
        emit1()
    for c, a in zip(bal, after):
        assert c.n == a, "unbalanced branch sem counts"


def _f_ctx(F, nc, es, semnames, need_sel, selname=None):
    if F is None:
        consts = make_consts(nc)
        hs = {n: es.enter_context(nc.semaphore(n)) for n in semnames}
        block = es.enter_context(nc.Block())
        ld = Ctr(hs["ld"])
        wld = Ctr(hs["wld"]) if "wld" in hs else None
        ctrs = {"pe": Ctr(hs["pe"]), "sca": Ctr(hs["sca"]), "vec": Ctr(hs["vec"])}
        sto = Ctr(hs["sto"])
        load_consts(nc, block, consts, ld, need_sel=need_sel)
        return block, consts, ld, wld, ctrs, sto
    return F["block"], F["consts"], F["ld"], F["wld"], F["ctrs"], F["sto"]


def kernel(**inputs):
    if FUSED:
        return kernel_fused(inputs)
    x = np.ascontiguousarray(inputs["L0_attn_x"], dtype=np.float32)
    pre = run_mamba_pre(x, inputs)
    ys = run_scan(pre, inputs)
    x2 = run_mamba_post(x, pre, ys, inputs)
    x3 = run_rwkv(x2, inputs)
    x4 = run_attn(x3, inputs)
    out = run_moe(x4, inputs)
    return out


def ln_stream_T(nc, block, ctrs, ld, x_dram, nt, Tloc, out_tiles, consts, bufs,
                xch, w_cols=None, b_cols=None):
    t_ctr, s_ctr, v_ctr = ctrs["pe"], ctrs["sca"], ctrs["vec"]
    xb, xsq = bufs["xb"], bufs["xsq"]
    row_mu, row_ms, row_rs = bufs["row_mu"], bufs["row_ms"], bufs["row_rs"]
    ps_sum, ps_sq = bufs["ps_sum"], bufs["ps_sq"]
    ps_mu, ps_rs = bufs["ps_mu"], bufs["ps_rs"]
    c_starts = [i * CH for i in range(Tloc // CH)]
    if Tloc % CH:
        c_starts.append(Tloc - CH)

    for c0 in c_starts:
        w = CH
        cs = slice(c0, c0 + w)

        @block.sync
        def _(q):
            s_ctr.wait(q)
            for i in range(nt):
                ld.dma(q.dma_start(out=xch[i][:, 0:w], in_=x_dram.ap()[i * 128:(i + 1) * 128, cs]))

        @block.scalar
        def _(s):
            ld.wait(s)
            t_ctr.wait(s)
            for i in range(nt):
                s.copy(xb[i][:, 0:w], xch[i][:, 0:w])
                ins = s.activation(out=xsq[i][:, 0:w], in_=xch[i][:, 0:w], func=AF.Square)
            s_ctr.inc(ins)

        @block.tensor
        def _(t):
            s_ctr.wait(t)
            for i in range(nt):
                t.matmul(ps_sum[0:1, 0:w], consts["ones_col_bf"][:], xb[i][:, 0:w],
                         start=(i == 0), stop=(i == nt - 1))
            for i in range(nt):
                ins = t.matmul(ps_sq[0:1, 0:w], consts["ones_col_bf"][:], xsq[i][:, 0:w],
                               start=(i == 0), stop=(i == nt - 1))
            t_ctr.inc(ins)

        @block.scalar
        def _(s):
            t_ctr.wait(s)
            s.activation(out=row_mu[0:1, cs], in_=ps_sum[0:1, 0:w], func=AF.Copy,
                         scale=1.0 / (nt * 128.0))
            ins = s.activation(out=row_ms[0:1, cs], in_=ps_sq[0:1, 0:w], func=AF.Copy,
                               scale=1.0 / (nt * 128.0))
            s_ctr.inc(ins)

    @block.vector
    def _(v):
        s_ctr.wait(v)
        v.tensor_tensor(bufs["row_d"][:], row_mu[:], row_mu[:], ALU.mult)
        ins = v.tensor_sub(bufs["row_d"][:], row_ms[:], bufs["row_d"][:])
        v_ctr.inc(ins)

    @block.scalar
    def _(s):
        v_ctr.wait(s)
        ins = s.activation(out=row_rs[:], in_=bufs["row_d"][:], func=AF.Ln, bias=consts["eps11"][:])
        s_ctr.inc(ins)
        s_ctr.wait(s)
        ins = s.activation(out=row_rs[:], in_=row_rs[:], func=AF.Exp, scale=-0.5)
        s_ctr.inc(ins)

    for c0 in c_starts:
        w = CH
        cs = slice(c0, c0 + w)

        @block.sync
        def _(q):
            v_ctr.wait(q)
            for i in range(nt):
                ld.dma(q.dma_start(out=xch[i][:, 0:w], in_=x_dram.ap()[i * 128:(i + 1) * 128, cs]))

        @block.tensor
        def _(t):
            s_ctr.wait(t)
            v_ctr.wait(t)
            t.matmul(ps_mu[:, 0:w], consts["ones_row_f32"][:], row_mu[0:1, cs], start=True, stop=True)
            ins = t.matmul(ps_rs[:, 0:w], consts["ones_row_f32"][:], row_rs[0:1, cs], start=True, stop=True)
            t_ctr.inc(ins)

        @block.vector
        def _(v):
            t_ctr.wait(v)
            ld.wait(v)
            for i in range(nt):
                v.tensor_sub(bufs["tmp"][:, 0:w], xch[i][:, 0:w], ps_mu[:, 0:w])
                if w_cols is None:
                    ins = v.tensor_mul(out_tiles[i][:, cs], bufs["tmp"][:, 0:w], ps_rs[:, 0:w])
                else:
                    v.tensor_mul(bufs["tmp"][:, 0:w], bufs["tmp"][:, 0:w], ps_rs[:, 0:w])
                    ins = v.tensor_scalar(out=out_tiles[i][:, cs], in0=bufs["tmp"][:, 0:w],
                                          scalar1=w_cols[i], scalar2=b_cols[i],
                                          op0=ALU.mult, op1=ALU.add)
            v_ctr.inc(ins)


def build_mamba_pre(F=None):
    standalone = F is None
    nc = bass.Bass("TRN2", target_bir_lowering=False, debug=False, num_devices=8) if standalone else F["nc"]
    TE = TK + 6
    DIT = DI // 128
    xh_d = nc.dram_tensor("xhT", [D, TE], F32, kind="ExternalInput")
    lnw_d = nc.dram_tensor("lnw", [D, 1], F32, kind="ExternalInput")
    lnbb_d = nc.dram_tensor("lnb", [D, 1], F32, kind="ExternalInput")
    lm_d = nc.dram_tensor("lmask", [128, 1], F32, kind="ExternalInput")
    rm_d = nc.dram_tensor("rmask", [128, 1], F32, kind="ExternalInput")
    dirs = ("f", "b")
    din = {}
    dout = {}
    for d in dirs:
        din[d] = {
            "inWT": nc.dram_tensor(f"inWT_{d}", [D, 2 * DI], F32, kind="ExternalInput"),
            "convW": nc.dram_tensor(f"convW_{d}", [DI, 4], F32, kind="ExternalInput"),
            "convb": nc.dram_tensor(f"convb_{d}", [DI, 1], F32, kind="ExternalInput"),
            "xpdtT": nc.dram_tensor(f"xpdtT_{d}", [DI, RK], F32, kind="ExternalInput"),
            "xpBT": nc.dram_tensor(f"xpBT_{d}", [DI, DS], F32, kind="ExternalInput"),
            "xpCT": nc.dram_tensor(f"xpCT_{d}", [DI, DS], F32, kind="ExternalInput"),
            "dtWT": nc.dram_tensor(f"dtWT_{d}", [RK, DI], F32, kind="ExternalInput"),
            "dtb": nc.dram_tensor(f"dtb_{d}", [DI, 1], F32, kind="ExternalInput"),
        }
        if standalone:
            dout[d] = {
                "xs": nc.dram_tensor(f"xs_{d}", [DI, TK], F32, kind="ExternalOutput"),
                "zs": nc.dram_tensor(f"zs_{d}", [DI, TK], F32, kind="ExternalOutput"),
                "dt": nc.dram_tensor(f"dt_{d}", [DI, TK], F32, kind="ExternalOutput"),
                "Bm": nc.dram_tensor(f"Bm_{d}", [DS, TK], F32, kind="ExternalOutput"),
                "Cm": nc.dram_tensor(f"Cm_{d}", [DS, TK], F32, kind="ExternalOutput"),
            }
        else:
            dout[d] = {
                "xs": F["io"][f"sh_xs_{d}"], "dt": F["io"][f"sh_dt_{d}"],
                "Bm": F["io"][f"sh_Bm_{d}"], "Cm": F["io"][f"sh_Cm_{d}"],
                "zs": F["io"][f"zs_{d}"],
            }

    normed = sb_tiles(nc, "normed", DT, 128, TE, BF16)
    xch = sb_tiles(nc, "xch", DT, 128, CH, F32)
    wz = sb_tiles(nc, "wz", DT, 128, DI, BF16)
    xspre = sb_tiles(nc, "xspre", DIT, 128, CH + 6, BF16)
    xs_ch = sb_tiles(nc, "xs_ch", DIT, 128, CH, BF16)
    lncols = sb_tiles(nc, "lncol", DT, 128, 1, F32)
    lnbcols = sb_tiles(nc, "lnbcol", DT, 128, 1, F32)
    lmask = sb(nc, "lmask_s", 128, 1, F32)
    rmask = sb(nc, "rmask_s", 128, 1, F32)
    convw = {d: sb_tiles(nc, f"convw_{d}", DIT, 128, 4, F32) for d in dirs}
    convb = {d: sb_tiles(nc, f"convb_{d}", DIT, 128, 1, F32) for d in dirs}
    dtb = {d: sb_tiles(nc, f"dtb_{d}", DIT, 128, 1, F32) for d in dirs}
    xpdt = {d: sb_tiles(nc, f"xpdt_{d}", DIT, 128, RK, BF16) for d in dirs}
    xpB = {d: sb_tiles(nc, f"xpB_{d}", DIT, 128, DS, BF16) for d in dirs}
    xpC = {d: sb_tiles(nc, f"xpC_{d}", DIT, 128, DS, BF16) for d in dirs}
    dtw = {d: sb(nc, f"dtw_{d}", RK, DI, BF16) for d in dirs}
    zstage = [sb(nc, f"zstage{k}", 128, CH, F32) for k in range(2)]
    xsstage = [sb(nc, f"xsstage{k}", 128, CH, F32) for k in range(2)]
    acc = [sb(nc, f"acc{k}", 128, CH, F32) for k in range(2)]
    dtraw = sb(nc, "dtraw", RK, CH, BF16)
    bstage = sb(nc, "bstage", DS, CH, F32)
    cstage = sb(nc, "cstage", DS, CH, F32)
    sp1 = sb(nc, "sp1", 128, CH, F32)
    sp2 = sb(nc, "sp2", 128, CH, F32)
    sp3 = sb(nc, "sp3", 128, CH, F32)
    lnbufs = ln_alloc(nc, "m", DT, TE)

    psA = psum(nc, "psA", 128, CH)
    psB = psum(nc, "psB", 128, CH)
    psH = psum(nc, "psH", 128, 8)
    psDT = psum(nc, "psDT", RK, CH)

    import contextlib
    es = contextlib.ExitStack()
    block, consts, ld, wld, ctrs, sto = _f_ctx(
        F, nc, es, ["ld", "wld", "pe", "sca", "vec", "sto"], need_sel=False)
    with es:

        @block.sync
        def _(q):
            for i in range(DT):
                ld.dma(q.dma_start(out=lncols[i][:], in_=lnw_d.ap()[i * 128:(i + 1) * 128, :]))
                ld.dma(q.dma_start(out=lnbcols[i][:], in_=lnbb_d.ap()[i * 128:(i + 1) * 128, :]))
            ld.dma(q.dma_start(out=lmask[:], in_=lm_d.ap()))
            ld.dma(q.dma_start(out=rmask[:], in_=rm_d.ap()))

        @block.gpsimd
        def _(g):
            for d in dirs:
                for i in range(DIT):
                    r = slice(i * 128, (i + 1) * 128)
                    ld.dma(g.dma_start(out=convw[d][i][:], in_=din[d]["convW"].ap()[r, :]))
                    ld.dma(g.dma_start(out=convb[d][i][:], in_=din[d]["convb"].ap()[r, :]))
                    ld.dma(g.dma_start(out=dtb[d][i][:], in_=din[d]["dtb"].ap()[r, :]))
                    ld.dma(g.dma_start(out=xpdt[d][i][:], in_=din[d]["xpdtT"].ap()[r, :]))
                    ld.dma(g.dma_start(out=xpB[d][i][:], in_=din[d]["xpBT"].ap()[r, :]))
                    ld.dma(g.dma_start(out=xpC[d][i][:], in_=din[d]["xpCT"].ap()[r, :]))
                ld.dma(g.dma_start(out=dtw[d][:], in_=din[d]["dtWT"].ap()))

        ln_stream_T(nc, block, ctrs, ld, xh_d, DT, TE, normed, consts, lnbufs, xch,
                    w_cols=[lncols[i][:] for i in range(DT)],
                    b_cols=[lnbcols[i][:] for i in range(DT)])

        @block.vector
        def _(v):
            ld.wait(v)
            for i in range(DT):
                v.tensor_scalar_mul(normed[i][:, 0:3], normed[i][:, 0:3], lmask[:])
                ins = v.tensor_scalar_mul(normed[i][:, TE - 3:TE], normed[i][:, TE - 3:TE], rmask[:])
            ctrs["vec"].inc(ins)

        for d in dirs:
            @block.gpsimd
            def _(g):
                ctrs["pe"].wait(g)
                for i in range(DT):
                    wld.dma(g.dma_start(out=wz[i][:], in_=din[d]["inWT"].ap()[i * 128:(i + 1) * 128, DI:2 * DI]))

            for c in range(NCH):
                tok = slice(3 + c * CH, 3 + (c + 1) * CH)
                for j in range(DIT):
                    ps = psA if j % 2 == 0 else psB
                    jc = slice(j * 128, (j + 1) * 128)

                    @block.tensor
                    def _(t):
                        wld.wait(t)
                        ctrs["vec"].wait(t)
                        ctrs["sca"].wait_relax(t, c * DIT + j)
                        for i in range(DT):
                            ins = t.matmul(ps[:], wz[i][:, jc], normed[i][:, tok],
                                           start=(i == 0), stop=(i == DT - 1))
                        ctrs["pe"].inc(ins)

                    @block.scalar
                    def _(s):
                        ctrs["pe"].wait(s)
                        sto.wait(s, sto.n - 16 if sto.n >= 16 else 0)
                        zst = zstage[j % 2]
                        ins = s.activation(out=zst[:], in_=ps[:], func=AF.Silu)
                        ctrs["sca"].inc(ins)

                    @block.sync
                    def _(q):
                        ctrs["sca"].wait(q)
                        sto.dma(q.dma_start(out=dout[d]["zs"].ap()[jc, c * CH:(c + 1) * CH],
                                            in_=zstage[j % 2][:]))

            @block.gpsimd
            def _(g):
                ctrs["pe"].wait(g)
                for i in range(DT):
                    wld.dma(g.dma_start(out=wz[i][:], in_=din[d]["inWT"].ap()[i * 128:(i + 1) * 128, 0:DI]))

            for c in range(NCH):
                ext = slice(c * CH, c * CH + CH)
                ext2 = slice(c * CH + CH, c * CH + CH + 6)

                for j in range(DIT):
                    ps = psA if j % 2 == 0 else psB
                    jc = slice(j * 128, (j + 1) * 128)

                    @block.tensor
                    def _(t):
                        wld.wait(t)
                        ctrs["sca"].wait(t)
                        for i in range(DT):
                            t.matmul(ps[:], wz[i][:, jc], normed[i][:, ext],
                                     start=(i == 0), stop=(i == DT - 1))
                        for i in range(DT):
                            ins = t.matmul(psH[:, 0:6], wz[i][:, jc], normed[i][:, ext2],
                                           start=(i == 0), stop=(i == DT - 1))
                        ctrs["pe"].inc(ins)

                    @block.scalar
                    def _(s):
                        ctrs["pe"].wait(s)
                        ctrs["vec"].wait(s)
                        s.copy(xspre[j][:, 0:CH], ps[:])
                        ins = s.copy(xspre[j][:, CH:CH + 6], psH[:, 0:6])
                        ctrs["sca"].inc(ins)

                for j in range(DIT):
                    a0, a1 = acc[0], acc[1]

                    @block.vector
                    def _(v):
                        ctrs["sca"].wait(v)
                        shifts = (0, 1, 2, 3) if d == "f" else (6, 5, 4, 3)
                        v.tensor_scalar_mul(a0[:], xspre[j][:, shifts[0]:shifts[0] + CH],
                                            convw[d][j][:, 0:1])
                        for tap in (1, 2, 3):
                            sh = shifts[tap]
                            v.tensor_scalar_mul(a1[:], xspre[j][:, sh:sh + CH], convw[d][j][:, tap:tap + 1])
                            ins = v.tensor_add(a0[:], a0[:], a1[:])
                        ctrs["vec"].inc(ins)

                    @block.scalar
                    def _(s):
                        ctrs["vec"].wait(s)
                        ctrs["pe"].wait(s)
                        sto.wait(s, max(0, sto.n - 16))
                        xst = xsstage[j % 2]
                        s.activation(out=xst[:], in_=a0[:], func=AF.Silu, bias=convb[d][j][:])
                        ins = s.activation(out=xs_ch[j][:], in_=a0[:], func=AF.Silu, bias=convb[d][j][:])
                        ctrs["sca"].inc(ins)

                    @block.sync
                    def _(q):
                        ctrs["sca"].wait(q)
                        if standalone:
                            sto.dma(q.dma_start(out=dout[d]["xs"].ap()[j * 128:(j + 1) * 128, c * CH:(c + 1) * CH],
                                                in_=xsstage[j % 2][:]))
                        else:
                            par_branch(q,
                                lambda: sto.dma(q.dma_start(out=dout[d]["xs"].ap()[j * 128:(j + 1) * 128, c * CH:(c + 1) * CH], in_=xsstage[j % 2][:])),
                                lambda: sto.dma(q.dma_start(out=dout[d]["xs"].ap()[j * 128:(j + 1) * 128, TK + c * CH:TK + (c + 1) * CH], in_=xsstage[j % 2][:])),
                                bal=(sto,))

                @block.tensor
                def _(t):
                    ctrs["sca"].wait(t)
                    for i in range(DIT):
                        ins = t.matmul(psDT[:], xpdt[d][i][:], xs_ch[i][:],
                                       start=(i == 0), stop=(i == DIT - 1))
                    ctrs["pe"].inc(ins)

                @block.scalar
                def _(s):
                    ctrs["pe"].wait(s)
                    ins = s.copy(dtraw[:], psDT[:])
                    ctrs["sca"].inc(ins)

                @block.tensor
                def _(t):
                    ctrs["sca"].wait(t)
                    for i in range(DIT):
                        ins = t.matmul(psDT[0:DS, :], xpB[d][i][:], xs_ch[i][:],
                                       start=(i == 0), stop=(i == DIT - 1))
                    ctrs["pe"].inc(ins)

                @block.scalar
                def _(s):
                    ctrs["pe"].wait(s)
                    sto.wait(s, max(0, sto.n - 32))
                    ins = s.copy(bstage[:], psDT[0:DS, :])
                    ctrs["sca"].inc(ins)

                @block.sync
                def _(q):
                    ctrs["sca"].wait(q)
                    if standalone:
                        sto.dma(q.dma_start(out=dout[d]["Bm"].ap()[:, c * CH:(c + 1) * CH], in_=bstage[:]))
                    else:
                        par_branch(q,
                            lambda: sto.dma(q.dma_start(out=dout[d]["Bm"].ap()[:, c * CH:(c + 1) * CH], in_=bstage[:])),
                            lambda: sto.dma(q.dma_start(out=dout[d]["Bm"].ap()[:, TK + c * CH:TK + (c + 1) * CH], in_=bstage[:])),
                            bal=(sto,))

                @block.tensor
                def _(t):
                    ctrs["sca"].wait(t)
                    for i in range(DIT):
                        ins = t.matmul(psDT[0:DS, :], xpC[d][i][:], xs_ch[i][:],
                                       start=(i == 0), stop=(i == DIT - 1))
                    ctrs["pe"].inc(ins)

                @block.scalar
                def _(s):
                    ctrs["pe"].wait(s)
                    sto.wait(s, max(0, sto.n - 32))
                    ins = s.copy(cstage[:], psDT[0:DS, :])
                    ctrs["sca"].inc(ins)

                @block.sync
                def _(q):
                    ctrs["sca"].wait(q)
                    if standalone:
                        sto.dma(q.dma_start(out=dout[d]["Cm"].ap()[:, c * CH:(c + 1) * CH], in_=cstage[:]))
                    else:
                        par_branch(q,
                            lambda: sto.dma(q.dma_start(out=dout[d]["Cm"].ap()[:, c * CH:(c + 1) * CH], in_=cstage[:])),
                            lambda: sto.dma(q.dma_start(out=dout[d]["Cm"].ap()[:, TK + c * CH:TK + (c + 1) * CH], in_=cstage[:])),
                            bal=(sto,))

                for j in range(DIT):
                    ps = psA if j % 2 == 0 else psB
                    jc = slice(j * 128, (j + 1) * 128)

                    @block.tensor
                    def _(t):
                        ctrs["sca"].wait(t)
                        ctrs["vec"].wait(t)
                        ins = t.matmul(ps[:], dtw[d][0:RK, jc], dtraw[:], start=True, stop=True)
                        ctrs["pe"].inc(ins)

                    @block.vector
                    def _(v):
                        ctrs["pe"].wait(v)
                        sto.wait(v)
                        ins = v.tensor_scalar_add(sp1[:], ps[:], dtb[d][j][:])
                        ctrs["vec"].inc(ins)

                    @block.scalar
                    def _(s):
                        ctrs["vec"].wait(s)
                        ins = s.activation(out=sp2[:], in_=sp1[:], func=AF.Abs)
                        ctrs["sca"].inc(ins)
                        ctrs["sca"].wait(s)
                        ins = s.activation(out=sp3[:], in_=sp2[:], func=AF.Exp, scale=-1.0)
                        ctrs["sca"].inc(ins)
                        ctrs["sca"].wait(s)
                        ins = s.activation(out=sp2[:], in_=sp3[:], func=AF.Ln, bias=consts["ones_col_f32"][:])
                        ctrs["sca"].inc(ins)

                    @block.vector
                    def _(v):
                        ctrs["sca"].wait(v)
                        v.tensor_scalar_max(sp3[:], sp1[:], 0.0)
                        ins = v.tensor_add(sp1[:], sp3[:], sp2[:])
                        ctrs["vec"].inc(ins)

                    @block.sync
                    def _(q):
                        ctrs["vec"].wait(q)
                        if standalone:
                            sto.dma(q.dma_start(out=dout[d]["dt"].ap()[jc, c * CH:(c + 1) * CH], in_=sp1[:]))
                        else:
                            par_branch(q,
                                lambda: sto.dma(q.dma_start(out=dout[d]["dt"].ap()[jc, c * CH:(c + 1) * CH], in_=sp1[:])),
                                lambda: sto.dma(q.dma_start(out=dout[d]["dt"].ap()[jc, TK + c * CH:TK + (c + 1) * CH], in_=sp1[:])),
                                bal=(sto,))

        @block.sync
        def _(q):
            sto.wait(q)
    return nc


def run_mamba_pre(x, inputs):
    nc = build_mamba_pre()
    base = {}
    for d, pref in (("f", "L0_attn_f_"), ("b", "L0_attn_b_")):
        base[f"inWT_{d}"] = np.ascontiguousarray(inputs[pref + "inW"].T)
        base[f"convW_{d}"] = np.ascontiguousarray(inputs[pref + "convW"].reshape(DI, 4))
        base[f"convb_{d}"] = np.ascontiguousarray(inputs[pref + "convb"].reshape(DI, 1))
        xp = inputs[pref + "xprojW"]
        base[f"xpdtT_{d}"] = np.ascontiguousarray(xp[0:RK].T)
        base[f"xpBT_{d}"] = np.ascontiguousarray(xp[RK:RK + DS].T)
        base[f"xpCT_{d}"] = np.ascontiguousarray(xp[RK + DS:RK + 2 * DS].T)
        base[f"dtWT_{d}"] = np.ascontiguousarray(inputs[pref + "dtW"].T)
        base[f"dtb_{d}"] = np.ascontiguousarray(inputs[pref + "dtb"].reshape(DI, 1))
    base["lnw"] = np.ascontiguousarray(inputs["L0_attn_ln_w"].reshape(D, 1))
    base["lnb"] = np.ascontiguousarray(inputs["L0_attn_ln_b"].reshape(D, 1))
    import ml_dtypes
    in_maps = []
    for c in range(NCORES):
        m = dict(base)
        b, h = c // 2, c % 2
        m["xhT"] = _shard_tok_T(x, c, halo_left=3, halo_right=3)
        m["lmask"] = np.full((128, 1), 0.0 if h == 0 else 1.0, np.float32)
        m["rmask"] = np.full((128, 1), 0.0 if h == 1 else 1.0, np.float32)
        in_maps.append(m)
    res = run_spmd(nc, in_maps, core_ids=list(range(NCORES)))
    out = {}
    for name, rows in (("xs", DI), ("zs", DI), ("dt", DI), ("Bm", DS), ("Cm", DS)):
        for d in ("f", "b"):
            a = np.empty((B, L, rows), np.float32)
            for c in range(NCORES):
                b_, h = c // 2, c % 2
                a[b_, h * TK:(h + 1) * TK] = res.results[c][f"{name}_{d}"].T
            out[f"{name}_{d}"] = a
    return out


SCH = 256
SW = DS * (SCH + 1)


def build_scan():
    nc = bass.Bass("TRN2", target_bir_lowering=False, debug=False, num_devices=8)
    DIT = DI // 128
    NSC = L // SCH
    dt_d = nc.dram_tensor("dt", [DI, L], F32, kind="ExternalInput")
    xs_d = nc.dram_tensor("xs", [DI, L], F32, kind="ExternalInput")
    bm_d = nc.dram_tensor("bm", [DS, L], F32, kind="ExternalInput")
    cm_d = nc.dram_tensor("cm", [DS, L], F32, kind="ExternalInput")
    al_d = nc.dram_tensor("A_log", [DI, DS], F32, kind="ExternalInput")
    y_d = nc.dram_tensor("y", [DI, L], F32, kind="ExternalOutput")

    A = sb_tiles(nc, "A", DIT, 128, DS, F32)
    hstate = sb_tiles(nc, "hstate", DIT, 128, DS, F32)
    bmr = sb(nc, "bmr", DS, L, F32)
    cmr = sb(nc, "cmr", DS, L, F32)
    bbc = sb(nc, "bbc", 128, SW, F32)
    cbc = sb(nc, "cbc", 128, SW, F32)
    dtc = [sb(nc, f"dtc{k}", 128, SCH, F32) for k in range(2)]
    xsc = [sb(nc, f"xsc{k}", 128, SCH, F32) for k in range(2)]
    wt = sb(nc, "wt", 128, SCH, F32)
    da = sb(nc, "da", 128, SW, F32)
    dbx = sb(nc, "dbx", 128, SW, F32)
    hh = sb(nc, "hh", 128, SW, F32)
    yt = [sb(nc, f"yt{k}", 128, SCH, F32) for k in range(2)]
    psb = psum(nc, "psb", 128, SCH)
    psc = psum(nc, "psc", 128, SCH)

    with (
        nc.semaphore("ld") as ld_h,
        nc.semaphore("pe") as pe_h,
        nc.semaphore("sca") as sca_h,
        nc.semaphore("vec") as vec_h,
        nc.semaphore("sto") as sto_h,
        nc.Block() as block,
    ):
        ld = Ctr(ld_h)
        ctrs = {"pe": Ctr(pe_h), "sca": Ctr(sca_h), "vec": Ctr(vec_h)}
        pe, sca, vec = ctrs["pe"], ctrs["sca"], ctrs["vec"]
        sto = Ctr(sto_h)
        consts = make_consts(nc)
        load_consts(nc, block, consts, ld)

        @block.sync
        def _(q):
            for i in range(DIT):
                ld.dma(q.dma_start(out=A[i][:], in_=al_d.ap()[i * 128:(i + 1) * 128, :]))
            ld.dma(q.dma_start(out=bmr[:], in_=bm_d.ap()))
            ld.dma(q.dma_start(out=cmr[:], in_=cm_d.ap()))

        @block.vector
        def _(v):
            ld.wait(v)
            for i in range(DIT):
                ins = v.memset(hstate[i][:], 0.0)
            vec.inc(ins)

        @block.scalar
        def _(s):
            ld.wait(s)
            for i in range(DIT):
                ins = s.activation(out=A[i][:], in_=A[i][:], func=AF.Exp)
            sca.inc(ins)

        @block.vector
        def _(v):
            sca.wait(v)
            for i in range(DIT):
                ins = v.tensor_scalar_mul(A[i][:], A[i][:], -1.0)
            vec.inc(ins)

        @block.vector
        def _(v):
            slots = bass.AP(da, 0, [list(da[:].ap[0]), [SCH + 1, DS]])
            ins = v.memset(slots, 0.0)
            vec.inc(ins)

        for c in range(NSC):
            tc = slice(c * SCH, (c + 1) * SCH)
            for s in range(DS):
                reg = slice(s * (SCH + 1) + 1, s * (SCH + 1) + 1 + SCH)

                @block.tensor
                def _(t):
                    ld.wait(t)
                    vec.wait(t)
                    sca.wait(t)
                    selp = consts["sel16"][0:DS, s * 128:(s + 1) * 128]
                    t.matmul(psb[:], selp, bmr[0:DS, tc], start=True, stop=True)
                    ins = t.matmul(psc[:], selp, cmr[0:DS, tc], start=True, stop=True)
                    pe.inc(ins)

                @block.scalar
                def _(s_):
                    pe.wait(s_)
                    s_.copy(bbc[:, reg], psb[:])
                    ins = s_.copy(cbc[:, reg], psc[:])
                    sca.inc(ins)

            for i in range(DIT):
                buf = c % 2

                @block.sync
                def _(q):
                    vec.wait(q)
                    ld.dma(q.dma_start(out=dtc[buf][:], in_=dt_d.ap()[i * 128:(i + 1) * 128, tc]))
                    ld.dma(q.dma_start(out=xsc[buf][:], in_=xs_d.ap()[i * 128:(i + 1) * 128, tc]))

                @block.scalar
                def _(s_):
                    ld.wait(s_)
                    vec.wait(s_)
                    for s in range(DS):
                        reg = slice(s * (SCH + 1) + 1, s * (SCH + 1) + 1 + SCH)
                        ins = s_.activation(out=da[:, reg], in_=dtc[buf][:], func=AF.Exp,
                                            scale=A[i][:, s:s + 1])
                    sca.inc(ins)

                @block.vector
                def _(v):
                    ld.wait(v)
                    sca.wait(v)
                    v.tensor_mul(wt[:], dtc[buf][:], xsc[buf][:])
                    slots = bass.AP(dbx, 0, [list(dbx[:].ap[0]), [SCH + 1, DS]])
                    v.tensor_copy(slots, hstate[i][:])
                    for s in range(DS):
                        reg = slice(s * (SCH + 1) + 1, s * (SCH + 1) + 1 + SCH)
                        v.tensor_tensor(dbx[:, reg], wt[:], bbc[:, reg], ALU.mult)
                    v.tensor_tensor_scan(out=hh[:], data0=da[:], data1=dbx[:],
                                         initial=0.0, op0=ALU.mult, op1=ALU.add)
                    laststates = bass.AP(hh, SCH, [list(hh[:].ap[0]), [SCH + 1, DS]])
                    v.tensor_copy(hstate[i][:], laststates)
                    for s in range(DS):
                        reg = slice(s * (SCH + 1) + 1, s * (SCH + 1) + 1 + SCH)
                        v.tensor_tensor(dbx[:, reg], hh[:, reg], cbc[:, reg], ALU.mult)
                    sto.wait(v, max(0, sto.n - 16))
                    cyv = bass.AP(dbx, 1, [list(dbx[:].ap[0]), [1, SCH], [SCH + 1, DS]])
                    ins = v.tensor_reduce(yt[i % 2][:], cyv, axis=mybir.AxisListType.X, op=ALU.add)
                    vec.inc(ins)

                @block.sync
                def _(q):
                    vec.wait(q)
                    sto.dma(q.dma_start(out=y_d.ap()[i * 128:(i + 1) * 128, tc], in_=yt[i % 2][:]))

        @block.sync
        def _(q):
            sto.wait(q)
    return nc


def run_scan(pre, inputs):
    nc = build_scan()
    in_maps = []
    for c in range(NCORES):
        b, d = c % 4, ("f" if c < 4 else "b")
        rev = d == "b"

        def prep(a):
            a = a[::-1] if rev else a
            return np.ascontiguousarray(a.T)
        m = {
            "dt": prep(pre[f"dt_{d}"][b]),
            "xs": prep(pre[f"xs_{d}"][b]),
            "A_log": np.ascontiguousarray(inputs[f"L0_attn_{d}_A_log"]),
        }
        m["bm"] = prep(pre[f"Bm_{d}"][b])
        m["cm"] = prep(pre[f"Cm_{d}"][b])
        in_maps.append(m)
    res = run_spmd(nc, in_maps, core_ids=list(range(NCORES)))
    out = {}
    for d in ("f", "b"):
        y = np.empty((B, L, DI), np.float32)
        for b in range(4):
            c = b if d == "f" else b + 4
            yy = res.results[c]["y"].T
            y[b] = yy[::-1] if d == "b" else yy
        out[f"y_{d}"] = y
    return out


MPOST_PHASES = 3

def build_mamba_post(F=None):
    standalone = F is None
    nc = bass.Bass("TRN2", target_bir_lowering=False, debug=False, num_devices=8) if standalone else F["nc"]
    DIT = DI // 128
    dirs = ("f", "b")
    din = {}
    for d in dirs:
        din[d] = {
            "Dsk": nc.dram_tensor(f"Dsk_{d}", [DI, 1], F32, kind="ExternalInput"),
            "outWT": nc.dram_tensor(f"outWT_{d}", [DI, D], F32, kind="ExternalInput"),
            "projT": nc.dram_tensor(f"projT_{d}", [D, D], F32, kind="ExternalInput"),
        }
        if standalone:
            din[d]["y"] = nc.dram_tensor(f"y_{d}", [DI, TK], F32, kind="ExternalInput")
            din[d]["xs"] = nc.dram_tensor(f"xs2_{d}", [DI, TK], F32, kind="ExternalInput")
            din[d]["zs"] = nc.dram_tensor(f"zs2_{d}", [DI, TK], F32, kind="ExternalInput")
        else:
            din[d]["y"] = F["io"][f"sh_y_{d}"]
            din[d]["xs"] = F["io"][f"sh_xs_{d}"]
            din[d]["zs"] = F["io"][f"zs_{d}"]
    projb_d = nc.dram_tensor("projb", [D, 1], F32, kind="ExternalInput")
    xT_d = nc.dram_tensor("xT", [D, TK], F32, kind="ExternalInput")
    if standalone:
        o_d = nc.dram_tensor("x2T", [D, TK], F32, kind="ExternalOutput")
    else:
        o_d = F["io"]["x2T"]

    wbig = sb_tiles(nc, "wbig", DIT, 128, D, BF16)
    fo = sb_tiles(nc, "fo", DT, 128, TK, BF16)
    bo = sb_tiles(nc, "bo", DT, 128, TK, BF16)
    ystr = [sb(nc, f"ystr{k}", 128, CH, F32) for k in range(2)]
    xsstr = [sb(nc, f"xsstr{k}", 128, CH, F32) for k in range(2)]
    zsstr = [sb(nc, f"zsstr{k}", 128, CH, F32) for k in range(2)]
    dcol = {d: sb_tiles(nc, f"dcol_{d}", DIT, 128, 1, F32) for d in dirs}
    pbcol = sb_tiles(nc, "pbcol", DT, 128, 1, F32)
    yfin = sb_tiles(nc, "yfin", DIT, 128, CH, BF16)
    xch = sb_tiles(nc, "xch", DT, 128, CH, F32)
    x2p = sb_tiles(nc, "x2p", DT, 128, CH, F32)
    t1 = sb(nc, "t1", 128, CH, F32)
    lnb = ln_alloc(nc, "p", DT, CH)
    psA = psum(nc, "psA", 128, CH)
    psB = psum(nc, "psB", 128, CH)

    import contextlib
    es = contextlib.ExitStack()
    block, consts, ld, wld, ctrs, sto = _f_ctx(
        F, nc, es, ["ld", "wld", "pe", "sca", "vec", "sto"], need_sel=False)
    with es:
        pe, sca, vec = ctrs["pe"], ctrs["sca"], ctrs["vec"]


        @block.gpsimd
        def _(g):
            for d in dirs:
                for i in range(DIT):
                    ld.dma(g.dma_start(out=dcol[d][i][:], in_=din[d]["Dsk"].ap()[i * 128:(i + 1) * 128, :]))
            for i in range(DT):
                ld.dma(g.dma_start(out=pbcol[i][:], in_=projb_d.ap()[i * 128:(i + 1) * 128, :]))

        if MPOST_PHASES < 1:
            @block.sync
            def _(q):
                ld.wait(q)
            return nc
        for d in dirs:
            dst = fo if d == "f" else bo

            @block.gpsimd
            def _(g):
                pe.wait(g)
                for i in range(DIT):
                    wld.dma(g.dma_start(out=wbig[i][:], in_=din[d]["outWT"].ap()[i * 128:(i + 1) * 128, :]))

            for c in range(NCH):
                tc = slice(c * CH, (c + 1) * CH)
                for i in range(DIT):
                    bu = i % 2

                    @block.sync
                    def _(q):
                        vec.wait_relax(q, i)
                        r = slice(i * 128, (i + 1) * 128)
                        if standalone:
                            ld.dma(q.dma_start(out=ystr[bu][:], in_=din[d]["y"].ap()[r, tc]))
                            ld.dma(q.dma_start(out=xsstr[bu][:], in_=din[d]["xs"].ap()[r, tc]))
                        else:
                            def _l(off):
                                cs2 = slice(off + c * CH, off + (c + 1) * CH)
                                ld.dma(q.dma_start(out=ystr[bu][:], in_=din[d]["y"].ap()[r, cs2]))
                                ld.dma(q.dma_start(out=xsstr[bu][:], in_=din[d]["xs"].ap()[r, cs2]))
                            par_branch(q, lambda: _l(0), lambda: _l(TK), bal=(ld,))
                        ld.dma(q.dma_start(out=zsstr[bu][:], in_=din[d]["zs"].ap()[r, tc]))

                    @block.vector
                    def _(v):
                        ld.wait(v)
                        pe.wait(v)
                        v.tensor_scalar_mul(t1[:], xsstr[bu][:], dcol[d][i][:])
                        v.tensor_add(t1[:], ystr[bu][:], t1[:])
                        ins = v.tensor_mul(yfin[i][:], t1[:], zsstr[bu][:])
                        vec.inc(ins)

                for j in range(DT):
                    ps = psA if j % 2 == 0 else psB
                    jc = slice(j * 128, (j + 1) * 128)

                    @block.tensor
                    def _(t):
                        wld.wait(t)
                        vec.wait(t)
                        sca.wait(t)
                        for i in range(DIT):
                            ins = t.matmul(ps[:], wbig[i][:, jc], yfin[i][:],
                                           start=(i == 0), stop=(i == DIT - 1))
                        pe.inc(ins)

                    @block.scalar
                    def _(s):
                        pe.wait(s)
                        ins = s.copy(dst[j][:, tc], ps[:])
                        sca.inc(ins)

        if MPOST_PHASES < 2:
            @block.sync
            def _(q):
                sto.wait(q)
            return nc
        @block.gpsimd
        def _(g):
            pe.wait(g)
            sca.wait(g)
            for i in range(DT):
                wld.dma(g.dma_start(out=wbig[i][:], in_=din["f"]["projT"].ap()[i * 128:(i + 1) * 128, :]))
                wld.dma(g.dma_start(out=wbig[DT + i][:], in_=din["b"]["projT"].ap()[i * 128:(i + 1) * 128, :]))

        for c in range(NCH):
            tc = slice(c * CH, (c + 1) * CH)

            @block.sync
            def _(q):
                vec.wait(q)
                for i in range(DT):
                    ld.dma(q.dma_start(out=xch[i][:], in_=xT_d.ap()[i * 128:(i + 1) * 128, tc]))

            for j in range(DT):
                ps = psA if j % 2 == 0 else psB
                jc = slice(j * 128, (j + 1) * 128)

                @block.tensor
                def _(t):
                    wld.wait(t)
                    vec.wait_relax(t, j)
                    sca.wait(t)
                    for i in range(DT):
                        t.matmul(ps[:], wbig[i][:, jc], fo[i][:, tc], start=(i == 0), stop=False)
                    for i in range(DT):
                        ins = t.matmul(ps[:], wbig[DT + i][:, jc], bo[i][:, tc],
                                       start=False, stop=(i == DT - 1))
                    pe.inc(ins)

                @block.vector
                def _(v):
                    pe.wait(v)
                    ld.wait(v)
                    v.scalar_tensor_tensor(out=t1[:], in0=xch[j][:], scalar=2.0, in1=ps[:],
                                           op0=ALU.mult, op1=ALU.add)
                    ins = v.tensor_scalar_add(x2p[j][:], t1[:], pbcol[j][:])
                    vec.inc(ins)

            if MPOST_PHASES >= 3:
                ln_plain_T(nc, block, ctrs, x2p, x2p, DT, CH, consts, lnb)

            @block.sync
            def _(q):
                vec.wait(q)
                for i in range(DT):
                    sto.dma(q.dma_start(out=o_d.ap()[i * 128:(i + 1) * 128, tc], in_=x2p[i][:]))
                if (not standalone) and c == NCH - 1:
                    def _st(sl):
                        for i in range(DT):
                            sto.dma(q.dma_start(
                                out=F["io"]["x2c"].ap()[i * 128:(i + 1) * 128, sl],
                                in_=x2p[i][:, CH - 1:CH],
                                allow_slow_non_contiguous=True))
                    par_branch(q, lambda: _st(slice(0, 1)), lambda: _st(slice(1, 2)), bal=(sto,))

        @block.sync
        def _(q):
            sto.wait(q)
    return nc


def run_mamba_post(x, pre, ys, inputs):
    nc = build_mamba_post()
    base = {"projb": np.ascontiguousarray(inputs["L0_attn_proj_b"].reshape(D, 1))}
    pw = inputs["L0_attn_proj_W"]
    for d, pref in (("f", "L0_attn_f_"), ("b", "L0_attn_b_")):
        base[f"Dsk_{d}"] = np.ascontiguousarray(inputs[pref + "Dskip"].reshape(DI, 1))
        base[f"outWT_{d}"] = np.ascontiguousarray(inputs[pref + "outW"].T)
    base["projT_f"] = np.ascontiguousarray(pw[:, 0:D].T)
    base["projT_b"] = np.ascontiguousarray(pw[:, D:2 * D].T)
    in_maps = []
    for c in range(NCORES):
        b, h = c // 2, c % 2
        m = dict(base)
        m["xT"] = _shard_tok_T(x, c)
        for d in ("f", "b"):
            m[f"y_{d}"] = np.ascontiguousarray(ys[f"y_{d}"][b, h * TK:(h + 1) * TK].T)
            m[f"xs2_{d}"] = np.ascontiguousarray(pre[f"xs_{d}"][b, h * TK:(h + 1) * TK].T)
            m[f"zs2_{d}"] = np.ascontiguousarray(pre[f"zs_{d}"][b, h * TK:(h + 1) * TK].T)
        in_maps.append(m)
    res = run_spmd(nc, in_maps, core_ids=list(range(NCORES)))
    x2 = np.empty((B, L, D), np.float32)
    for c in range(NCORES):
        b, h = c // 2, c % 2
        x2[b, h * TK:(h + 1) * TK] = res.results[c]["x2T"].T
    return x2


HEADS, KVH, DH, ROT = 16, 8, 64, 32
ASCALE = DH ** (-0.5)


def _rope_tables(n0, n, scale):
    pos = np.arange(n0, n0 + n, dtype=np.float64)
    inv = 1.0 / 10000.0 ** (np.arange(0, ROT, 2, dtype=np.float64) / ROT)
    cos = np.full((128, n), scale, np.float64)
    sin = np.zeros((128, n), np.float64)
    for half in (0, 64):
        for r in range(ROT):
            f = inv[r // 2]
            cos[half + r] = np.cos(pos * f) * scale
            sin[half + r] = np.sin(pos * f) * scale
    return cos.astype(np.float32), sin.astype(np.float32)


def _rope_perm():
    P = np.zeros((128, 128), np.float32)
    for half in (0, 64):
        for r in range(0, ROT, 2):
            P[half + r + 1, half + r] = -1.0
            P[half + r, half + r + 1] = 1.0
    return P


def build_attn(F=None):
    standalone = F is None
    nc = bass.Bass("TRN2", target_bir_lowering=False, debug=False, num_devices=8) if standalone else F["nc"]
    NKT = L // 128
    if standalone:
        x3f_d = nc.dram_tensor("x3bf", [D, L], F32, kind="ExternalInput")
        x3o_d = nc.dram_tensor("x3own", [D, TK], F32, kind="ExternalInput")
    else:
        x3f_d = F["io"]["sh_x3"]
        x3o_d = F["io"]["x3T"]
    wq_d = nc.dram_tensor("Wq", [D, D], F32, kind="ExternalInput")
    wk_d = nc.dram_tensor("Wk", [D, KVH * DH], F32, kind="ExternalInput")
    wv_d = nc.dram_tensor("Wv", [D, KVH * DH], F32, kind="ExternalInput")
    wo_d = nc.dram_tensor("Wo", [D, D], F32, kind="ExternalInput")
    bo_d = nc.dram_tensor("bo", [D, 1], F32, kind="ExternalInput")
    cosq_d = nc.dram_tensor("cosq", [128, TK], F32, kind="ExternalInput")
    sinq_d = nc.dram_tensor("sinq", [128, TK], F32, kind="ExternalInput")
    cosk_d = nc.dram_tensor("cosk", [128, L], F32, kind="ExternalInput")
    sink_d = nc.dram_tensor("sink", [128, L], F32, kind="ExternalInput")
    ropeP_d = nc.dram_tensor("ropeP", [128, 128], F32, kind="ExternalInput")
    if standalone:
        o_d = nc.dram_tensor("x4T", [D, TK], F32, kind="ExternalOutput")
    else:
        o_d = F["io"]["x4T"]

    KD = KVH * DH
    x3c = sb_tiles(nc, "x3c", DT, 128, CH, BF16)
    wqb = sb_tiles(nc, "wqb", DT, 128, D, BF16)
    wkb = sb_tiles(nc, "wkb", DT, 128, KD, BF16)
    wvb = sb_tiles(nc, "wvb", DT, 128, KD, BF16)
    wob = sb_tiles(nc, "wob", DT, 128, D, BF16)
    bocol = sb_tiles(nc, "bocol", DT, 128, 1, F32)
    ropeP = sb(nc, "ropeP_s", 128, 128, F32)
    ctab = sb(nc, "ctab", 128, CH, F32)
    stab = sb(nc, "stab", 128, CH, F32)
    krope = [sb(nc, f"krope{i}", 128, L, BF16) for i in range(KD // 128)]
    vaug = [sb(nc, f"vaug{i}", 128, KVH * (DH + 1), BF16) for i in range(NKT)]
    qrope = sb_tiles(nc, "qrope", DT, 128, CH, F32)
    qaug = sb(nc, "qaug", DH + 1, CH, BF16)
    kaug = sb(nc, "kaug", DH + 1, L, BF16)
    pT = [sb(nc, f"pT{k}", 128, CH, BF16) for k in range(2)]
    sqv = sb(nc, "sqv", 128, CH, BF16)
    nrow = sb(nc, "nrow", 1, CH, F32)
    brow = sb(nc, "brow", 1, CH, F32)
    krow = sb(nc, "krow", 1, CH, F32)
    kmax = sb(nc, "kmax", 1, 1, F32)
    kcur = sb(nc, "kcur", 1, 1, F32)
    kmaxs = sb_tiles(nc, "kmaxs", KVH, 1, 1, F32)
    orow = sb(nc, "orow", 1, CH, F32)
    of32 = sb(nc, "of32", DH, CH, F32)
    otb = sb_tiles(nc, "otb", DT, 128, CH, BF16)
    x4p = sb_tiles(nc, "x4p", DT, 128, CH, F32)
    t1 = sb(nc, "at1", 128, CH, F32)
    t2 = sb(nc, "at2", 128, CH, F32)
    lnb = ln_alloc(nc, "a", DT, CH, xb=x3c, xsq=otb)
    psA = psum(nc, "psA", 128, CH)
    psB = psum(nc, "psB", 128, CH)
    psO = psum(nc, "psO", DH + 1, CH)
    psR = psum(nc, "psR", DH, CH)

    import contextlib
    es = contextlib.ExitStack()
    block, consts, ld, wld, ctrs, sto = _f_ctx(
        F, nc, es, ["ld", "pe", "sca", "vec", "sto"], need_sel=False)
    with es:
        pe, sca, vec = ctrs["pe"], ctrs["sca"], ctrs["vec"]

        @block.gpsimd
        def _(g):
            for i in range(DT):
                r = slice(i * 128, (i + 1) * 128)
                ld.dma(g.dma_start(out=wqb[i][:], in_=wq_d.ap()[r, :]))
                ld.dma(g.dma_start(out=wkb[i][:], in_=wk_d.ap()[r, :]))
                ld.dma(g.dma_start(out=wvb[i][:], in_=wv_d.ap()[r, :]))
                ld.dma(g.dma_start(out=wob[i][:], in_=wo_d.ap()[r, :]))
                ld.dma(g.dma_start(out=bocol[i][:], in_=bo_d.ap()[r, :]))
            ld.dma(g.dma_start(out=ropeP[:], in_=ropeP_d.ap()))

        for c in range(L // CH):
            tc = slice(c * CH, (c + 1) * CH)

            @block.gpsimd
            def _(g):
                pe.wait(g)
                for i in range(DT):
                    ld.dma(g.dma_start(out=x3c[i][:], in_=x3f_d.ap()[i * 128:(i + 1) * 128, tc]))

            @block.sync
            def _(q):
                vec.wait(q)
                ld.dma(q.dma_start(out=ctab[:], in_=cosk_d.ap()[:, tc]))
                ld.dma(q.dma_start(out=stab[:], in_=sink_d.ap()[:, tc]))

            for j in range(KD // 128):
                ps = psA if j % 2 == 0 else psB
                rps = psB if j % 2 == 0 else psA
                jc = slice(j * 128, (j + 1) * 128)

                @block.tensor
                def _(t):
                    ld.wait(t)
                    sca.wait(t)
                    vec.wait(t)
                    for i in range(DT):
                        ins = t.matmul(ps[:], wkb[i][:, jc], x3c[i][:], start=(i == 0), stop=(i == DT - 1))
                    pe.inc(ins)

                @block.scalar
                def _(s):
                    pe.wait(s)
                    ins = s.copy(t1[:], ps[:])
                    sca.inc(ins)

                @block.tensor
                def _(t):
                    sca.wait(t)
                    ins = t.matmul(rps[:], ropeP[:], t1[:], start=True, stop=True)
                    pe.inc(ins)

                @block.vector
                def _(v):
                    pe.wait(v)
                    sca.wait(v)
                    v.tensor_tensor(t2[:], t1[:], ctab[:], ALU.mult)
                    v.tensor_tensor(t1[:], rps[:], stab[:], ALU.mult)
                    ins = v.tensor_add(krope[j][:, tc], t2[:], t1[:])
                    vec.inc(ins)

            for tt in range(CH // 128):
                kt = c * (CH // 128) + tt
                ps = psA if tt % 2 == 0 else psB

                @block.tensor
                def _(t):
                    vec.wait(t)
                    sca.wait(t)
                    for i in range(DT):
                        ins = t.matmul(ps[:], x3c[i][:, tt * 128:(tt + 1) * 128], wvb[i][:],
                                       start=(i == 0), stop=(i == DT - 1))
                    pe.inc(ins)

                @block.scalar
                def _(s):
                    pe.wait(s)
                    for h2 in range(KVH):
                        ins = s.copy(vaug[kt][:, h2 * (DH + 1):h2 * (DH + 1) + DH],
                                     ps[:, h2 * DH:(h2 + 1) * DH])
                    sca.inc(ins)

        @block.vector
        def _(v):
            sca.wait(v)
            for kt in range(NKT):
                oc = bass.AP(vaug[kt], DH, [list(vaug[kt][:].ap[0]), [DH + 1, KVH]])
                ins = v.memset(oc, 1.0)
            vec.inc(ins)

        for h2 in range(KVH):
            ti, ro = h2 // 2, (h2 % 2) * 64

            for c in range(L // CH):
                @block.scalar
                def _(s):
                    pe.wait(s)
                    vec.wait(s)
                    ins = s.activation(out=sqv[0:DH, 0:CH], in_=krope[ti][ro:ro + DH, c * CH:(c + 1) * CH],
                                       func=AF.Square)
                    sca.inc(ins)

                @block.tensor
                def _(t):
                    sca.wait(t)
                    vec.wait(t)
                    ins = t.matmul(psA[0:1, 0:CH], consts["ones_col_bf"][0:DH, :], sqv[0:DH, 0:CH],
                                   start=True, stop=True)
                    pe.inc(ins)

                @block.vector
                def _(v):
                    pe.wait(v)
                    v.tensor_reduce(kcur[:], psA[0:1, 0:CH], axis=mybir.AxisListType.X, op=ALU.max)
                    if c == 0:
                        ins = v.tensor_copy(kmax[:], kcur[:])
                    else:
                        ins = v.tensor_max(kmax[:], kmax[:], kcur[:])
                    vec.inc(ins)

            @block.scalar
            def _(s):
                vec.wait(s)
                ins = s.activation(out=kmaxs[h2][:], in_=kmax[:], func=AF.Sqrt)
                sca.inc(ins)

        for c in range(NCH):
            tc = slice(c * CH, (c + 1) * CH)

            @block.gpsimd
            def _(g):
                pe.wait(g)
                for i in range(DT):
                    ld.dma(g.dma_start(out=x3c[i][:], in_=x3o_d.ap()[i * 128:(i + 1) * 128, tc]))

            @block.sync
            def _(q):
                vec.wait(q)
                ld.dma(q.dma_start(out=ctab[:], in_=cosq_d.ap()[:, tc]))
                ld.dma(q.dma_start(out=stab[:], in_=sinq_d.ap()[:, tc]))

            for j in range(DT):
                ps = psA if j % 2 == 0 else psB
                rps = psB if j % 2 == 0 else psA
                jc = slice(j * 128, (j + 1) * 128)

                @block.tensor
                def _(t):
                    ld.wait(t)
                    vec.wait(t)
                    sca.wait(t)
                    for i in range(DT):
                        ins = t.matmul(ps[:], wqb[i][:, jc], x3c[i][:], start=(i == 0), stop=(i == DT - 1))
                    pe.inc(ins)

                @block.scalar
                def _(s):
                    pe.wait(s)
                    ins = s.copy(t1[:], ps[:])
                    sca.inc(ins)

                @block.tensor
                def _(t):
                    sca.wait(t)
                    ins = t.matmul(rps[:], ropeP[:], t1[:], start=True, stop=True)
                    pe.inc(ins)

                @block.vector
                def _(v):
                    pe.wait(v)
                    sca.wait(v)
                    v.tensor_tensor(t2[:], t1[:], ctab[:], ALU.mult)
                    v.tensor_tensor(t1[:], rps[:], stab[:], ALU.mult)
                    ins = v.tensor_add(qrope[j][:], t2[:], t1[:])
                    vec.inc(ins)

            for h in range(HEADS):
                ti, ro = h // 2, (h % 2) * 64
                h2 = h // 2

                @block.scalar
                def _(s):
                    vec.wait(s)
                    pe.wait(s)
                    s.copy(qaug[0:DH, :], qrope[ti][ro:ro + DH, :])
                    ins = s.activation(out=sqv[0:DH, 0:CH], in_=qrope[ti][ro:ro + DH, :], func=AF.Square)
                    sca.inc(ins)

                @block.tensor
                def _(t):
                    sca.wait(t)
                    vec.wait(t)
                    ins = t.matmul(psO[0:1, 0:CH], consts["ones_col_bf"][0:DH, :], sqv[0:DH, 0:CH],
                                   start=True, stop=True)
                    pe.inc(ins)

                @block.scalar
                def _(s):
                    pe.wait(s)
                    ins = s.activation(out=nrow[:], in_=psO[0:1, 0:CH], func=AF.Sqrt)
                    sca.inc(ins)

                @block.vector
                def _(v):
                    sca.wait(v)
                    v.tensor_scalar(out=brow[:], in0=nrow[:], scalar1=kmaxs[h2][:],
                                    scalar2=-1.02, op0=ALU.mult, op1=ALU.mult)
                    ins = v.tensor_copy(qaug[DH:DH + 1, :], brow[:])
                    vec.inc(ins)

                if h % 2 == 0:
                    kti, kro = h2 // 2, (h2 % 2) * 64
                    @block.vector
                    def _(v):
                        pe.wait(v)
                        v.tensor_copy(kaug[0:DH, :], krope[kti][kro:kro + DH, :])
                        ins = v.memset(kaug[DH:DH + 1, :], 1.0)
                        vec.inc(ins)

                for kt in range(NKT):
                    ps = psA if kt % 2 == 0 else psB

                    @block.tensor
                    def _(t):
                        vec.wait(t)
                        sca.wait_relax(t, kt)
                        ins = t.matmul(ps[:], kaug[0:DH + 1, kt * 128:(kt + 1) * 128],
                                       qaug[0:DH + 1, :], start=True, stop=True)
                        pe.inc(ins)

                    @block.scalar
                    def _(s):
                        pe.wait(s)
                        ins = s.activation(out=pT[kt % 2][:], in_=ps[:], func=AF.Exp)
                        sca.inc(ins)

                    @block.tensor
                    def _(t):
                        sca.wait(t)
                        ins = t.matmul(psO[0:DH + 1, 0:CH], vaug[kt][:, h2 * (DH + 1):(h2 + 1) * (DH + 1)],
                                       pT[kt % 2][:], start=(kt == 0), stop=(kt == NKT - 1))
                        pe.inc(ins)

                @block.vector
                def _(v):
                    pe.wait(v)
                    ins = v.reciprocal(orow[:], psO[DH:DH + 1, 0:CH])
                    vec.inc(ins)

                @block.scalar
                def _(s):
                    pe.wait(s)
                    ins = s.copy(of32[:], psO[0:DH, 0:CH])
                    sca.inc(ins)

                @block.tensor
                def _(t):
                    vec.wait(t)
                    sca.wait(t)
                    ins = t.matmul(psR[0:DH, 0:CH], consts["ones_row_f32"][0:1, 0:DH], orow[:],
                                   start=True, stop=True)
                    pe.inc(ins)

                @block.vector
                def _(v):
                    pe.wait(v)
                    sca.wait(v)
                    ins = v.tensor_tensor(otb[ti][ro:ro + DH, :], of32[:], psR[0:DH, 0:CH], ALU.mult)
                    vec.inc(ins)

            xoch = qrope
            @block.sync
            def _(q):
                vec.wait(q)
                sca.wait(q)
                for i in range(DT):
                    ld.dma(q.dma_start(out=xoch[i][:], in_=x3o_d.ap()[i * 128:(i + 1) * 128, tc]))

            for j in range(DT):
                ps = psA if j % 2 == 0 else psB
                jc = slice(j * 128, (j + 1) * 128)

                @block.tensor
                def _(t):
                    vec.wait(t)
                    sca.wait(t)
                    for i in range(DT):
                        ins = t.matmul(ps[:], wob[i][:, jc], otb[i][:], start=(i == 0), stop=(i == DT - 1))
                    pe.inc(ins)

                @block.vector
                def _(v):
                    pe.wait(v)
                    ld.wait(v)
                    v.tensor_scalar_add(t1[:], ps[:], bocol[j][:])
                    ins = v.tensor_add(x4p[j][:], xoch[j][:], t1[:])
                    vec.inc(ins)

            ln_plain_T(nc, block, ctrs, x4p, x4p, DT, CH, consts, lnb)

            @block.sync
            def _(q):
                vec.wait(q)
                for i in range(DT):
                    sto.dma(q.dma_start(out=o_d.ap()[i * 128:(i + 1) * 128, tc], in_=x4p[i][:]))

        @block.sync
        def _(q):
            sto.wait(q)
    return nc


def run_attn(x3, inputs):
    nc = build_attn()
    base = {
        "Wq": np.ascontiguousarray(inputs["L1_attn_Wq"]),
        "Wk": np.ascontiguousarray(inputs["L1_attn_Wk"]),
        "Wv": np.ascontiguousarray(inputs["L1_attn_Wv"]),
        "Wo": np.ascontiguousarray(inputs["L1_attn_Wo"]),
        "bo": np.ascontiguousarray(inputs["L1_attn_bo"].reshape(D, 1)),
        "ropeP": _rope_perm(),
    }
    ck, sk = _rope_tables(0, L, 1.0)
    base["cosk"], base["sink"] = ck, sk
    in_maps = []
    for c in range(NCORES):
        b, h = c // 2, c % 2
        m = dict(base)
        x3T = np.ascontiguousarray(x3[b].T)
        m["x3bf"] = x3T
        m["x3own"] = np.ascontiguousarray(x3T[:, h * TK:(h + 1) * TK])
        cq, sq = _rope_tables(h * TK, TK, ASCALE)
        m["cosq"], m["sinq"] = cq, sq
        in_maps.append(m)
    res = run_spmd(nc, in_maps, core_ids=list(range(NCORES)))
    x4 = np.empty((B, L, D), np.float32)
    for c in range(NCORES):
        b, h = c // 2, c % 2
        x4[b, h * TK:(h + 1) * TK] = res.results[c]["x4T"].T
    return x4


NE = 16
NG, EPG, TKG, TOPK = 4, 4, 2, 4
MSCALE = 2.5
HALF = TK // 2


def build_moe(F=None):
    standalone = F is None
    nc = bass.Bass("TRN2", target_bir_lowering=False, debug=False, num_devices=8) if standalone else F["nc"]
    x4_d = nc.dram_tensor("x4T", [D, TK], F32, kind="ExternalInput") if standalone else F["io"]["x4T"]
    rt_d = nc.dram_tensor("routerT", [D, NE], F32, kind="ExternalInput")
    bias_d = nc.dram_tensor("bias16", [1, NE], F32, kind="ExternalInput")
    gw_d = nc.dram_tensor("gate_w", [NE, D, 1024], F32, kind="ExternalInput")
    uw_d = nc.dram_tensor("up_w", [NE, D, 1024], F32, kind="ExternalInput")
    dw_d = nc.dram_tensor("down_w", [NE, 1024, D], F32, kind="ExternalInput")
    shg_d = nc.dram_tensor("shg", [D, 2048], F32, kind="ExternalInput")
    shu_d = nc.dram_tensor("shu", [D, 2048], F32, kind="ExternalInput")
    shd_d = nc.dram_tensor("shd", [2048, D], F32, kind="ExternalInput")
    id_d = nc.dram_tensor("ident128", [128, 128], F32, kind="ExternalInput")
    o_d = nc.dram_tensor("outT", [D, TK], F32, kind="ExternalOutput")

    x4bf = sb_tiles(nc, "m_x4bf", DT, 128, HALF, BF16)
    racc = sb_tiles(nc, "m_racc", DT, 128, HALF, F32)
    wbuf2 = [sb_tiles(nc, f"m_wbuf{k}", DT, 128, 2048, BF16) for k in range(2)]
    hush = sb_tiles(nc, "m_hush", 2 * DT, 128, HALF, BF16)
    hs = hush[0:DT]
    rtb = sb_tiles(nc, "m_rtb", DT, 128, NE, BF16)
    biasr = sb(nc, "m_biasr", 1, NE, F32)
    biasb = sb(nc, "m_biasb", 128, NE, F32)
    ident = sb(nc, "m_ident", 128, 128, F32)
    scoresT = sb(nc, "m_scoresT", NE, HALF, F32)
    combT = sb(nc, "m_combT", NE, HALF, F32)
    stok = sb(nc, "m_stok", 128, NE, F32)
    sbv = sb(nc, "m_sb", 128, NE, F32)
    rank16 = sb(nc, "m_rank16", 128, NE, F32)
    m01 = sb(nc, "m_m01", 128, NE, F32)
    gsc = sb(nc, "m_gsc", 128, NG, F32)
    grank = sb(nc, "m_grank", 128, NG, F32)
    gm01 = sb(nc, "m_gm01", 128, NG, F32)
    emask = sb(nc, "m_emask", 128, NE, F32)
    ones16 = sb(nc, "m_ones16", 128, NE, F32)
    neg1 = sb(nc, "m_neg1", 128, NE, F32)
    den = sb(nc, "m_den", 128, 1, F32)
    thrc = sb(nc, "m_thrc", 128, 1, F32)
    ut = sb(nc, "m_ut", 128, CH, BF16)
    cb = sb(nc, "m_cb", 128, CH, F32)
    xch = sb_tiles(nc, "m_xch", DT, 128, CH, F32)
    x5p = sb_tiles(nc, "m_x5p", DT, 128, CH, F32)
    lnb = ln_alloc(nc, "m5", DT, CH, xb=hush[0:DT], xsq=hush[DT:2 * DT])
    tmoe = lnb["tmp"]
    psA = psum(nc, "psA", 128, CH)
    psB = psum(nc, "psB", 128, CH)
    psC = psum(nc, "psC", NE, CH)
    psT = psum(nc, "psT", 128, CH)

    import contextlib
    es = contextlib.ExitStack()
    block, consts, ld, wld, ctrs, sto = _f_ctx(
        F, nc, es, ["ld", "wld", "pe", "sca", "vec", "sto"], need_sel=True)
    with es:
        pe, sca, vec = ctrs["pe"], ctrs["sca"], ctrs["vec"]
        if F is not None:
            sl16 = sb(nc, "c_sel16_moe", 16, 16 * 128, F32)

            @block.sync
            def _(q):
                ld.dma(q.dma_start(out=sl16[:], in_=consts["t_sel16"].ap()))
            consts = dict(consts)
            consts["sel16"] = sl16

        @block.sync
        def _(q):
            ld.dma(q.dma_start(out=biasr[:], in_=bias_d.ap()))
            ld.dma(q.dma_start(out=ident[:], in_=id_d.ap()))

        @block.gpsimd
        def _(g):
            for i in range(DT):
                ld.dma(g.dma_start(out=rtb[i][:], in_=rt_d.ap()[i * 128:(i + 1) * 128, :]))

        @block.vector
        def _(v):
            ld.wait(v)
            v.memset(ones16[:], 1.0)
            ins = v.memset(neg1[:], -1.0)
            vec.inc(ins)

        @block.tensor
        def _(t):
            ld.wait(t)
            ins = t.matmul(psT[0:128, 0:NE], consts["ones_row_f32"][:], biasr[:], start=True, stop=True)
            pe.inc(ins)

        @block.scalar
        def _(s):
            pe.wait(s)
            ins = s.copy(biasb[:], psT[0:128, 0:NE])
            sca.inc(ins)

        for half in range(2):
            hs_tok = slice(half * HALF, (half + 1) * HALF)

            @block.gpsimd
            def _(g):
                pe.wait(g)
                for i in range(DT):
                    ld.dma(g.dma_start(out=x4bf[i][:], in_=x4_d.ap()[i * 128:(i + 1) * 128, hs_tok]))

            @block.vector
            def _(v):
                ld.wait(v)
                sto.wait(v)
                for i in range(DT):
                    ins = v.memset(racc[i][:], 0.0)
                vec.inc(ins)

            for c2 in range(HALF // CH):
                cs = slice(c2 * CH, (c2 + 1) * CH)

                @block.tensor
                def _(t):
                    ld.wait(t)
                    sca.wait(t)
                    for i in range(DT):
                        ins = t.matmul(psC[:, 0:CH], rtb[i][:], x4bf[i][:, cs],
                                       start=(i == 0), stop=(i == DT - 1))
                    pe.inc(ins)

                @block.scalar
                def _(s):
                    pe.wait(s)
                    ins = s.activation(out=scoresT[0:NE, cs], in_=psC[:, 0:CH], func=AF.Sigmoid)
                    sca.inc(ins)

            for tt in range(HALF // 128):
                @block.tensor
                def _(t):
                    sca.wait(t)
                    vec.wait(t)
                    ins = t.transpose(psT[0:128, 0:NE], scoresT[0:NE, tt * 128:(tt + 1) * 128], ident[0:NE, 0:NE])
                    pe.inc(ins)

                @block.vector
                def _(v):
                    def E(ins):
                        vec.inc(ins)
                        vec.wait(v)
                    pe.wait(v)
                    E(v.tensor_copy(stok[:], psT[0:128, 0:NE]))
                    E(v.tensor_add(sbv[:], stok[:], biasb[:]))
                    E(v.memset(rank16[:], 0.0))
                    for g in range(NG):
                        gsl = slice(g * EPG, (g + 1) * EPG)
                        for e2 in range(EPG):
                            col = sbv[:, g * EPG + e2:g * EPG + e2 + 1]
                            v.scalar_tensor_tensor(out=rank16[:, gsl], in0=sbv[:, gsl], scalar=col,
                                                   in1=rank16[:, gsl], op0=ALU.is_lt, op1=ALU.add)
                            if e2 < EPG - 1:
                                t2sl = slice(g * EPG + e2 + 1, (g + 1) * EPG)
                                v.scalar_tensor_tensor(out=rank16[:, t2sl], in0=sbv[:, t2sl], scalar=col,
                                                       in1=rank16[:, t2sl], op0=ALU.is_equal, op1=ALU.add)
                    E(v.memset(thrc[:], 2.0))
                    E(v.tensor_single_scalar(m01[:], rank16[:], thrc[:], ALU.is_lt))
                    E(v.tensor_mul(m01[:], m01[:], sbv[:]))
                    for g in range(NG):
                        E(v.tensor_reduce(gsc[:, g:g + 1], m01[:, g * EPG:(g + 1) * EPG],
                                          axis=mybir.AxisListType.X, op=ALU.add))
                    E(v.memset(grank[:], 0.0))
                    for g2 in range(NG):
                        col = gsc[:, g2:g2 + 1]
                        v.scalar_tensor_tensor(out=grank[:], in0=gsc[:], scalar=col,
                                               in1=grank[:], op0=ALU.is_lt, op1=ALU.add)
                        if g2 < NG - 1:
                            t2sl = slice(g2 + 1, NG)
                            v.scalar_tensor_tensor(out=grank[:, t2sl], in0=gsc[:, t2sl], scalar=col,
                                                   in1=grank[:, t2sl], op0=ALU.is_equal, op1=ALU.add)
                    E(v.engine_nop())
                    E(v.tensor_single_scalar(gm01[:], grank[:], thrc[:], ALU.is_lt))
                    for g in range(NG):
                        E(v.tensor_scalar_mul(emask[:, g * EPG:(g + 1) * EPG], ones16[:, 0:EPG],
                                              gm01[:, g:g + 1]))
                    E(v.tensor_scalar_add(m01[:], sbv[:], 1.0))
                    E(v.tensor_mul(m01[:], m01[:], emask[:]))
                    E(v.tensor_scalar_add(sbv[:], m01[:], -1.0))
                    E(v.memset(rank16[:], 0.0))
                    for e2 in range(NE):
                        col = sbv[:, e2:e2 + 1]
                        v.scalar_tensor_tensor(out=rank16[:], in0=sbv[:], scalar=col,
                                               in1=rank16[:], op0=ALU.is_lt, op1=ALU.add)
                        if e2 < NE - 1:
                            t2sl = slice(e2 + 1, NE)
                            v.scalar_tensor_tensor(out=rank16[:, t2sl], in0=sbv[:, t2sl], scalar=col,
                                                   in1=rank16[:, t2sl], op0=ALU.is_equal, op1=ALU.add)
                    E(v.memset(thrc[:], float(TOPK)))
                    E(v.tensor_single_scalar(m01[:], rank16[:], thrc[:], ALU.is_lt))
                    E(v.tensor_mul(m01[:], m01[:], stok[:]))
                    E(v.tensor_reduce(den[:], m01[:], axis=mybir.AxisListType.X, op=ALU.add))
                    E(v.tensor_scalar_add(den[:], den[:], 1e-20))
                    E(v.reciprocal(den[:], den[:]))
                    E(v.tensor_scalar(out=stok[:], in0=m01[:], scalar1=den[:],
                                      scalar2=MSCALE, op0=ALU.mult, op1=ALU.mult))

                @block.tensor
                def _(t):
                    vec.wait(t)
                    sca.wait(t)
                    ins = t.transpose(psT[0:NE, 0:128], stok[:, 0:NE], ident[:])
                    pe.inc(ins)

                @block.scalar
                def _(s):
                    pe.wait(s)
                    ins = s.copy(combT[0:NE, tt * 128:(tt + 1) * 128], psT[0:NE, 0:128])
                    sca.inc(ins)

            phist = [pe.n, pe.n]
            tgi = [0]
            for e in range(NE):
                for wsrc, tag in ((gw_d, "g"), (uw_d, "u"), (dw_d, "d")):
                    wbuf = wbuf2[tgi[0] % 2]

                    @block.gpsimd
                    def _(g):
                        pe.wait(g, phist[-2])
                        for i in range(DT):
                            wld.dma(g.dma_start(out=wbuf[i][:, 0:1024],
                                                in_=wsrc.ap()[e, i * 128:(i + 1) * 128, :]))

                    if tag == "g":
                        for tch in range(HALF // CH):
                            ts = slice(tch * CH, (tch + 1) * CH)
                            for fj in range(DT):
                                ps = psA if fj % 2 == 0 else psB

                                @block.tensor
                                def _(t):
                                    wld.wait(t)
                                    sca.wait_relax(t, tch * DT + fj)
                                    vec.wait(t)
                                    for i in range(DT):
                                        ins = t.matmul(ps[:], wbuf[i][:, fj * 128:(fj + 1) * 128],
                                                       x4bf[i][:, ts], start=(i == 0), stop=(i == DT - 1))
                                    pe.inc(ins)

                                @block.scalar
                                def _(s):
                                    pe.wait(s)
                                    ins = s.activation(out=hs[fj][:, ts], in_=ps[:], func=AF.Silu)
                                    sca.inc(ins)
                    elif tag == "u":
                        for tch in range(HALF // CH):
                            ts = slice(tch * CH, (tch + 1) * CH)
                            for fj in range(DT):
                                ps = psA if fj % 2 == 0 else psB

                                @block.tensor
                                def _(t):
                                    wld.wait(t)
                                    sca.wait(t)
                                    vec.wait(t)
                                    for i in range(DT):
                                        ins = t.matmul(ps[:], wbuf[i][:, fj * 128:(fj + 1) * 128],
                                                       x4bf[i][:, ts], start=(i == 0), stop=(i == DT - 1))
                                    pe.inc(ins)

                                @block.vector
                                def _(v):
                                    pe.wait(v)
                                    ins = v.tensor_mul(hs[fj][:, ts], hs[fj][:, ts], ps[:])
                                    vec.inc(ins)
                    else:
                        for tch in range(HALF // CH):
                            ts = slice(tch * CH, (tch + 1) * CH)

                            @block.tensor
                            def _(t):
                                sca.wait(t)
                                vec.wait(t)
                                ins = t.matmul(psT[:, 0:CH], consts["sel16"][0:NE, e * 128:(e + 1) * 128],
                                               combT[0:NE, ts], start=True, stop=True)
                                pe.inc(ins)

                            @block.scalar
                            def _(s):
                                pe.wait(s)
                                ins = s.copy(cb[:], psT[:, 0:CH])
                                sca.inc(ins)

                            for dj in range(DT):
                                ps = psA if dj % 2 == 0 else psB

                                @block.tensor
                                def _(t):
                                    wld.wait(t)
                                    sca.wait(t)
                                    vec.wait(t)
                                    for fj in range(DT):
                                        ins = t.matmul(ps[:], wbuf[fj][:, dj * 128:(dj + 1) * 128],
                                                       hs[fj][:, ts], start=(fj == 0), stop=(fj == DT - 1))
                                    pe.inc(ins)

                                @block.vector
                                def _(v):
                                    pe.wait(v)
                                    sca.wait(v)
                                    v.tensor_tensor(tmoe[:], ps[:], cb[:], ALU.mult)
                                    ins = v.tensor_add(racc[dj][:, ts], racc[dj][:, ts], tmoe[:])
                                    vec.inc(ins)
                    phist.append(pe.n)
                    tgi[0] += 1

            pe_sh0 = pe.n
            for phase in ("g", "u", "d"):
                wbuf = wbuf2[{"g": 0, "u": 1, "d": 0}[phase]]

                @block.gpsimd
                def _(g):
                    if phase == "g":
                        pe.wait(g, phist[-1])
                        pe.wait(g)
                    elif phase == "u":
                        pe.wait(g, pe_sh0)
                    else:
                        pe.wait(g)
                    if phase in ("g", "u"):
                        src = shg_d if phase == "g" else shu_d
                        for i in range(DT):
                            wld.dma(g.dma_start(out=wbuf[i][:], in_=src.ap()[i * 128:(i + 1) * 128, :]))
                    else:
                        for k in range(2 * DT):
                            wld.dma(g.dma_start(out=wbuf[k // 2][:, (k % 2) * 1024:(k % 2 + 1) * 1024],
                                                in_=shd_d.ap()[k * 128:(k + 1) * 128, :]))

                if phase in ("g", "u"):
                    for tch in range(HALF // CH):
                        ts = slice(tch * CH, (tch + 1) * CH)
                        for fj in range(2 * DT):
                            ps = psA if fj % 2 == 0 else psB

                            @block.tensor
                            def _(t):
                                wld.wait(t)
                                sca.wait_relax(t, tch * 2 * DT + fj)
                                vec.wait(t)
                                for i in range(DT):
                                    ins = t.matmul(ps[:], wbuf[i][:, fj * 128:(fj + 1) * 128],
                                                   x4bf[i][:, ts], start=(i == 0), stop=(i == DT - 1))
                                pe.inc(ins)

                            if phase == "g":
                                @block.scalar
                                def _(s):
                                    pe.wait(s)
                                    ins = s.activation(out=hush[fj][:, ts], in_=ps[:], func=AF.Silu)
                                    sca.inc(ins)
                            else:
                                @block.vector
                                def _(v):
                                    pe.wait(v)
                                    ins = v.tensor_mul(hush[fj][:, ts], hush[fj][:, ts], ps[:])
                                    vec.inc(ins)
                else:
                    for tch in range(HALF // CH):
                        ts = slice(tch * CH, (tch + 1) * CH)
                        for dj in range(DT):
                            ps = psA if dj % 2 == 0 else psB

                            @block.tensor
                            def _(t):
                                wld.wait(t)
                                sca.wait(t)
                                vec.wait(t)
                                for k in range(2 * DT):
                                    ins = t.matmul(ps[:], wbuf[k // 2][:, (k % 2) * 1024 + dj * 128:
                                                                       (k % 2) * 1024 + (dj + 1) * 128],
                                                   hush[k][:, ts], start=(k == 0), stop=(k == 2 * DT - 1))
                                pe.inc(ins)

                            @block.vector
                            def _(v):
                                pe.wait(v)
                                ins = v.tensor_add(racc[dj][:, ts], racc[dj][:, ts], ps[:])
                                vec.inc(ins)

            for tch in range(HALF // CH):
                ts = slice(tch * CH, (tch + 1) * CH)
                gts = slice(half * HALF + tch * CH, half * HALF + (tch + 1) * CH)

                @block.sync
                def _(q):
                    vec.wait(q)
                    for i in range(DT):
                        ld.dma(q.dma_start(out=xch[i][:], in_=x4_d.ap()[i * 128:(i + 1) * 128, gts]))

                @block.vector
                def _(v):
                    ld.wait(v)
                    sto.wait(v, max(0, sto.n - 16 * DT))
                    for i in range(DT):
                        ins = v.tensor_add(x5p[i][:], xch[i][:], racc[i][:, ts])
                    vec.inc(ins)

                ln_plain_T(nc, block, ctrs, x5p, x5p, DT, CH, consts, lnb)

                @block.sync
                def _(q):
                    vec.wait(q)
                    for i in range(DT):
                        sto.dma(q.dma_start(out=o_d.ap()[i * 128:(i + 1) * 128, gts], in_=x5p[i][:]))

        @block.sync
        def _(q):
            sto.wait(q)
    return nc


def run_moe(x4, inputs):
    nc = build_moe()
    base = {
        "routerT": np.ascontiguousarray(inputs["L1_ffn_router_w"].T),
        "bias16": np.ascontiguousarray(inputs["L1_ffn_bias"].reshape(1, NE)),
        "gate_w": np.ascontiguousarray(inputs["L1_ffn_gate_w"]),
        "up_w": np.ascontiguousarray(inputs["L1_ffn_up_w"]),
        "down_w": np.ascontiguousarray(inputs["L1_ffn_down_w"]),
        "shg": np.ascontiguousarray(inputs["L1_ffn_sh_gate_w"]),
        "shu": np.ascontiguousarray(inputs["L1_ffn_sh_up_w"]),
        "shd": np.ascontiguousarray(inputs["L1_ffn_sh_down_w"]),
        "ident128": np.eye(128, dtype=np.float32),
    }
    in_maps = []
    for c in range(NCORES):
        m = dict(base)
        m["x4T"] = _shard_tok_T(x4, c)
        in_maps.append(m)
    res = run_spmd(nc, in_maps, core_ids=list(range(NCORES)))
    out = np.empty((B, L, D), np.float32)
    for c in range(NCORES):
        b, h = c // 2, c % 2
        out[b, h * TK:(h + 1) * TK] = res.results[c]["outT"].T
    return out


def emit_scan_fused(F):
    nc, block = F["nc"], F["block"]
    ld, ctrs, sto = F["ld"], F["ctrs"], F["sto"]
    pe, sca, vec = ctrs["pe"], ctrs["sca"], ctrs["vec"]
    consts = F["consts"]
    NDT = 8
    NSC = L // SCH
    al_d = {d: nc.dram_tensor(f"al_{d}", [1024, DS], F32, kind="ExternalInput") for d in ("f", "b")}

    sl16 = sb(nc, "c_sel16_scan", 16, 16 * 128, F32)
    A = sb_tiles(nc, "sc_A", NDT, 128, DS, F32)
    hstate = sb_tiles(nc, "sc_hstate", NDT, 128, DS, F32)
    bmr = sb(nc, "sc_bmr", DS, L, F32)
    cmr = sb(nc, "sc_cmr", DS, L, F32)
    bbc = sb(nc, "sc_bbc", 128, SW, F32)
    cbc = sb(nc, "sc_cbc", 128, SW, F32)
    bstg = sb(nc, "sc_bstg", 128, SCH, F32)
    dtc = [sb(nc, f"sc_dtc{k}", 128, SCH, F32) for k in range(2)]
    xsc = [sb(nc, f"sc_xsc{k}", 128, SCH, F32) for k in range(2)]
    dtr = sb(nc, "sc_dtr", 128, SCH, F32)
    xsr = sb(nc, "sc_xsr", 128, SCH, F32)
    wt = sb(nc, "sc_wt", 128, SCH, F32)
    da2 = [sb(nc, f"sc_da{k}", 128, SW, F32) for k in range(2)]
    dbx = sb(nc, "sc_dbx", 128, SW, F32)
    hh = sb(nc, "sc_hh", 128, SW, F32)
    yt = [sb(nc, f"sc_yt{k}", 128, SCH, F32) for k in range(2)]
    ytr = sb(nc, "sc_ytr", 128, SCH, F32)
    psb = psum(nc, "sc_psb", 128, 2 * SCH)
    psc = psum(nc, "sc_psc", 128, 2 * SCH)

    @block.sync
    def _(q):
        ld.dma(q.dma_start(out=sl16[:], in_=consts["t_sel16"].ap()))

    for d in ("f", "b"):
        rev = d == "b"
        sh_dt, sh_xs = F["io"][f"sh_dt_{d}"], F["io"][f"sh_xs_{d}"]
        sh_bm, sh_cm = F["io"][f"sh_Bm_{d}"], F["io"][f"sh_Cm_{d}"]
        sh_y = F["io"][f"sh_y_{d}"]

        @block.sync
        def _(q):
            vec.wait(q)
            sca.wait(q)
            for i in range(NDT):
                ld.dma(q.dma_start(out=A[i][:],
                                   in_=al_d[d].ap()[i * 128:(i + 1) * 128, :]))
            ld.dma(q.dma_start(out=bmr[:], in_=sh_bm.ap()))
            ld.dma(q.dma_start(out=cmr[:], in_=sh_cm.ap()))

        @block.vector
        def _(v):
            ld.wait(v)
            for i in range(NDT):
                ins = v.memset(hstate[i][:], 0.0)
            ctrs["vec"].inc(ins)

        @block.scalar
        def _(s):
            ld.wait(s)
            vec.wait(s)
            for i in range(NDT):
                ins = s.activation(out=A[i][:], in_=A[i][:], func=AF.Exp)
            sca.inc(ins)

        @block.vector
        def _(v):
            sca.wait(v)
            for i in range(NDT):
                v.tensor_scalar_mul(A[i][:], A[i][:], -1.0)
            for k in range(2):
                slots = bass.AP(da2[k], 0, [list(da2[k][:].ap[0]), [SCH + 1, DS]])
                ins = v.memset(slots, 0.0)
            vec.inc(ins)
        vhist = []

        for c in range(NSC):
            rc0 = (L - (c + 1) * SCH) if rev else (c * SCH)
            tc = slice(rc0, rc0 + SCH)
            for sp in range(DS // 2):
                s0, s1 = 2 * sp, 2 * sp + 1
                reg0 = slice(s0 * (SCH + 1) + 1, s0 * (SCH + 1) + 1 + SCH)
                reg1 = slice(s1 * (SCH + 1) + 1, s1 * (SCH + 1) + 1 + SCH)
                h0, h1 = slice(0, SCH), slice(SCH, 2 * SCH)

                @block.tensor
                def _(t):
                    ld.wait(t)
                    vec.wait(t)
                    sca.wait(t)
                    sel0 = sl16[0:DS, s0 * 128:(s0 + 1) * 128]
                    sel1 = sl16[0:DS, s1 * 128:(s1 + 1) * 128]
                    t.matmul(psb[:, h0], sel0, bmr[0:DS, tc], start=True, stop=True)
                    t.matmul(psb[:, h1], sel1, bmr[0:DS, tc], start=True, stop=True)
                    t.matmul(psc[:, h0], sel0, cmr[0:DS, tc], start=True, stop=True)
                    ins = t.matmul(psc[:, h1], sel1, cmr[0:DS, tc], start=True, stop=True)
                    pe.inc(ins)

                if not rev:
                    @block.scalar
                    def _(s_):
                        pe.wait(s_)
                        s_.copy(bbc[:, reg0], psb[:, h0])
                        s_.copy(bbc[:, reg1], psb[:, h1])
                        s_.copy(cbc[:, reg0], psc[:, h0])
                        ins = s_.copy(cbc[:, reg1], psc[:, h1])
                        sca.inc(ins)
                else:
                    @block.scalar
                    def _(s_):
                        pe.wait(s_)
                        s_.copy(bstg[:], psb[:, h0])
                        s_.copy(dtr[:], psb[:, h1])
                        s_.copy(xsr[:], psc[:, h0])
                        ins = s_.copy(ytr[:], psc[:, h1])
                        sca.inc(ins)

                    @block.vector
                    def _(v):
                        sca.wait(v)
                        v.tensor_copy(bbc[:, reg0], bstg[:, ::-1])
                        v.tensor_copy(bbc[:, reg1], dtr[:, ::-1])
                        v.tensor_copy(cbc[:, reg0], xsr[:, ::-1])
                        ins = v.tensor_copy(cbc[:, reg1], ytr[:, ::-1])
                        vec.inc(ins)

            for i in range(NDT):
                buf = (c * NDT + i) % 2

                @block.gpsimd
                def _(g):
                    vec.wait_relax(g, c * NDT + i, back=(4 if rev else 1))

                    def _lr(base):
                        ro = slice(base + i * 128, base + (i + 1) * 128)
                        ld.dma(g.dma_start(out=dtc[buf][:], in_=sh_dt.ap()[ro, tc]))
                        ld.dma(g.dma_start(out=xsc[buf][:], in_=sh_xs.ap()[ro, tc]))
                    par_branch(g, lambda: _lr(0), lambda: _lr(1024), bal=(ld,))

                if rev:
                    @block.vector
                    def _(v):
                        ld.wait(v)
                        sca.wait(v)
                        v.tensor_copy(dtr[:], dtc[buf][:, ::-1])
                        ins = v.tensor_copy(xsr[:], xsc[buf][:, ::-1])
                        vec.inc(ins)
                    dts, xss = dtr, xsr
                else:
                    dts, xss = dtc[buf], xsc[buf]

                ig = c * NDT + i
                da = da2[ig % 2]

                @block.scalar
                def _(s_):
                    ld.wait(s_)
                    if rev or len(vhist) < 2:
                        vec.wait(s_)
                    else:
                        vec.wait(s_, vhist[-2])
                    for s in range(DS):
                        reg = slice(s * (SCH + 1) + 1, s * (SCH + 1) + 1 + SCH)
                        ins = s_.activation(out=da[:, reg], in_=dts[:], func=AF.Exp,
                                            scale=A[i][:, s:s + 1])
                    sca.inc(ins)

                @block.vector
                def _(v):
                    ld.wait(v)
                    sca.wait(v)
                    v.tensor_mul(wt[:], dts[:], xss[:])
                    slots = bass.AP(dbx, 0, [list(dbx[:].ap[0]), [SCH + 1, DS]])
                    v.tensor_copy(slots, hstate[i][:])
                    for s in range(DS):
                        reg = slice(s * (SCH + 1) + 1, s * (SCH + 1) + 1 + SCH)
                        v.tensor_tensor(dbx[:, reg], wt[:], bbc[:, reg], ALU.mult)
                    v.tensor_tensor_scan(out=hh[:], data0=da[:], data1=dbx[:],
                                         initial=0.0, op0=ALU.mult, op1=ALU.add)
                    laststates = bass.AP(hh, SCH, [list(hh[:].ap[0]), [SCH + 1, DS]])
                    v.tensor_copy(hstate[i][:], laststates)
                    for s in range(DS):
                        reg = slice(s * (SCH + 1) + 1, s * (SCH + 1) + 1 + SCH)
                        v.tensor_tensor(dbx[:, reg], hh[:, reg], cbc[:, reg], ALU.mult)
                    sto.wait(v, max(0, sto.n - 16))
                    cyv = bass.AP(dbx, 1, [list(dbx[:].ap[0]), [1, SCH], [SCH + 1, DS]])
                    ins = v.tensor_reduce(yt[i % 2][:], cyv, axis=mybir.AxisListType.X, op=ALU.add)
                    if rev:
                        vec.inc(ins)
                        vec.wait(v)
                        ins = v.tensor_copy(ytr[:], yt[i % 2][:, ::-1])
                        vec.inc(ins)
                        vec.wait(v)
                        ins = v.tensor_copy(yt[i % 2][:], ytr[:])
                    vec.inc(ins)

                vhist.append(vec.n)

                @block.gpsimd
                def _(g):
                    vec.wait(g)

                    def _sr(base):
                        ro = slice(base + i * 128, base + (i + 1) * 128)
                        sto.dma(g.dma_start(out=sh_y.ap()[ro, tc], in_=yt[i % 2][:]))
                    par_branch(g, lambda: _sr(0), lambda: _sr(1024), bal=(sto,))

    @block.gpsimd
    def _(g):
        sto.wait(g)


I32 = mybir.dt.int32
FUSED = True


def _scope(pfx):
    import contextlib

    class _S:
        def __enter__(self):
            self.st = contextlib.ExitStack()
            self.old = dict(_ALLOC)
            _ALLOC["stack"] = self.st
            _ALLOC["pfx"] = pfx
            self.st.__enter__()
            return self

        def __exit__(self, *a):
            r = self.st.__exit__(*a)
            _ALLOC.update(self.old)
            return r
    return _S()


def build_fused():
    nc = bass.Bass("TRN2", target_bir_lowering=False, debug=False, num_devices=8)
    consts = make_consts(nc)

    def shd(name, shape, dt):
        return nc.dram_tensor(name, shape, dt, kind="Internal", addr_space="Shared")

    def locd(name, shape, dt):
        return nc.dram_tensor(name, shape, dt, kind="Internal")
    io = {}
    for d in ("f", "b"):
        io[f"sh_xs_{d}"] = shd(f"sh_xs_{d}", [DI, L], F32)
        io[f"sh_dt_{d}"] = shd(f"sh_dt_{d}", [DI, L], F32)
        io[f"sh_Bm_{d}"] = shd(f"sh_Bm_{d}", [DS, L], F32)
        io[f"sh_Cm_{d}"] = shd(f"sh_Cm_{d}", [DS, L], F32)
        io[f"sh_y_{d}"] = shd(f"sh_y_{d}", [DI, L], F32)
        io[f"zs_{d}"] = locd(f"zs_{d}", [DI, TK], F32)
    io["x2T"] = locd("x2T", [D, TK], F32)
    io["x2c"] = shd("sh_x2c", [D, 2], F32)
    io["x3T"] = locd("x3T", [D, TK], F32)
    io["sh_x3"] = shd("sh_x3", [D, L], BF16)
    io["x4T"] = locd("x4T", [D, TK], F32)
    sh_flag = shd("sh_flag", [1, 2], I32)
    salt_d = nc.dram_tensor("salt", [1, 8], I32, kind="ExternalInput")

    saltt = sb(nc, "saltt", 1, 8, I32)
    fread = sb(nc, "fread", 1, 1, I32)

    import contextlib
    with contextlib.ExitStack() as TOP:
        hs = {n: TOP.enter_context(nc.semaphore(n)) for n in
              ("ld", "wld", "pe", "sca", "vec", "sto", "xc", "fl")}
        block = TOP.enter_context(nc.Block())
        ld, wld = Ctr(hs["ld"]), Ctr(hs["wld"])
        ctrs = {"pe": Ctr(hs["pe"]), "sca": Ctr(hs["sca"]), "vec": Ctr(hs["vec"])}
        sto = Ctr(hs["sto"])
        xcc = Ctr(hs["xc"])
        flc = Ctr(hs["fl"])
        load_consts(nc, block, consts, ld, need_sel=False)

        @block.sync
        def _(q):
            ld.dma(q.dma_start(out=saltt[:], in_=salt_d.ap()))

        F = {"nc": nc, "block": block, "consts": consts, "ld": ld, "wld": wld,
             "ctrs": ctrs, "sto": sto, "io": io}
        allc = [ld, wld, sto, xcc, flc] + list(ctrs.values())

        def barrier():
            for engname in ("sync", "gpsimd", "tensor", "scalar", "vector"):
                def go(e):
                    for ct in allc:
                        ct.wait(e)
                getattr(block, engname)(go)

        def sync_point(k):
            @block.gpsimd
            def _(g):
                for ct in [ld, wld, sto] + list(ctrs.values()):
                    ct.wait(g)
                par_branch(g,
                    lambda: flc.dma(g.dma_start(out=sh_flag.ap()[0:1, 0:1], in_=saltt[0:1, k - 1:k])),
                    lambda: flc.dma(g.dma_start(out=sh_flag.ap()[0:1, 1:2], in_=saltt[0:1, k - 1:k])),
                    bal=(flc,))
                with g.register(f"flr{k}") as reg, g.register(f"tg{k}") as tgt:
                    g.reg_load(tgt, saltt[0:1, k - 1:k])

                    def cond():
                        par_branch(g,
                            lambda: flc.dma(g.dma_start(out=fread[:], in_=sh_flag.ap()[0:1, 1:2])),
                            lambda: flc.dma(g.dma_start(out=fread[:], in_=sh_flag.ap()[0:1, 0:1])),
                            bal=(flc,))
                        g.wait_ge(hs["fl"], flc.n)
                        g.reg_load(reg, fread[0:1, 0:1])
                        return g.scalar_reg_alu(ALU.not_equal, reg, tgt)
                    with g.While(cond):
                        g.engine_nop()
                xcc.inc(g.engine_nop())
            barrier()

        with _scope("P1_"):
            build_mamba_pre(F)
        sync_point(1)
        with _scope("P2_"):
            emit_scan_fused(F)
        sync_point(2)
        with _scope("P3_"):
            build_mamba_post(F)
        sync_point(3)
        with _scope("P4_"):
            build_rwkv(F)
        sync_point(4)
        with _scope("P5_"):
            build_attn(F)
        barrier()
        with _scope("P6_"):
            build_moe(F)
    return nc


def kernel_fused(inputs):
    if "fused" not in _PROGS:
        _PROGS["fused"] = build_fused()
    nc = _PROGS["fused"]
    x = np.ascontiguousarray(inputs["L0_attn_x"], dtype=np.float32)
    base = {}
    for d, pref in (("f", "L0_attn_f_"), ("b", "L0_attn_b_")):
        base[f"inWT_{d}"] = np.ascontiguousarray(inputs[pref + "inW"].T)
        base[f"convW_{d}"] = np.ascontiguousarray(inputs[pref + "convW"].reshape(DI, 4))
        base[f"convb_{d}"] = np.ascontiguousarray(inputs[pref + "convb"].reshape(DI, 1))
        xp = inputs[pref + "xprojW"]
        base[f"xpdtT_{d}"] = np.ascontiguousarray(xp[0:RK].T)
        base[f"xpBT_{d}"] = np.ascontiguousarray(xp[RK:RK + DS].T)
        base[f"xpCT_{d}"] = np.ascontiguousarray(xp[RK + DS:RK + 2 * DS].T)
        base[f"dtWT_{d}"] = np.ascontiguousarray(inputs[pref + "dtW"].T)
        base[f"dtb_{d}"] = np.ascontiguousarray(inputs[pref + "dtb"].reshape(DI, 1))
        base[f"Dsk_{d}"] = np.ascontiguousarray(inputs[pref + "Dskip"].reshape(DI, 1))
        base[f"outWT_{d}"] = np.ascontiguousarray(inputs[pref + "outW"].T)
    base["lnw"] = np.ascontiguousarray(inputs["L0_attn_ln_w"].reshape(D, 1))
    base["lnb"] = np.ascontiguousarray(inputs["L0_attn_ln_b"].reshape(D, 1))
    pw = inputs["L0_attn_proj_W"]
    base["projT_f"] = np.ascontiguousarray(pw[:, 0:D].T)
    base["projT_b"] = np.ascontiguousarray(pw[:, D:2 * D].T)
    base["projb"] = np.ascontiguousarray(inputs["L0_attn_proj_b"].reshape(D, 1))
    base["wrT"] = np.ascontiguousarray(inputs["L0_ffn_W_r"].T)
    base["wkT"] = np.ascontiguousarray(inputs["L0_ffn_W_k"].T)
    base["wvT"] = np.ascontiguousarray(inputs["L0_ffn_W_v"].T)
    base["mur"] = np.ascontiguousarray(inputs["L0_ffn_mu_r"].reshape(D, 1))
    base["muk"] = np.ascontiguousarray(inputs["L0_ffn_mu_k"].reshape(D, 1))
    base["Wq"] = np.ascontiguousarray(inputs["L1_attn_Wq"])
    base["Wk"] = np.ascontiguousarray(inputs["L1_attn_Wk"])
    base["Wv"] = np.ascontiguousarray(inputs["L1_attn_Wv"])
    base["Wo"] = np.ascontiguousarray(inputs["L1_attn_Wo"])
    base["bo"] = np.ascontiguousarray(inputs["L1_attn_bo"].reshape(D, 1))
    base["ropeP"] = _rope_perm()
    ck, sk = _rope_tables(0, L, 1.0)
    base["cosk"], base["sink"] = ck, sk
    base["routerT"] = np.ascontiguousarray(inputs["L1_ffn_router_w"].T)
    base["bias16"] = np.ascontiguousarray(inputs["L1_ffn_bias"].reshape(1, NE))
    base["gate_w"] = np.ascontiguousarray(inputs["L1_ffn_gate_w"])
    base["up_w"] = np.ascontiguousarray(inputs["L1_ffn_up_w"])
    base["down_w"] = np.ascontiguousarray(inputs["L1_ffn_down_w"])
    base["shg"] = np.ascontiguousarray(inputs["L1_ffn_sh_gate_w"])
    base["shu"] = np.ascontiguousarray(inputs["L1_ffn_sh_up_w"])
    base["shd"] = np.ascontiguousarray(inputs["L1_ffn_sh_down_w"])
    base["ident128"] = np.eye(128, dtype=np.float32)
    salt = int(np.random.randint(1, 2**28))
    base["salt"] = (salt + np.arange(1, 9, dtype=np.int64)).astype(np.int32).reshape(1, 8)
    in_maps = []
    for c in range(NCORES):
        b, h = c // 2, c % 2
        m = dict(base)
        m["xhT"] = _shard_tok_T(x, c, halo_left=3, halo_right=3)
        m["lmask"] = np.full((128, 1), 0.0 if h == 0 else 1.0, np.float32)
        m["rmask"] = np.full((128, 1), 0.0 if h == 1 else 1.0, np.float32)
        m["pmask"] = np.full((128, 1), 0.0 if h == 0 else 1.0, np.float32)
        for d, pref in (("f", "L0_attn_f_"), ("b", "L0_attn_b_")):
            m[f"al_{d}"] = np.ascontiguousarray(inputs[pref + "A_log"][h * 1024:(h + 1) * 1024])
        m["xT"] = _shard_tok_T(x, c)
        cq, sq = _rope_tables(h * TK, TK, ASCALE)
        m["cosq"], m["sinq"] = cq, sq
        in_maps.append(m)
    res = run_spmd(nc, in_maps, core_ids=list(range(NCORES)))
    out = np.empty((B, L, D), np.float32)
    for c in range(NCORES):
        b, h = c // 2, c % 2
        out[b, h * TK:(h + 1) * TK] = res.results[c]["outT"].T
    return out
